# version 24
# baseline (speedup 1.0000x reference)
import numpy as np
import concourse.bass as bass
import concourse.mybir as mybir
from concourse.bass_utils import run_bass_kernel_spmd

F32 = mybir.dt.float32
BF16 = mybir.dt.bfloat16
AF = mybir.ActivationFunctionType
ALU = mybir.AluOpType

NCORE = 8
T = 2048
Q = 128
NCH = 16
GRP = 4
NGRP = 4
D = 1024
DIN = 2048
NH, HD, DS = 32, 64, 64
CONVD = 2176
FF = 8192
EPS = 1e-5
NEG = -1e30

CC_GROUPS = [[0, 1], [2, 3], [4, 5], [6, 7]]



def _host_layout(inputs):
    ip = np.ascontiguousarray(inputs["L0_attn_in_proj_w"], np.float32)
    shared = {}
    wxbc = ip[2048:4224]
    shared["wxbc_p"] = np.ascontiguousarray(
        wxbc.reshape(17, 128, 8, 128).transpose(3, 0, 2, 1)).reshape(128, 17 * 8 * 128)
    wdt = ip[4224:4256]
    shared["wdt_p"] = np.ascontiguousarray(
        wdt.reshape(32, 8, 128).transpose(2, 1, 0)).reshape(128, 8 * 32)
    shared["wz_t"] = np.ascontiguousarray(ip[0:2048].T)
    shared["wop_t"] = np.ascontiguousarray(
        np.asarray(inputs["L0_attn_out_proj_w"], np.float32).T)
    fc1 = np.asarray(inputs["L0_attn_fc1_w"], np.float32)
    shared["fc1_p"] = np.ascontiguousarray(
        fc1.reshape(64, 128, 8, 128).transpose(3, 0, 2, 1)).reshape(128, 64 * 8 * 128)
    shared["fc2_t"] = np.ascontiguousarray(
        np.asarray(inputs["L0_attn_fc2_w"], np.float32).T)
    wk = np.asarray(inputs["L0_ffn_W_k"], np.float32)
    shared["wk_p"] = np.ascontiguousarray(
        wk.reshape(8, 128, 8, 128).transpose(3, 0, 2, 1)).reshape(128, 8 * 8 * 128)
    shared["wr_t"] = np.ascontiguousarray(np.asarray(inputs["L0_ffn_W_r"], np.float32).T)
    shared["wkt_t"] = np.ascontiguousarray(wk.T)
    shared["wv_t"] = np.ascontiguousarray(np.asarray(inputs["L0_ffn_W_v"], np.float32).T)
    cw = np.asarray(inputs["L0_attn_conv_w"], np.float32)
    shared["conv_w_p"] = np.ascontiguousarray(
        cw.reshape(17, 128, 4).transpose(1, 0, 2)).reshape(128, 68)
    cb = np.asarray(inputs["L0_attn_conv_b"], np.float32)
    shared["conv_b_p"] = np.ascontiguousarray(cb.reshape(17, 128).T)
    shared["dt_bias_c"] = np.asarray(inputs["L0_attn_dt_bias"], np.float32).reshape(32, 1)
    shared["a_log_c"] = np.asarray(inputs["L0_attn_A_log"], np.float32).reshape(32, 1)
    shared["d_row"] = np.asarray(inputs["L0_attn_D"], np.float32).reshape(1, 32)
    shared["ssm_w_row"] = np.asarray(
        inputs["L0_attn_ssm_norm_w"], np.float32).reshape(1, 2048)
    shared["rms_w_row"] = np.asarray(inputs["L0_attn_rms_w"], np.float32).reshape(1, 1024)
    shared["mu_r_p"] = np.ascontiguousarray(
        np.asarray(inputs["L0_ffn_mu_r"], np.float32).reshape(8, 128).T)
    shared["mu_k_p"] = np.ascontiguousarray(
        np.asarray(inputs["L0_ffn_mu_k"], np.float32).reshape(8, 128).T)
    shared["mu_r_row"] = np.asarray(inputs["L0_ffn_mu_r"], np.float32).reshape(1, 1024)
    shared["mu_k_row"] = np.asarray(inputs["L0_ffn_mu_k"], np.float32).reshape(1, 1024)

    x = np.asarray(inputs["L0_attn_x"], np.float32)
    in_maps = []
    for c in range(NCORE):
        b, half = c // 2, c % 2
        m = dict(shared)
        m["x"] = np.ascontiguousarray(x[b, half * T:(half + 1) * T])
        if half == 1:
            m["xh"] = np.ascontiguousarray(x[b, T - 3:T])
        else:
            m["xh"] = np.zeros((3, 1024), np.float32)
        m["cmcol"] = np.full((128, 1), 1.0 if half == 0 else 0.0, np.float32)
        m["umcol"] = np.full((128, 1), 0.0 if half == 0 else 1.0, np.float32)
        in_maps.append(m)
    return in_maps



class Slot:
    __slots__ = ("writer", "readers")

    def __init__(self):
        self.writer = None
        self.readers = []


class Ring:

    def __init__(self, n):
        self.slots = [Slot() for _ in range(n)]
        self.n = n

    def slot(self, i):
        return self.slots[i % self.n]


class Sched:

    ENGINES = ("tensor", "vector", "scalar", "sync", "gpsimd")
    QUEUES = ("sync", "gpsimd", "scalar")
    POOL = 8

    def __init__(self, nc):
        self.nc = nc
        self.sem = {e: nc.alloc_semaphore(name=f"sem_{e}") for e in self.ENGINES}
        self.cnt = {e: 0 for e in self.ENGINES}
        self.dsem = {q: [nc.alloc_semaphore(name=f"dma_{q}_{i}")
                         for i in range(self.POOL)] for q in self.QUEUES}
        self.dcnt = {q: 0 for q in self.QUEUES}
        self.ccsem = nc.alloc_semaphore(name="ccsem")
        self.cccnt = 0
        self.ops = {e: [] for e in self.ENGINES}
        self.mark = {e: {} for e in self.ENGINES}

    def _resolve(self, eng, deps):
        waits = []
        for d in deps:
            if d is None:
                continue
            if d[0] == "D":
                _, q, sidx, target = d
                key = ("D", q, sidx)
                sem = self.dsem[q][sidx]
            elif d[0] == "C":
                key = ("C",)
                sem = self.ccsem
                target = d[1]
            else:
                se, target = d
                if se == eng and se == "tensor":
                    continue
                key = ("E", se)
                sem = self.sem[se]
            if self.mark[eng].get(key, 0) >= target:
                continue
            self.mark[eng][key] = target
            waits.append((sem, target))
        return waits

    def emit(self, eng, fn, deps=(), inc=1):
        waits = self._resolve(eng, deps)
        self.cnt[eng] += inc
        self.ops[eng].append((waits, fn, self.sem[eng], inc))
        return (eng, self.cnt[eng])

    def dma(self, eng, fn, deps=()):
        q = eng
        j = self.dcnt[q]
        self.dcnt[q] = j + 1
        sidx = j % self.POOL
        use = j // self.POOL
        dd = list(deps)
        if use > 0:
            dd.append(("D", q, sidx, 16 * use))
        waits = self._resolve(eng, dd)
        self.ops[eng].append((waits, fn, self.dsem[q][sidx], 16))
        return ("D", q, sidx, 16 * (use + 1))

    def emit_cc(self, fn, deps=()):
        waits = self._resolve("gpsimd", deps)
        self.cccnt += 1
        self.ops["gpsimd"].append((waits, fn, self.ccsem, 1))
        return ("C", self.cccnt)

    def barrier_tokens(self):
        toks = [(e, self.cnt[e]) for e in self.ENGINES]
        toks += [("C", self.cccnt)]
        for q in self.QUEUES:
            n = self.dcnt[q]
            for sidx in range(self.POOL):
                uses = (n - sidx + self.POOL - 1) // self.POOL
                if uses > 0:
                    toks.append(("D", q, sidx, 16 * uses))
        return toks

    def wait_only(self, eng, deps):
        waits = self._resolve(eng, deps)
        if waits:
            self.ops[eng].append((waits, None, None, 0))

    def replay(self, block):
        def mk(eng_name):
            def body(e):
                for waits, fn, sem, inc in self.ops[eng_name]:
                    for sm, v in waits:
                        e.wait_ge(sm, v)
                    if fn is not None:
                        inst = fn(e)
                        inst.then_inc(sem, inc)
            return body
        block.tensor(mk("tensor"))
        block.vector(mk("vector"))
        block.scalar(mk("scalar"))
        block.sync(mk("sync"))
        block.gpsimd(mk("gpsimd"))


def rd(slot, tok):
    slot.readers.append(tok)
    return tok


def wdeps(slot, *extra):
    dd = list(slot.readers)
    if slot.writer is not None:
        dd.append(slot.writer)
    dd.extend(extra)
    return dd


def wrote(slot, tok):
    slot.writer = tok
    slot.readers = []
    return tok



def build_nc(debug=False, stop_after=None):
    nc = bass.Bass("TRN2", target_bir_lowering=False)

    def inp(name, shp):
        return nc.dram_tensor(name, shp, F32, kind="ExternalInput")

    h_x = inp("x", [T, D])
    h_xh = inp("xh", [3, D])
    h_cm = inp("cmcol", [128, 1])
    h_um = inp("umcol", [128, 1])
    h_wxbc = inp("wxbc_p", [128, 17 * 8 * 128])
    h_wdt = inp("wdt_p", [128, 8 * 32])
    h_wz = inp("wz_t", [D, DIN])
    h_wop = inp("wop_t", [DIN, D])
    h_fc1 = inp("fc1_p", [128, 64 * 8 * 128])
    h_fc2 = inp("fc2_t", [4096, D])
    h_wk = inp("wk_p", [128, 8 * 8 * 128])
    h_wkt = inp("wkt_t", [D, D])
    h_wr = inp("wr_t", [D, D])
    h_wv = inp("wv_t", [D, D])
    h_cw = inp("conv_w_p", [128, 68])
    h_cb = inp("conv_b_p", [128, 17])
    h_dtb = inp("dt_bias_c", [32, 1])
    h_alog = inp("a_log_c", [32, 1])
    h_drow = inp("d_row", [1, 32])
    h_ssmw = inp("ssm_w_row", [1, DIN])
    h_rmsw = inp("rms_w_row", [1, D])
    h_murp = inp("mu_r_p", [128, 8])
    h_mukp = inp("mu_k_p", [128, 8])
    h_murrow = inp("mu_r_row", [1, D])
    h_mukrow = inp("mu_k_row", [1, D])

    h_out = nc.dram_tensor("out", [T, D], F32, kind="ExternalOutput")

    ones_row_np = np.ones((1, 128), np.float32)
    eye_np = np.eye(128, dtype=np.float32)
    mask_np = np.where(np.triu(np.ones((Q, Q), np.float32)) > 0, 0.0, NEG).astype(np.float32)
    c_ones = nc.inline_tensor(ones_row_np, "c_ones")
    c_eye = nc.inline_tensor(eye_np, "c_eye")
    c_mask = nc.inline_tensor(mask_np, "c_mask")
    c_mask01 = nc.inline_tensor(np.triu(np.ones((Q, Q), np.float32)), "c_mask01")

    dram = nc.dram_tensor
    d_zsp = dram("zspill", [T, DIN], BF16, kind="Internal")
    d_ysp = dram("yspill", [T, DIN], BF16, kind="Internal")
    d_hsp = dram("hspill", [NCH, 64, DIN], BF16, kind="Internal")
    d_cs = dram("cs_dram", [NGRP, 32, 512], F32, kind="Internal")
    d_row1 = dram("rowscratch1", [1, D], BF16, kind="Internal")
    d_xft = dram("xftspill", [4, 128, 8 * 512], BF16, kind="Internal")
    d_cc1i = dram("cc1_in", [64, DIN], F32, kind="Internal")
    d_cc1o = dram("cc1_out", [64, DIN], F32, kind="Internal")
    d_cc2i = dram("cc2_in", [1, D], F32, kind="Internal")
    d_cc2o = dram("cc2_out", [1, D], F32, kind="Internal")
    d_fc1s = dram("fc1_s", [128, 64 * 8 * 128], BF16, kind="Internal")
    d_fc2s = dram("fc2_s", [4096, D], BF16, kind="Internal")
    d_wks = dram("wk_s", [128, 8 * 8 * 128], BF16, kind="Internal")
    d_wrs = dram("wr_s", [D, D], BF16, kind="Internal")
    d_wvs = dram("wv_s", [D, D], BF16, kind="Internal")
    d_wops = dram("wop_s", [DIN, D], BF16, kind="Internal")

    dbg_outs = {}
    if debug:
        for nm, shp in [("dbg_u0", [128, D]), ("dbg_xbca0", [128, 512]),
                        ("dbg_dt0", [32, 512]), ("dbg_cs0", [32, 512]),
                        ("dbg_y0", [128, DIN]), ("dbg_h0", [64, DIN]),
                        ("dbg_z0", [128, DIN]), ("dbg_yfull0", [128, DIN]),
                        ("dbg_x2_0", [128, D]), ("dbg_x3_0", [128, D]),
                        ("dbg_gam0", [64, 32]), ("dbg_hbu", [64, DIN])]:
            dbg_outs[nm] = nc.dram_tensor(nm, shp, F32, kind="ExternalOutput")

    SB = nc.alloc_sbuf_tensor

    t_lam = SB("lamstore", [128, NCH * 32], F32)
    t_cstore = SB("cstore", [64, NCH * 128], BF16)
    t_csend = SB("csendstore", [64, NCH * 32], F32)
    t_convw = SB("convw", [128, 68], F32)
    t_convb = SB("convb", [128, 17], F32)
    t_dtb = SB("dtbias", [32, 1], F32)
    t_alog = SB("alog", [32, 1], F32)
    t_expa = SB("expa", [32, 1], F32)
    t_cm = SB("cm", [128, 1], F32)
    t_um = SB("um", [128, 1], F32)
    t_dtile = SB("dtile", [128, 32], F32)
    t_ssmw = SB("ssmwrep", [128, DIN], BF16)
    t_omrp = SB("omrp", [128, 8], F32)
    t_omkp = SB("omkp", [128, 8], F32)
    t_zero32 = SB("zero32", [32, 128], F32)
    t_eyebf = SB("eyebf", [128, 128], BF16)
    t_rmsw = SB("rmswrep", [128, D], F32)
    t_murp = SB("murp", [128, 8], F32)
    t_mukp = SB("mukp", [128, 8], F32)
    t_ones = SB("ones", [1, 128], F32)
    t_eye = SB("eye", [128, 128], F32)
    t_mask = SB("maskneg", [128, 128], F32)
    t_wdtw = SB("wdtw", [128, 8 * 32], BF16)
    t_stat = SB("stat", [128, 64], F32)
    t_sqd = SB("sqdump", [128, DIN], BF16)
    t_hbu = SB("hbu", [64, DIN], F32)
    t_xr = SB("xring", [128, 2 * D], F32)
    t_xtf = SB("xtfring", [128, 2 * D], BF16)
    t_big = SB("bigf32", [128, DIN], F32)
    t_gamt = SB("gamtile", [64, 8 * 32], F32)
    t_tfdt8 = SB("tfdt8", [128, 8 * 96], F32)
    t_g = SB("gtile", [128, 2 * 128], BF16)
    t_mask01 = SB("mask01", [128, 128], F32)
    t_const = SB("constap", [128, 2], F32)
    nc.const_aps.register(F32, EPS, t_const.ap()[:, 0:1])
    nc.const_aps.register(F32, 1.0, t_const.ap()[:, 1:2])
    t_x3n0 = SB("x3n0", [1, D], BF16)
    t_xa0 = SB("xa0", [1, D], F32)
    t_halo = SB("halorow", [1, D], BF16)

    AR_BYTES = 146112
    t_arena = SB("ARENA", [128, AR_BYTES], mybir.dt.uint8)
    base = nc.lookup_mloc(t_arena).addr

    def at(name, shape, dtype, off):
        return nc.alloc_sbuf_tensor_at(name, shape, dtype, offset=base + off)

    a_wz = at("wzres", [128, 8 * DIN], BF16, 0)
    a_wxbc = at("wxbc", [128, 17 * 8 * 128], BF16, 0)
    a_zbuf = at("zbuf", [128, DIN], BF16, 34816)
    a_zbuf2 = at("zbuf2", [128, DIN], BF16, 38912)
    a_pre = at("xbcpre", [128, 17 * 515], BF16, 38912)
    a_xbca = at("xbca", [128, 17 * 512], BF16, 56448)
    a_xft = at("xftgrp", [128, 8 * 512], BF16, 73856)
    a_cwk = at("convwork", [128, 2 * 512], F32, 82048)
    a_xs = at("xstf", [128, DIN], BF16, 86144)
    a_xs2 = at("xstf2", [128, DIN], BF16, 34816)
    a_bctf = at("bctf", [128, 2 * 128], BF16, 90240)
    a_cskh = at("cskh", [32, 2 * 512], BF16, 90752)
    a_cskl = at("cskl", [32, 2 * 512], BF16, 92800)
    a_arg = at("argbuf", [128, 2 * 512], F32, 94848)
    a_gamma = at("gammabuf", [128, 2 * 512], BF16, 98944)
    a_mt = at("mtbuf", [128, 2 * 512], BF16, 100992)
    a_wdxs = at("wdxs", [128, DIN], BF16, 103040)
    a_yp = at("ypartring", [128, 2 * DIN], BF16, 107136)
    a_stack = at("stack", [96, 512], F32, 115328)
    a_csft = at("csft", [32, 512], F32, 117376)
    a_dtft = at("dtft", [32, 512], F32, 119424)
    a_lndt = at("lndtft", [32, 512], F32, 121472)
    a_w32 = at("work32", [32, 512], F32, 123520)
    a_tfdt = at("tfdt", [128, 4 * 96], F32, 125568)
    a_hprev = at("hprev", [64, DIN], F32, 127104)
    a_hnew = at("hnew", [64, DIN], F32, 135296)
    a_scr = at("scrbuf", [128, 2 * 512], BF16, 143488)
    a_xhft = at("xhft", [128, 8 * 3], BF16, 145600)
    a_hsave = at("hsave", [128, 17 * 3], BF16, 145664)

    b_gated = at("gated", [128, 32 * 512], BF16, 0)
    b_yft = at("yftpair", [128, 16 * 256], BF16, 32768)
    b_x3ft = at("x3ftgrp", [128, 8 * 528], BF16, 40960)
    b_fstr = at("fstream", [128, 4 * 1024], BF16, 49408)
    b_yrd = at("yread", [128, 2 * DIN], BF16, 57600)
    b_zrd = at("zread", [128, 2 * DIN], BF16, 65792)
    b_yg = at("ygbuf", [128, DIN], BF16, 73984)
    b_yhat = at("yhat", [128, DIN], BF16, 78080)
    b_u = at("uring", [128, 2 * D], F32, 82176)
    b_x2 = at("x2ring", [128, 2 * D], F32, 90368)
    b_w2 = at("w2ring", [128, 4 * D], F32, 98560)
    b_hin = at("hin", [64, DIN], BF16, 114944)
    b_hloc = at("hlocrd", [64, DIN], BF16, 119040)
    b_wstr = at("wstream", [128, 4 * D], BF16, 123136)
    b_pigs = at("pigsum", [64, 64], F32, 131328)
    b_pigt = at("pigtile", [64, 32], F32, 131616)
    b_rst = at("rstore", [128, 2 * D], BF16, 131776)
    b_rk = at("rkscr", [128, 512], BF16, 135872)
    b_sg = at("sgscr", [128, 512], BF16, 136896)
    b_ka = at("kaft", [128, 8 * 512], BF16, 137920)

    p_murrow = at("murrow", [1, D], F32, 0)
    p_mukrow = at("mukrow", [1, D], F32, 4096)
    p_omr = at("onemur", [1, D], F32, 8192)
    p_omk = at("onemuk", [1, D], F32, 12288)
    p_rowA = at("rowA", [1, D], F32, 16384)
    p_rowB = at("rowB", [1, D], F32, 20480)
    p_rowC = at("rowC", [1, D], BF16, 24576)
    p_rowD = at("rowD", [1, D], BF16, 26624)
    p_rowE = at("rowE", [1, D], BF16, 28672)
    p_wcol = at("wcol", [128, 24], BF16, 30720)

    PSA = nc.alloc_psum_tensor
    PS0 = PSA("ps0", [128, 2048], F32)
    PS1 = PSA("ps1", [128, 2048], F32)

    s = Sched(nc)
    return _emit(nc, s, locals(), debug, stop_after)



class Buf:

    def __init__(self):
        self.w = []
        self.r = []

    def reset(self):
        old = self.r + self.w
        self.w = []
        self.r = []
        return old


def _emit(nc, s, L, debug, stop_after=None):
    from types import SimpleNamespace
    E = SimpleNamespace(**{k: v for k, v in L.items() if not k.startswith("__")})
    TE, V, SC, SY, GP = "tensor", "vector", "scalar", "sync", "gpsimd"

    def vtt(out, a, b, op, deps):
        return s.emit(V, lambda e, o=out, x=a, y=b, p=op: e.tensor_tensor(o, x, y, p), deps)

    def vts(out, a, s1, s2, op0, op1, deps):
        if s2 is None:
            return s.emit(V, lambda e, o=out, x=a, u=s1, p0=op0:
                          e.tensor_scalar(o, x, u, None, p0), deps)
        return s.emit(V, lambda e, o=out, x=a, u=s1, w=s2, p0=op0, p1=op1:
                      e.tensor_scalar(o, x, u, w, p0, p1), deps)

    def vstt(out, in0, sc, in1, op0, op1, deps):
        return s.emit(V, lambda e, o=out, x=in0, u=sc, y=in1, p0=op0, p1=op1:
                      e.scalar_tensor_tensor(o, x, u, y, p0, p1), deps)

    def pstt(out, in0, sc, in1, op0, op1, deps):
        return s.emit(GP, lambda e, o=out, x=in0, u=sc, y=in1, p0=op0, p1=op1:
                      e.scalar_tensor_tensor(o, x, u, y, p0, p1), deps)

    def pts(out, a, s1, s2, op0, op1, deps):
        if s2 is None:
            return s.emit(GP, lambda e, o=out, x=a, u=s1, p0=op0:
                          e.tensor_scalar(o, x, u, None, p0), deps)
        return s.emit(GP, lambda e, o=out, x=a, u=s1, w=s2, p0=op0, p1=op1:
                      e.tensor_scalar(o, x, u, w, p0, p1), deps)

    def vcopy(out, a, deps):
        return s.emit(V, lambda e, o=out, x=a: e.tensor_copy(o, x), deps)

    def vmemset(out, val, deps):
        return s.emit(V, lambda e, o=out, v=val: e.memset(o, v), deps)

    def vscan(out, d0, d1, init, op0, op1, deps):
        return s.emit(V, lambda e, o=out, a=d0, b=d1, i=init, p0=op0, p1=op1:
                      e.tensor_tensor_scan(o, a, b, i, p0, p1), deps)

    def act(out, in_, func, deps, bias=0.0, scale=1.0, accum=None):
        dd = list(deps)
        if isinstance(bias, float) and bias != 0.0:
            dd += [tok_c1, tok_c2]
        return s.emit(SC, lambda e, o=out, x=in_, f=func, b=bias, sc=scale, ac=accum:
                      e.activation(o, x, f, bias=b, scale=sc, accum_out=ac), dd)

    def scopy(out, in_, deps):
        return s.emit(SC, lambda e, o=out, x=in_: e.copy(o, x), deps)

    def mm(out, l, r, st, sp, deps, tp=None):
        return s.emit(TE, lambda e, o=out, a=l, b=r, s1=st, s2=sp, t=tp:
                      e.matmul(o, a, b, start=s1, stop=s2, tile_position=t), deps)

    def ptr(out, in_, ident, deps):
        return s.emit(TE, lambda e, o=out, x=in_, i=ident: e.transpose(o, x, i), deps)

    def dmaS(out, in_, deps):
        return s.dma(SY, lambda e, o=out, x=in_: e.dma_start(out=o, in_=x), deps)

    def dmaT(out, in_, deps):
        return s.dma(SY, lambda e, o=out, x=in_: e.dma_start(out=o, in_=x, transpose=True), deps)

    def dmaT2(out, in_, deps):
        return s.dma(SC, lambda e, o=out, x=in_: e.dma_start(out=o, in_=x, transpose=True), deps)

    def dmaS2(out, in_, deps):
        return s.dma(SC, lambda e, o=out, x=in_: e.dma_start(out=o, in_=x), deps)

    def dmaG(out, in_, deps):
        return s.dma(GP, lambda e, o=out, x=in_: e.dma_start(out=o, in_=x), deps)

    def join_all():
        toks = s.barrier_tokens()
        for e in s.ENGINES:
            s.wait_only(e, toks)

    def bcast64(ap2d, nblk):
        return ap2d[:, :, None].broadcast_to([ap2d.shape[0], nblk, 64])

    bx = [Buf(), Buf()]
    bxtf = [Buf(), Buf()]
    bstat = [Buf(), Buf()]
    bxft = Buf()
    bpre = Buf()
    bxbca = Buf()
    bcwk = [Buf(), Buf()]
    bxs = [Buf(), Buf()]
    bbctf = [Buf(), Buf()]
    bcsk = [Buf(), Buf()]
    barg = [Buf(), Buf()]
    bgamma = [Buf(), Buf()]
    bmt = [Buf(), Buf()]
    bwdxs = Buf()
    byp = [Buf(), Buf()]
    bzbuf = [Buf(), Buf()]
    bstack = Buf()
    bcsft = Buf()
    bdtft = Buf()
    blndt = Buf()
    bw32 = Buf()
    btfdt = [Buf() for _ in range(8)]
    bh = [Buf(), Buf()]
    bbig = Buf()
    bgamt = [Buf() for _ in range(8)]
    bcsend = Buf()
    blam = Buf()
    bcstore = Buf()
    bscr = [Buf(), Buf()]
    brscr = [Buf(), Buf()]
    bg = [Buf(), Buf()]
    bhbu = Buf()
    bhsave = Buf()
    ps0a = Buf()
    ps0b = Buf()
    ps0c = Buf()
    ps0d = Buf()
    ps1 = Buf()
    byrd = [Buf(), Buf()]
    bzrd = [Buf(), Buf()]
    byg = Buf()
    byhat = Buf()
    bu = [Buf(), Buf()]
    bx2 = [Buf(), Buf()]
    bw2 = [Buf(), Buf(), Buf(), Buf()]
    bhin = Buf()
    bhloc = [Buf(), Buf()]
    bwstr = [Buf(), Buf(), Buf(), Buf()]
    bfstr = [Buf(), Buf(), Buf(), Buf()]
    bpigs = Buf()
    bpigt = Buf()
    brst = [Buf(), Buf()]
    brk = Buf()
    bsg = Buf()
    bgated = Buf()
    byft = Buf()
    bx3ft = Buf()
    bka = Buf()
    bdram = {}
    for nm in ("zsp", "ysp", "hsp", "cs", "row1", "cc1i", "cc1o", "cc2i", "cc2o",
               "fc1s", "fc2s", "wks", "wrs", "wvs", "wops", "xft"):
        bdram[nm] = Buf()

    PS0, PS1 = E.PS0, E.PS1
    E.a_tfdt = E.t_tfdt8


    ld = []
    ld.append(dmaG(E.t_convw[:], E.h_cw.ap(), []))
    ld.append(dmaG(E.t_convb[:], E.h_cb.ap(), []))
    ld.append(dmaG(E.t_dtb[:], E.h_dtb.ap(), []))
    ld.append(dmaG(E.t_alog[:], E.h_alog.ap(), []))
    ld.append(dmaG(E.t_cm[:], E.h_cm.ap(), []))
    ld.append(dmaG(E.t_um[:], E.h_um.ap(), []))
    ld.append(dmaG(E.t_murp[:], E.h_murp.ap(), []))
    ld.append(dmaG(E.t_mukp[:], E.h_mukp.ap(), []))
    ld.append(dmaG(E.t_wdtw[:], E.h_wdt.ap(), []))
    def repsrc(hap, n):
        return hap[0:1, :, None].broadcast_to([1, n, 128]).transpose([0, 2, 1]).squeeze(0)
    ld.append(dmaG(E.t_dtile[:], repsrc(E.h_drow.ap(), 32), []))
    ld.append(dmaG(E.t_ssmw[:], repsrc(E.h_ssmw.ap(), DIN), []))
    ld.append(dmaG(E.t_rmsw[:], repsrc(E.h_rmsw.ap(), D), []))
    ld.append(dmaS(E.t_ones[:], E.c_ones.ap(), []))
    ld.append(dmaS(E.t_eye[:], E.c_eye.ap(), []))
    ld.append(dmaS(E.t_mask[:], E.c_mask.ap(), []))
    ld.append(dmaS(E.t_mask01[:], E.c_mask01.ap(), []))
    ld_mask01 = ld[-1]
    tok_c1 = vmemset(E.t_const[:, 0:1], EPS, [])
    tok_c2 = vmemset(E.t_const[:, 1:2], 1.0, [])
    tok_eyebf = vcopy(E.t_eyebf[:], E.t_eye[:], [ld[13]])
    tok_expa = act(E.t_expa[:], E.t_alog[:], AF.Exp, [ld[3]])
    tok_z32 = vmemset(E.t_zero32[:], 0.0, [])
    tok_omr = vts(E.t_omrp[:], E.t_murp[:], -1.0, 1.0, ALU.mult, ALU.add, [ld[6]])
    tok_omk = vts(E.t_omkp[:], E.t_mukp[:], -1.0, 1.0, ALU.mult, ALU.add, [ld[7]])
    tok_h0 = vmemset(E.a_hprev[:], 0.0, [])
    bh[0].w.append(tok_h0)

    wz_toks = []
    for d in range(8):
        wz_toks.append(dmaG(E.a_wz[:, d * DIN:(d + 1) * DIN],
                            E.h_wz.ap()[d * 128:(d + 1) * 128, :], []))


    last_dump = [None]

    def dump_act(out, in_, func, deps, accum):
        t = act(out, in_, func, list(deps) + ([last_dump[0]] if last_dump[0] else []),
                accum=accum)
        last_dump[0] = t
        return t

    def emit_ln1(cc, want_bf16_tf, want_f32_u, u_dst=None, u_deps=()):
        xs_ = bx[cc % 2]
        old = xs_.reset()
        x_ap = E.t_xr[:, (cc % 2) * D:(cc % 2 + 1) * D]
        t_x = dmaS(x_ap, E.h_x.ap()[cc * 128:(cc + 1) * 128, :], old)
        xs_.w.append(t_x)
        st = bstat[cc % 2]
        stold = st.reset()
        c0 = (cc % 2) * 16
        sa = E.t_stat[:, c0:c0 + 1]
        sb_ = E.t_stat[:, c0 + 1:c0 + 2]
        sm = E.t_stat[:, c0 + 2:c0 + 3]
        sq = E.t_stat[:, c0 + 3:c0 + 4]
        sv = E.t_stat[:, c0 + 4:c0 + 5]
        sl = E.t_stat[:, c0 + 5:c0 + 6]
        sr = E.t_stat[:, c0 + 6:c0 + 7]
        t1 = dump_act(E.t_sqd[:, 0:D], x_ap, AF.Copy, [t_x] + stold, sa)
        t2 = dump_act(E.t_sqd[:, 0:D], x_ap, AF.Square, [t_x], sb_)
        st.w += [t1, t2]
        t3 = vts(sm, sa, 1.0 / D, None, ALU.mult, None, [t1])
        t4 = vtt(sq, sm, sm, ALU.mult, [t3])
        t5 = vts(sv, sb_, 1.0 / D, sq, ALU.mult, ALU.subtract, [t2, t4])
        t6 = act(sl, sv, AF.Ln, [t5], bias=EPS)
        t7 = act(sr, sl, AF.Exp, [t6], scale=-0.5)
        st.w += [t3, t4, t5, t6, t7]
        u_tok = tf_tok = None
        if want_bf16_tf:
            tfs = bxtf[cc % 2]
            tfold = tfs.reset()
            tf_ap = E.t_xtf[:, (cc % 2) * D:(cc % 2 + 1) * D]
            tf_tok = vts(tf_ap, x_ap, sm, sr, ALU.subtract, ALU.mult, [t7, t_x] + tfold)
            tfs.w.append(tf_tok)
            xs_.r.append(tf_tok)
            st.r.append(tf_tok)
        if want_f32_u:
            u_tok = vts(u_dst, x_ap, sm, sr, ALU.subtract, ALU.mult,
                        [t7, t_x] + list(u_deps))
            xs_.r.append(u_tok)
            st.r.append(u_tok)
        return t_x, x_ap, u_tok, tf_tok

    def emit_xft(cc, tf_tok):
        ci = cc % 4
        xft_first = bxft.reset() if ci == 0 else []
        t = dmaT(E.a_xft[:].rearrange("p (d q) -> p d q", q=512)
                 [:, :, ci * 128:ci * 128 + 128],
                 E.t_xtf[:, (cc % 2) * D:(cc % 2 + 1) * D]
                 .rearrange("p (d f) -> p d f", d=8),
                 [tf_tok] + xft_first)
        bxft.w.append(t)
        bxtf[cc % 2].r.append(t)
        return [t]

    zctx = {}

    def p1a_front(cc):
        t_x, x_ap, _, tf_tok = emit_ln1(cc, True, False)
        zctx[cc] = emit_xft(cc, tf_tok)
        if cc % GRP == GRP - 1:
            t_xsp = dmaS(E.d_xft.ap()[cc // GRP], E.a_xft[:], list(bxft.w))
            bxft.r.append(t_xsp)
            bdram["xft"].w.append(t_xsp)

    def p1a_back(cc):
        ci = cc % GRP
        xtoks = zctx.pop(cc)
        zPS = PS1 if cc % 2 == 0 else PS0
        zbuf_ = ps1 if cc % 2 == 0 else ps0a
        zdeps = xtoks + wz_toks + zbuf_.reset() + (
            [] if cc % 2 == 0 else ps0b.reset() + ps0c.reset() + ps0d.reset())
        last = None
        for d in range(8):
            lhs = E.a_xft[:, d * 512 + ci * 128: d * 512 + ci * 128 + 128]
            for sl in range(4):
                last = mm(zPS[:, sl * 512:(sl + 1) * 512], lhs,
                          E.a_wz[:, d * DIN + sl * 512: d * DIN + (sl + 1) * 512],
                          d == 0, d == 7, zdeps)
        zbuf_.w.append(last)
        if cc % 2 == 1:
            ps0b.w.append(last); ps0c.w.append(last); ps0d.w.append(last)
        zb = bzbuf[cc % 2]
        zold = zb.reset()
        z_ap = E.a_zbuf[:, 0:DIN] if cc % 2 == 0 else E.a_zbuf2[:, 0:DIN]
        t_silu = act(z_ap, zPS[:, :], AF.Silu, [last] + zold)
        zb.w.append(t_silu)
        zbuf_.r.append(t_silu)
        if cc % 2 == 1:
            ps0b.r.append(t_silu); ps0c.r.append(t_silu); ps0d.r.append(t_silu)
        t_sp = dmaS(E.d_zsp.ap()[cc * 128:(cc + 1) * 128, :], z_ap, [t_silu])
        zb.r.append(t_sp)
        bdram["zsp"].w.append(t_sp)

    p1a_front(0)
    for cc in range(NCH):
        if cc + 1 < NCH:
            p1a_front(cc + 1)
        p1a_back(cc)

    def finalize_early():
        join_all()
        n = s.dcnt[SY]
        fw = []
        for sidx in range(s.POOL):
            uses = (n - sidx + s.POOL - 1) // s.POOL
            if uses > 0:
                fw.append((s.dsem[SY][sidx], 16 * uses))
        s.ops[SY].append((fw, None, None, 0))
        with nc.Block() as block:
            s.replay(block)
        return nc, E.dbg_outs

    if stop_after == "prologue":
        return finalize_early()

    join_all()

    if stop_after == "1a":
        return finalize_early()

    tok_wxbc = dmaG(E.a_wxbc[:], E.h_wxbc.ap(), [])
    if stop_after == "1bwx0":
        return finalize_early()

    t_xh = dmaG(E.t_big[0:3, 0:D], E.h_xh.ap(), bbig.reset())
    bbig.w.append(t_xh)
    sa = E.t_stat[0:3, 20:21]; sb_ = E.t_stat[0:3, 21:22]
    sm = E.t_stat[0:3, 22:23]; sq = E.t_stat[0:3, 23:24]
    sv = E.t_stat[0:3, 24:25]; sl_ = E.t_stat[0:3, 25:26]; sr = E.t_stat[0:3, 26:27]
    th1 = dump_act(E.t_sqd[0:3, 0:D], E.t_big[0:3, 0:D], AF.Copy, [t_xh], sa)
    th2 = dump_act(E.t_sqd[0:3, 0:D], E.t_big[0:3, 0:D], AF.Square, [t_xh], sb_)
    th3 = vts(sm, sa, 1.0 / D, None, ALU.mult, None, [th1])
    th4 = vtt(sq, sm, sm, ALU.mult, [th3])
    th5 = vts(sv, sb_, 1.0 / D, sq, ALU.mult, ALU.subtract, [th2, th4])
    th6 = act(sl_, sv, AF.Ln, [th5], bias=EPS)
    th7 = act(sr, sl_, AF.Exp, [th6], scale=-0.5)
    if stop_after == "1bh1":
        return finalize_early()
    uh_ap = E.t_big[0:3, D:2 * D]
    th8 = vts(uh_ap, E.t_big[0:3, 0:D], sm, sr, ALU.subtract, ALU.mult, [th7])
    bbig.w.append(th8)
    if stop_after == "1bh1a":
        return finalize_early()
    xhft_toks = []
    for d in range(8):
        tt_ = ptr(PS0[0:128, 1536 + d * 4:1536 + d * 4 + 3],
                  uh_ap[:, d * 128:(d + 1) * 128], E.t_eye[0:3, 0:3],
                  [th8] + ps0d.reset())
        ps0d.w.append(tt_)
        if stop_after != "1bh1b":
            if stop_after == "1bh2alt":
                dst = E.t_sqd[:, d * 3:(d + 1) * 3]
            elif stop_after == "1bh2alt2":
                dst = E.a_zbuf[:, d * 3:(d + 1) * 3]
            else:
                dst = E.a_xhft[:, d * 3:(d + 1) * 3]
            te_ = scopy(dst, PS0[0:128, 1536 + d * 4:1536 + d * 4 + 3], [tt_])
            ps0d.r.append(te_)
            xhft_toks.append(te_)
    if stop_after == "1bh1b":
        return finalize_early()
    if False:
        pass
    if stop_after in ("1bh2", "1bh2alt", "1bh2alt2"):
        return finalize_early()
    pre_old = bpre.reset()
    for o in range(17):
        lastm = None
        hdeps = [tok_wxbc] + xhft_toks + ps0d.reset() + pre_old
        for d in range(8):
            lastm = mm(PS0[0:128, 1960:1963],
                       E.a_wxbc[:, o * 1024 + d * 128: o * 1024 + d * 128 + 128],
                       E.a_xhft[:, d * 3:(d + 1) * 3],
                       d == 0, d == 7, hdeps)
        ps0d.w.append(lastm)
        th_e = scopy(E.a_pre[:, o * 515:o * 515 + 3], PS0[0:128, 1960:1963], [lastm])
        ps0d.r.append(th_e)
        bpre.w.append(th_e)
    if stop_after == "1bhalo":
        return finalize_early()

    if stop_after == "1bwx":
        return finalize_early()

    ctxs = {}

    def p1b_prep(g):
        ctx = {}
        ctxs[g] = ctx
        t_xrd = dmaS(E.a_xft[:], E.d_xft.ap()[g], bdram["xft"].w + bxft.reset())
        bxft.w.append(t_xrd)
        yield
        if g > 0:
            pre3 = E.a_pre[:, :].rearrange("p (o c) -> p o c", c=515)
            hs3 = E.a_hsave[:, :].rearrange("p (o c) -> p o c", c=3)
            t_halo_cp = vcopy(pre3[:, :, 0:3], hs3[:, :, :], bhsave.w + bpre.r + bpre.w)
            pre_old = bpre.reset()
            bpre.w.append(t_halo_cp)
            bhsave.r.append(t_halo_cp)
        else:
            pre_old = []
        ctx["pre_old"] = pre_old

        def io_chunk(o):
            reg = [ps0a, ps0b][o % 2]
            regap = PS0[:, (o % 2) * 512:(o % 2) * 512 + 512]
            lastm = None
            rdeps = bxft.w + [tok_wxbc] + reg.reset() + ctx["pre_old"]
            for d in range(8):
                lastm = mm(regap,
                           E.a_wxbc[:, o * 1024 + d * 128: o * 1024 + d * 128 + 128],
                           E.a_xft[:, d * 512:(d + 1) * 512],
                           d == 0, d == 7, rdeps)
            reg.w.append(lastm)
            t_e = scopy(E.a_pre[:, o * 515 + 3:o * 515 + 515], regap,
                        [lastm] + ctx["pre_old"] + list(bpre.w))
            reg.r.append(t_e)
            bpre.w.append(t_e)

        for o in range(9):
            io_chunk(o)
        yield
        for o in range(9, 17):
            io_chunk(o)
        pre3 = E.a_pre[:, :].rearrange("p (o c) -> p o c", c=515)
        hs3 = E.a_hsave[:, :].rearrange("p (o c) -> p o c", c=3)
        t_hs = vcopy(hs3[:, :, :], pre3[:, :, 512:515], bpre.w + bhsave.reset())
        bhsave.w.append(t_hs)
        bpre.r.append(t_hs)

        dt_old = ps0c.reset()
        lastm = None
        for d in range(8):
            lastm = mm(PS0[0:32, 1024:1536],
                       E.t_wdtw[:, d * 32:(d + 1) * 32],
                       E.a_xft[:, d * 512:(d + 1) * 512],
                       d == 0, d == 7, bxft.w + [ld[8]] + dt_old)
        ps0c.w.append(lastm)
        w32old = bw32.reset()
        dtftold = bdtft.reset()
        lndtold = blndt.reset()
        csftold = bcsft.reset()
        ta = vts(E.a_w32[:], PS0[0:32, 1024:1536], E.t_dtb[:], None, ALU.add, None,
                 [lastm, ld[2]] + w32old)
        ps0c.r.append(ta)
        tb = act(E.a_csft[:], E.a_w32[:], AF.Abs, [ta] + csftold)
        tc = act(E.a_lndt[:], E.a_csft[:], AF.Exp, [tb] + lndtold, scale=-1.0)
        td = act(E.a_csft[:], E.a_lndt[:], AF.Ln, [tc], bias=1.0)
        te = vts(E.a_dtft[:], E.a_w32[:], 0.0, None, ALU.max, None, [ta] + dtftold)
        tf_ = vtt(E.a_w32[:], E.a_dtft[:], E.a_csft[:], ALU.add, [td, te])
        bw32.w.append(tf_)
        tg = act(E.a_lndt[:], E.a_w32[:], AF.Ln, [tf_])
        blndt.w.append(tg)
        tlogda = vts(E.a_dtft[:], E.a_w32[:], E.t_expa[:], -1.0, ALU.mult, ALU.mult,
                     [tf_, tok_expa])
        bdtft.w.append(tlogda)
        cs_toks = []
        for ci in range(GRP):
            tcs = vscan(E.a_csft[:, ci * 128:(ci + 1) * 128],
                        E.a_dtft[:, ci * 128:(ci + 1) * 128],
                        E.t_zero32[:], 0.0, ALU.add, ALU.add, [tlogda, tok_z32, td])
            cs_toks.append(tcs)
            bcsft.w.append(tcs)
        stold = bstack.reset()
        tbias = vtt(E.a_stack[0:32, :], E.a_lndt[:], E.a_csft[:], ALU.subtract,
                    cs_toks + [tg] + stold)
        tlamf = act(E.a_stack[64:96, :], E.a_csft[:], AF.Exp, cs_toks + stold)
        bstack.w += [tbias, tlamf]
        bcsft.r += [tbias, tlamf]
        ksl = (g % 2) * 512
        t_kh = vcopy(E.a_cskh[:, ksl:ksl + 512], E.a_csft[:],
                     cs_toks + bcsk[g % 2].reset())
        t_kl = vtt(E.a_cskl[:, ksl:ksl + 512], E.a_csft[:],
                   E.a_cskh[:, ksl:ksl + 512], ALU.subtract, [t_kh])
        bcsk[g % 2].w += [t_kh, t_kl]
        bcsft.r += [t_kh, t_kl]
        ctx["csk"] = [t_kh, t_kl]
        for ci in range(GRP):
            cc = g * GRP + ci
            sl8 = (g % 2) * 4 + ci
            csend = E.a_csft[:, ci * 128 + 127:ci * 128 + 128]
            t1_ = vts(E.a_arg[0:32, 0:128], E.a_csft[:, ci * 128:(ci + 1) * 128],
                      csend, -1.0, ALU.subtract, ALU.mult,
                      cs_toks + barg[0].reset())
            t2_ = act(E.a_arg[0:32, 128:256], E.a_arg[0:32, 0:128], AF.Exp, [t1_])
            t3_ = vtt(E.a_stack[32:64, ci * 128:(ci + 1) * 128],
                      E.a_arg[0:32, 128:256], E.a_w32[:, ci * 128:(ci + 1) * 128],
                      ALU.mult, [t2_, tf_])
            barg[0].w += [t1_, t2_]
            bstack.w.append(t3_)
            bw32.r.append(t3_)
            tt1 = ptr(PS0[0:1, 1632:1664], csend, E.t_eye[0:32, 0:32],
                      ps0d.reset() + cs_toks)
            ps0d.w.append(tt1)
            tt2 = scopy(E.t_stat[0:1, 32:64], PS0[0:1, 1632:1664], [tt1])
            ps0d.r.append(tt2)
            tt3 = mm(PS0[0:64, 1664:1696], E.t_ones[0:1, 0:64], E.t_stat[0:1, 32:64],
                     True, True, [tt2, ld[12]])
            ps0d.w.append(tt3)
            tt4 = scopy(E.t_csend[:, cc * 32:(cc + 1) * 32], PS0[0:64, 1664:1696], [tt3])
            bcsend.w.append(tt4)
            ps0d.r.append(tt4)
            tt5 = ptr(PS0[0:128, 1536:1632], E.a_stack[:, ci * 128:(ci + 1) * 128],
                      E.t_eye[0:96, 0:96], bstack.w + ps0d.reset())
            ps0d.w.append(tt5)
            bstack.r.append(tt5)
            slot = btfdt[sl8]
            tfold = slot.reset()
            tt6 = scopy(E.a_tfdt[:, sl8 * 96:sl8 * 96 + 96],
                        PS0[0:128, 1536:1632], [tt5] + tfold)
            slot.w.append(tt6)
            ps0d.r.append(tt6)
            tt7 = vcopy(E.t_lam[:, cc * 32:(cc + 1) * 32],
                        E.a_tfdt[:, sl8 * 96 + 64:sl8 * 96 + 96], [tt6])
            blam.w.append(tt7)
            slot.r.append(tt7)
            tt8 = act(E.t_gamt[:, sl8 * 32:(sl8 + 1) * 32], PS0[0:64, 1664:1696],
                      AF.Exp, [tt3] + bgamt[sl8].reset())
            bgamt[sl8].w.append(tt8)
            ps0d.r.append(tt8)
        yield
        conv_toks = []
        for o in range(17):
            base = o * 515
            TS_, STT_ = (vts, vstt)
            half = E.a_cwk[:, (o % 2) * 512:(o % 2) * 512 + 512]
            wA = half[:, 0:512]
            d0 = bcwk[o % 2].reset()
            c1 = TS_(wA, E.a_pre[:, base + 0:base + 512], E.t_convw[:, o * 4:o * 4 + 1],
                     None, ALU.mult, None, bpre.w + [ld[0]] + d0)
            c2 = STT_(wA, E.a_pre[:, base + 1:base + 513], E.t_convw[:, o * 4 + 1:o * 4 + 2],
                      wA, ALU.mult, ALU.add, [c1])
            c3 = STT_(wA, E.a_pre[:, base + 2:base + 514], E.t_convw[:, o * 4 + 2:o * 4 + 3],
                      wA, ALU.mult, ALU.add, [c2])
            c4 = STT_(wA, E.a_pre[:, base + 3:base + 515], E.t_convw[:, o * 4 + 3:o * 4 + 4],
                      wA, ALU.mult, ALU.add, [c3])
            bcwk[o % 2].w.append(c4)
            bpre.r += [c1, c2, c3, c4]
            c5 = act(E.a_xbca[:].rearrange("p (c q) -> p c q", q=2176)
                     [:, :, o * 128:(o + 1) * 128], wA, AF.Silu,
                     [c4, ld[1]] + (bxbca.reset() if o == 0 else []),
                     bias=E.t_convb[:, o:o + 1])
            bcwk[o % 2].r.append(c5)
            bxbca.w.append(c5)
            conv_toks.append(c5)
        ctx["conv"] = conv_toks
        yield

    def p1b_scan_T(g, ci):
        ctx = ctxs[g]
        conv_toks = ctx["conv"]
        cc = g * GRP + ci
        slot = cc % 2
        xs_t = E.a_xs if slot == 0 else E.a_xs2
        xs_old = bxs[slot].reset()
        cdep = conv_toks[15] if ci == 0 else conv_toks[16]
        t = dmaT(xs_t[:].rearrange("p (c f) -> p c f", c=16),
                 E.a_xbca[:, ci * 2176:ci * 2176 + 2048],
                 [cdep] + xs_old)
        bxs[slot].w.append(t)
        bxbca.r.append(t)
        xs_toks = [t]
        bslot = bbctf[ci % 2]
        b_old = bslot.reset()
        t_bc = dmaT(E.a_bctf[:, (ci % 2) * 128:(ci % 2) * 128 + 128],
                    E.a_xbca[:, ci * 2176 + 2048: ci * 2176 + 2176],
                    [conv_toks[16]] + b_old)
        bslot.w.append(t_bc)
        bxbca.r.append(t_bc)
        cm_old = ps0d.reset()
        t_cmm = mm(PS0[0:64, 1920:2048], E.t_eyebf[:, 64:128],
                   E.a_xbca[:, ci * 2176 + 2048: ci * 2176 + 2176],
                   True, True, [conv_toks[16], tok_eyebf] + cm_old)
        ps0d.w.append(t_cmm)
        bxbca.r.append(t_cmm)
        t_cst = scopy(E.t_cstore[:, cc * 128:(cc + 1) * 128], PS0[0:64, 1920:2048],
                      [t_cmm] + (bcstore.reset() if cc == 0 else []))
        ps0d.r.append(t_cst)
        bcstore.w.append(t_cst)
        g_old = ps0d.reset()
        t_gmm = mm(PS0[:, 1792:1920],
                   E.a_xbca[0:64, ci * 2176 + 2048: ci * 2176 + 2176],
                   E.t_cstore[:, cc * 128:(cc + 1) * 128],
                   True, True, [conv_toks[16], t_cst] + g_old)
        ps0d.w.append(t_gmm)
        t_gev = vtt(E.t_g[:, slot * 128:(slot + 1) * 128], PS0[:, 1792:1920],
                    E.t_mask01[:], ALU.mult, [t_gmm, ld_mask01] + bg[slot].reset())
        bg[slot].w.append(t_gev)
        ps0d.r.append(t_gev)
        ctx[("sT", ci)] = (xs_toks, t_bc, t_gev)

    def p1b_scan_C(g, ci):
        ctx = ctxs[g]
        cc = g * GRP + ci
        slot = cc % 2
        xs_t = E.a_xs if slot == 0 else E.a_xs2
        sl8 = (g % 2) * 4 + ci
        xs_toks, t_bc, t_gev = ctx[("sT", ci)]
        bslot = bbctf[ci % 2]
        gslot = g % 2
        wdap = bcast64(E.a_tfdt[:, sl8 * 96 + 32:sl8 * 96 + 64], 32)
        gamap = bcast64(E.t_gamt[:, sl8 * 32:(sl8 + 1) * 32], 32)
        ps1_old = ps1.reset()
        y_mms = []

        def j_stage_a(j):
            pb = ps0c if j % 2 == 0 else ps0d
            bank_ap = PS0[:, 1024:1536] if j % 2 == 0 else PS0[:, 1536:2048]
            pb_old = pb.reset()
            cs_mms = []
            for hh in range(4):
                h = 4 * j + hh
                selap = E.t_eyebf[0:32, h:h + 1, None].broadcast_to([32, 1, 128]).squeeze(1)
                ccol = gslot * 512 + ci * 128
                t_h = mm(bank_ap[:, hh * 128:(hh + 1) * 128], selap,
                         E.a_cskh[0:32, ccol:ccol + 128],
                         True, False,
                         ctx["csk"] + [tok_eyebf] + (pb_old if hh == 0 else []))
                t_l = mm(bank_ap[:, hh * 128:(hh + 1) * 128], selap,
                         E.a_cskl[0:32, ccol:ccol + 128],
                         False, True, [t_h])
                pb.w += [t_h, t_l]
                bcsk[gslot].r += [t_h, t_l]
                cs_mms.append(t_l)
            ar_ = barg[j % 2]
            ar_old = ar_.reset()
            arg_ap = E.a_arg[:, (j % 2) * 512:(j % 2) * 512 + 512]
            stt_toks = []
            for hh in range(4):
                h = 4 * j + hh
                t_ = vts(arg_ap[:, hh * 128:(hh + 1) * 128],
                         bank_ap[:, hh * 128:(hh + 1) * 128],
                         E.a_tfdt[:, sl8 * 96 + h:sl8 * 96 + h + 1], 20.0,
                         ALU.add, ALU.min,
                         [cs_mms[-1]] + btfdt[sl8].w + ar_old)
                stt_toks.append(t_)
                pb.r.append(t_)
            ar_.w += stt_toks
            return ar_, arg_ap, stt_toks

        def j_stage_c(j, ar_, arg_ap, stt_toks):
            gm = bgamma[j % 2]
            gm_old = gm.reset()
            gam_ap = E.a_gamma[:, (j % 2) * 512:(j % 2) * 512 + 512]
            t_exp = act(gam_ap, arg_ap, AF.Exp, stt_toks + gm_old)
            gm.w.append(t_exp)
            ar_.r.append(t_exp)
            return gm, gam_ap, t_exp

        def j_stage_de(j, gm, gam_ap, t_exp):
            mt_ = bmt[j % 2]
            mt_old = mt_.reset()
            mt_ap = E.a_mt[:, (j % 2) * 512:(j % 2) * 512 + 512]
            mt_toks = []
            for hh in range(4):
                t_ = vtt(mt_ap[:, hh * 128:(hh + 1) * 128],
                         gam_ap[:, hh * 128:(hh + 1) * 128],
                         E.t_g[:, slot * 128:(slot + 1) * 128],
                         ALU.mult, [t_exp, t_gev] + mt_old)
                mt_toks.append(t_)
            mt_.w += mt_toks
            gm.r += mt_toks
            bg[slot].r += mt_toks
            for hh in range(4):
                h = 4 * j + hh
                t_ = mm(PS1[:, h * 64:(h + 1) * 64],
                        mt_ap[:, hh * 128:(hh + 1) * 128],
                        xs_t[:, h * 64:(h + 1) * 64],
                        True, True,
                        [mt_toks[hh]] + xs_toks + (ps1_old if j == 0 and hh == 0 else []))
                mt_.r.append(t_)
                bxs[slot].r.append(t_)
                y_mms.append(t_)

        pend = None
        for j in range(8):
            abc = j_stage_a(j)
            if pend is not None:
                j_stage_de(j - 1, *pend)
            pend = j_stage_c(j, *abc)
        j_stage_de(7, *pend)
        ps1.w += y_mms
        wd_old = bwdxs.reset()
        t_d = vtt(E.a_wdxs[:], xs_t[:], bcast64(E.t_dtile[:], 32), ALU.mult,
                  xs_toks + [ld[9]] + wd_old)
        bwdxs.w.append(t_d)
        bxs[slot].r.append(t_d)
        yp_ = byp[ci % 2]
        yp_old = yp_.reset()
        yp_ap = E.a_yp[:, (ci % 2) * DIN:(ci % 2 + 1) * DIN]
        t_yp = vtt(yp_ap, E.a_wdxs[:], PS1[:, :], ALU.add,
                   [t_d] + y_mms + yp_old)
        yp_.w.append(t_yp)
        bwdxs.r.append(t_yp)
        ps1.r.append(t_yp)
        t_ysp = dmaS(E.d_ysp.ap()[cc * 128:(cc + 1) * 128, :], yp_ap, [t_yp])
        yp_.r.append(t_ysp)
        bdram["ysp"].w.append(t_ysp)
        wd_old = bwdxs.reset()
        t_wdxs = vtt(E.a_wdxs[:], xs_t[:], wdap, ALU.mult,
                     xs_toks + btfdt[sl8].w + wd_old)
        bwdxs.w.append(t_wdxs)
        bxs[slot].r.append(t_wdxs)
        d_old = ps1.reset()
        d_mms = []
        for h in range(NH):
            t_ = mm(PS1[0:64, h * 64:(h + 1) * 64],
                    E.a_bctf[:, (ci % 2) * 128:(ci % 2) * 128 + 64],
                    E.a_wdxs[:, h * 64:(h + 1) * 64],
                    True, True, [t_bc, t_wdxs] + d_old)
            d_mms.append(t_)
        ps1.w += d_mms
        bwdxs.r += d_mms
        bslot.r += d_mms
        hp = bh[cc % 2]
        hn = bh[(cc + 1) % 2]
        hn_old = hn.reset()
        big_old = bbig.reset()
        t_s1 = vtt(E.t_big[0:64, :], E.a_hprev[:] if cc % 2 == 0 else E.a_hnew[:],
                   gamap, ALU.mult, hp.w + bgamt[sl8].w + big_old)
        hp.r.append(t_s1)
        bgamt[sl8].r.append(t_s1)
        bbig.w.append(t_s1)
        t_s2 = vtt(E.a_hnew[:] if cc % 2 == 0 else E.a_hprev[:],
                   E.t_big[0:64, :], PS1[0:64, :], ALU.add,
                   [t_s1] + d_mms + hn_old)
        hn.w.append(t_s2)
        bbig.r.append(t_s2)
        ps1.r.append(t_s2)
        t_hsp = dmaG(E.d_hsp.ap()[cc],
                     E.a_hnew[:] if cc % 2 == 0 else E.a_hprev[:],
                     [t_s2])
        hn.r.append(t_hsp)
        bdram["hsp"].w.append(t_hsp)
        if debug and cc == 0:
            dbg_y = dmaG(E.dbg_outs["dbg_y0"].ap(), yp_ap, [t_yp])
            yp_.r.append(dbg_y)

    scratch_jobs = []

    def add_jobs(src_h, dst_d, tag, nsplit):
        R = src_h.shape[0]
        step = R // nsplit
        for r0 in range(0, R, step):
            scratch_jobs.append((src_h, dst_d, tag, r0, step))

    add_jobs(E.h_fc1, E.d_fc1s, "fc1s", 4)
    add_jobs(E.h_fc2, E.d_fc2s, "fc2s", 4)
    add_jobs(E.h_wop, E.d_wops, "wops", 2)
    add_jobs(E.h_wk, E.d_wks, "wks", 1)
    add_jobs(E.h_wr, E.d_wrs, "wrs", 1)
    add_jobs(E.h_wv, E.d_wvs, "wvs", 1)

    def emit_scratch_jobs(n):
        for _ in range(n):
            if not scratch_jobs:
                return
            src_h, dst_d, tag, r0, step = scratch_jobs.pop(0)
            t = dmaG(dst_d.ap()[r0:r0 + step, :], src_h.ap()[r0:r0 + step, :], [])
            bdram[tag].w.append(t)

    gen = p1b_prep(0)
    for _ in gen:
        pass
    p1b_scan_T(0, 0)
    for g in range(NGRP):
        nxt = p1b_prep(g + 1) if g + 1 < NGRP else None
        for ci in range(GRP):
            if ci < GRP - 1:
                p1b_scan_T(g, ci + 1)
            if nxt is not None and ci < 3:
                next(nxt, None)
            emit_scratch_jobs(1)
            if ci == 2 and nxt is not None:
                next(nxt, None)
            p1b_scan_C(g, ci)
        if nxt is not None:
            p1b_scan_T(g + 1, 0)
        if isinstance(stop_after, str) and stop_after == f"1b:g{g}":
            return finalize_early()

    if stop_after == "1b":
        return finalize_early()


    hfin_ap = E.a_hnew[:] if (NCH - 1) % 2 == 0 else E.a_hprev[:]
    hfin = bh[NCH % 2]
    t_m1 = vtt(E.t_big[0:64, :], hfin_ap, E.t_cm[0:64, 0:1, None].broadcast_to([64, 1, DIN]).squeeze(1),
               ALU.mult, hfin.w + [ld[4]] + bbig.reset())
    bbig.w.append(t_m1)
    hfin.r.append(t_m1)
    t_w1 = dmaG(E.d_cc1i.ap(), E.t_big[0:64, :], [t_m1])
    bbig.r.append(t_w1)
    t_cc1 = s.emit_cc(lambda e: e.collective_compute(
        "AllReduce", ALU.add, replica_groups=CC_GROUPS,
        ins=[E.d_cc1i.ap().opt()], outs=[E.d_cc1o.ap().opt()]), [t_w1])
    t_r1 = dmaG(E.t_big[0:64, :], E.d_cc1o.ap(), [t_cc1] + bbig.reset())
    bbig.w.append(t_r1)
    t_hbu_ = vtt(E.t_hbu[:], E.t_big[0:64, :],
                 E.t_um[0:64, 0:1, None].broadcast_to([64, 1, DIN]).squeeze(1),
                 ALU.mult, [t_r1, ld[5]] + bhbu.reset())
    bhbu.w.append(t_hbu_)
    bbig.r.append(t_hbu_)
    if debug:
        t_dbh = dmaG(E.dbg_outs["dbg_hbu"].ap(), E.t_hbu[:], [t_hbu_])
        bhbu.r.append(t_dbh)

    join_all()

    if stop_after == "cc1":
        return finalize_early()

    t_pig0 = vmemset(E.b_pigs[:, 0:32], 0.0, [])
    bpigs.w.append(t_pig0)
    cc2_tok = {}
    out_toks = []

    for g in range(NGRP):
        if isinstance(stop_after, str) and stop_after.startswith("p2:g") and g > int(stop_after[4:]):
            return finalize_early()
        def p2_top(ci):
            cc = g * GRP + ci
            u_ap = E.b_u[:, (ci % 2) * D:(ci % 2 + 1) * D]
            us = bu[ci % 2]
            u_old = us.reset()
            t_x, x_ap, u_tok, _ = emit_ln1(cc, False, True, u_dst=u_ap, u_deps=u_old)
            us.w.append(u_tok)
            yr = byrd[ci % 2]
            yr_old = yr.reset()
            yr_ap = E.b_yrd[:, (ci % 2) * DIN:(ci % 2 + 1) * DIN]
            t_yr = dmaS(yr_ap, E.d_ysp.ap()[cc * 128:(cc + 1) * 128, :],
                        [bdram["ysp"].w[cc]] + yr_old)
            yr.w.append(t_yr)
            zr = bzrd[ci % 2]
            zr_old = zr.reset()
            zr_ap = E.b_zrd[:, (ci % 2) * DIN:(ci % 2 + 1) * DIN]
            t_zr = dmaS(zr_ap, E.d_zsp.ap()[cc * 128:(cc + 1) * 128, :],
                        [bdram["zsp"].w[cc]] + zr_old)
            zr.w.append(t_zr)
            pg_old = bpigt.reset()
            t_pig = act(E.b_pigt[:], E.b_pigs[:, (cc % 2) * 32:(cc % 2) * 32 + 32],
                        AF.Exp, bpigs.w + pg_old)
            bpigt.w.append(t_pig)
            bpigs.r.append(t_pig)
            t_ps = vtt(E.b_pigs[:, ((cc + 1) % 2) * 32:((cc + 1) % 2) * 32 + 32],
                       E.b_pigs[:, (cc % 2) * 32:(cc % 2) * 32 + 32],
                       E.t_csend[:, cc * 32:(cc + 1) * 32], ALU.add,
                       bpigs.w + bcsend.w)
            bpigs.w.append(t_ps)
            hin_old = bhin.reset()
            if cc == 0:
                t_hin = vtt(E.b_hin[:], E.t_hbu[:], bcast64(E.b_pigt[:], 32),
                            ALU.mult, bhbu.w + [t_pig] + hin_old)
            else:
                hl = bhloc[0]
                hl_old = hl.reset()
                hl_ap = E.b_hloc[:, 0:DIN]
                t_hl = dmaS(hl_ap, E.d_hsp.ap()[cc - 1],
                            [bdram["hsp"].w[cc - 1]] + hl_old)
                hl.w.append(t_hl)
                big_old = bbig.reset()
                t_tmp = vtt(E.t_big[0:64, :], E.t_hbu[:], bcast64(E.b_pigt[:], 32),
                            ALU.mult, bhbu.w + [t_pig] + big_old)
                bbig.w.append(t_tmp)
                t_hin = vtt(E.b_hin[:], E.t_big[0:64, :], hl_ap, ALU.add,
                            [t_tmp, t_hl] + hin_old)
                bbig.r.append(t_hin)
                hl.r.append(t_hin)
            bpigt.r.append(t_hin)
            bhin.w.append(t_hin)
            ps1_old = ps1.reset()
            yi_mms = []
            for h in range(NH):
                t_ = mm(PS1[:, h * 64:(h + 1) * 64],
                        E.t_cstore[:, cc * 128:(cc + 1) * 128],
                        E.b_hin[:, h * 64:(h + 1) * 64],
                        True, True, [t_hin] + bcstore.w + (ps1_old if h == 0 else []))
                yi_mms.append(t_)
                bhin.r.append(t_)
                bcstore.r.append(t_)
            ps1.w += yi_mms
            big_old = bbig.reset()
            t_l = vtt(E.t_big[:, :], PS1[:, :],
                      bcast64(E.t_lam[:, cc * 32:(cc + 1) * 32], 32), ALU.mult,
                      yi_mms + blam.w + big_old)
            bbig.w.append(t_l)
            ps1.r.append(t_l)
            blam.r.append(t_l)
            yg_old = byg.reset()
            t_yf = vtt(E.b_yg[:], E.t_big[:, :], yr_ap, ALU.add, [t_l, t_yr] + yg_old)
            byg.w.append(t_yf)
            bbig.r.append(t_yf)
            yr.r.append(t_yf)
            yh_old = byhat.reset()
            t_ygt = vtt(E.b_yhat[:], E.b_yg[:], zr_ap, ALU.mult, [t_yf, t_zr] + yh_old)
            byhat.w.append(t_ygt)
            byg.r.append(t_ygt)
            zr.r.append(t_ygt)
            c0 = (ci % 2) * 16
            sb2 = E.t_stat[:, c0 + 8:c0 + 9]
            sv2 = E.t_stat[:, c0 + 9:c0 + 10]
            sl2 = E.t_stat[:, c0 + 10:c0 + 11]
            sr2 = E.t_stat[:, c0 + 11:c0 + 12]
            t_q = dump_act(E.t_sqd[:, :], E.b_yhat[:], AF.Square, [t_ygt], sb2)
            t_v = vts(sv2, sb2, 1.0 / DIN, None, ALU.mult, None, [t_q])
            t_ln = act(sl2, sv2, AF.Ln, [t_v], bias=EPS)
            t_rs = act(sr2, sl2, AF.Exp, [t_ln], scale=-0.5)
            bstat[ci % 2].w += [t_q, t_v, t_ln, t_rs]
            t_yh = vstt(E.b_yg[:], E.b_yhat[:], sr2, E.t_ssmw[:], ALU.mult, ALU.mult,
                        [t_rs, ld[10]] + byg.reset())
            byg.w.append(t_yh)
            byhat.r.append(t_yh)
            bstat[ci % 2].r.append(t_yh)
            yft_first = byft.reset() if ci % 2 == 0 else []
            t_ = dmaT(E.b_yft[:].rearrange("p (k q) -> p k q", q=256)
                      [:, :, (ci % 2) * 128:(ci % 2) * 128 + 128],
                      E.b_yg[:].rearrange("p (k f) -> p k f", k=16),
                      [t_yh] + yft_first)
            byft.w.append(t_)
            byg.r.append(t_)
            yft_toks = [t_]

            if debug and cc == 0:
                d1 = dmaG(E.dbg_outs["dbg_z0"].ap(), zr_ap, [t_zr])
                zr.r.append(d1)
                d2 = dmaG(E.dbg_outs["dbg_yfull0"].ap(), E.b_yg[:], [t_yh])
                byg.r.append(d2)


        def p2_pair(ci):
            cc = g * GRP + ci
            pr_old = (ps0a.reset() + ps0b.reset() + ps0c.reset() + ps0d.reset())
            mo_mms = []
            for k in range(16):
                ws = bwstr[k % 4]
                ws_old = ws.reset()
                ws_ap = E.b_wstr[:, (k % 4) * D:(k % 4 + 1) * D]
                t_wl = dmaG(ws_ap, E.d_wops.ap()[k * 128:(k + 1) * 128, :],
                            ws_old)
                ws.w.append(t_wl)
                for co in range(2):
                    for n in range(2):
                        t_ = mm(PS0[:, co * 1024 + n * 512: co * 1024 + (n + 1) * 512],
                                E.b_yft[:, k * 256 + co * 128: k * 256 + co * 128 + 128],
                                ws_ap[:, n * 512:(n + 1) * 512],
                                k == 0, k == 15,
                                [t_wl] + byft.w + (pr_old if k == 0 else []))
                        mo_mms.append(t_)
                        ws.r.append(t_)
                        byft.r.append(t_)
            ps0a.w += mo_mms
            ps0b.w += mo_mms
            ps0c.w += mo_mms
            ps0d.w += mo_mms
            for co in range(2):
                ccp = cc - 1 + co
                cip = ccp % 4
                x2s = bx2[ccp % 2]
                x2_old = x2s.reset()
                x2_ap = E.b_x2[:, (ccp % 2) * D:(ccp % 2 + 1) * D]
                t_x2 = vtt(x2_ap, E.b_u[:, (ccp % 2) * D:(ccp % 2 + 1) * D],
                           PS0[:, co * 1024:co * 1024 + 1024], ALU.add,
                           mo_mms + bu[ccp % 2].w + x2_old)
                x2s.w.append(t_x2)
                bu[ccp % 2].r.append(t_x2)
                for b_ in (ps0a, ps0b, ps0c, ps0d):
                    b_.r.append(t_x2)
                w2s = bw2[cip]
                w2_old = w2s.reset()
                w2_ap = E.b_w2[:, cip * D:(cip + 1) * D]
                xslot = bx[ccp % 2]
                t_w2 = vtt(w2_ap, x2_ap,
                           E.t_xr[:, (ccp % 2) * D:(ccp % 2 + 1) * D], ALU.add,
                           [t_x2] + xslot.w + w2_old)
                w2s.w.append(t_w2)
                xslot.r.append(t_w2)
                x2s.r.append(t_w2)
                c0p = (ccp % 2) * 16
                sb3 = E.t_stat[:, c0p + 12:c0p + 13]
                sv3 = E.t_stat[:, c0p + 13:c0p + 14]
                sl3 = E.t_stat[:, c0p + 14:c0p + 15]
                sr3 = E.t_stat[:, c0p + 15:c0p + 16]
                t_q3 = dump_act(E.t_sqd[:, 0:D], x2_ap, AF.Square, [t_x2], sb3)
                t_v3 = vts(sv3, sb3, 1.0 / D, None, ALU.mult, None, [t_q3])
                t_l3 = act(sl3, sv3, AF.Ln, [t_v3], bias=EPS)
                t_r3 = act(sr3, sl3, AF.Exp, [t_l3], scale=-0.5)
                tfs = bxtf[ccp % 2]
                tf_old = tfs.reset()
                tf_ap = E.t_xtf[:, (ccp % 2) * D:(ccp % 2 + 1) * D]
                bstat[ccp % 2].w += [t_q3, t_v3, t_l3, t_r3]
                t_x2h = vstt(tf_ap, x2_ap, sr3, E.t_rmsw[:], ALU.mult, ALU.mult,
                             [t_r3, ld[11]] + tf_old)
                tfs.w.append(t_x2h)
                x2s.r.append(t_x2h)
                bstat[ccp % 2].r.append(t_x2h)
                if debug and ccp == 0:
                    d3 = dmaG(E.dbg_outs["dbg_x2_0"].ap(), x2_ap, [t_x2])
                    x2s.r.append(d3)
                if cip == 0:
                    x2ft_first_deps = bx3ft.reset()
                t_ = dmaT(E.b_x3ft[:].rearrange("p (d q) -> p d q", q=528)
                          [:, :, cip * 128:cip * 128 + 128],
                          tf_ap[:].rearrange("p (d f) -> p d f", d=8),
                          [t_x2h] + (x2ft_first_deps if cip == 0 else []))
                bx3ft.w.append(t_)
                tfs.r.append(t_)


        p2_top(0)
        p2_top(1)
        p2_pair(1)
        p2_top(2)
        p2_top(3)
        p2_pair(3)

        fc1_deps = bx3ft.w + bdram["fc1s"].w
        gt_old = bgated.reset()
        gated_toks = []
        fc1ps = [(ps0a, 0), (ps0b, 512), (ps0c, 1024), (ps0d, 1536)]
        for i in range(32):
            fa = bfstr[(2 * i) % 4]
            fg = bfstr[(2 * i + 1) % 4]
            fa_old = fa.reset()
            fg_old = fg.reset()
            fa_ap = E.b_fstr[:, ((2 * i) % 4) * 1024:((2 * i) % 4 + 1) * 1024]
            fg_ap = E.b_fstr[:, ((2 * i + 1) % 4) * 1024:((2 * i + 1) % 4 + 1) * 1024]
            t_fa = dmaG(fa_ap, E.d_fc1s.ap()[:, i * 1024:(i + 1) * 1024],
                        bdram["fc1s"].w + fa_old)
            fa.w.append(t_fa)
            t_fg = dmaG(fg_ap, E.d_fc1s.ap()[:, (32 + i) * 1024:(33 + i) * 1024],
                        bdram["fc1s"].w + fg_old)
            fg.w.append(t_fg)
            pa, ca = fc1ps[(i % 2) * 2]
            pg, cg = fc1ps[(i % 2) * 2 + 1]
            a_old = pa.reset()
            g_old2 = pg.reset()
            lastA = lastG = None
            for d in range(8):
                lastA = mm(PS0[:, ca:ca + 512], fa_ap[:, d * 128:(d + 1) * 128],
                           E.b_x3ft[:, d * 528:d * 528 + 512], d == 0, d == 7,
                           [t_fa] + fc1_deps + (a_old if d == 0 else []))
                fa.r.append(lastA)
            for d in range(8):
                lastG = mm(PS0[:, cg:cg + 512], fg_ap[:, d * 128:(d + 1) * 128],
                           E.b_x3ft[:, d * 528:d * 528 + 512], d == 0, d == 7,
                           [t_fg] + fc1_deps + (g_old2 if d == 0 else []))
                fg.r.append(lastG)
            pa.w.append(lastA)
            pg.w.append(lastG)
            sg_old = bsg.reset()
            t_sg = act(E.b_sg[:], PS0[:, cg:cg + 512], AF.Silu, [lastG] + sg_old)
            bsg.w.append(t_sg)
            pg.r.append(t_sg)
            t_gt = vtt(E.b_gated[:, i * 512:(i + 1) * 512], E.b_sg[:], PS0[:, ca:ca + 512],
                       ALU.mult, [t_sg, lastA] + (gt_old if i == 0 else []))
            bgated.w.append(t_gt)
            bsg.r.append(t_gt)
            pa.r.append(t_gt)
            gated_toks.append(t_gt)
        bx3ft.r.append(gated_toks[-1])

        q_old = (ps0a.reset() + ps0b.reset() + ps0c.reset() + ps0d.reset()
                 + ps1.reset())
        QAP = [PS0[:, 0:1024], PS0[:, 1024:2048], PS1[:, 0:1024], PS1[:, 1024:2048]]
        m_mms = []
        for k in range(32):
            ws = bwstr[k % 4]
            ws_old = ws.reset()
            ws_ap = E.b_wstr[:, (k % 4) * D:(k % 4 + 1) * D]
            t_wl = dmaG(ws_ap, E.d_fc2s.ap()[k * 128:(k + 1) * 128, :],
                        bdram["fc2s"].w + ws_old)
            ws.w.append(t_wl)
            for cq in range(4):
                for n in range(2):
                    t_ = mm(QAP[cq][:, n * 512:(n + 1) * 512],
                            E.b_gated[:, k * 512 + cq * 128: k * 512 + cq * 128 + 128],
                            ws_ap[:, n * 512:(n + 1) * 512],
                            k == 0, k == 31,
                            [t_wl] + gated_toks + (q_old if k == 0 else []))
                    m_mms.append(t_)
                    ws.r.append(t_)
                    bgated.r.append(t_)
        for b_ in (ps0a, ps0b, ps0c, ps0d, ps1):
            b_.w += m_mms

        x3v = E.b_x3ft[:, :].rearrange("p (d c) -> p d c", c=528)
        if g == 0:
            t_hz = vmemset(x3v[:, :, 15:16], 0.0, bx3ft.reset())
        else:
            prev3 = bx3ft.reset()
            t_hz = vcopy(x3v[:, :, 15:16], x3v[:, :, 527:528], prev3)
        bx3ft.w.append(t_hz)
        for ci in range(GRP):
            cc = g * GRP + ci
            w2_ap = E.b_w2[:, ci * D:(ci + 1) * D]
            t_xa = vtt(w2_ap, QAP[ci], w2_ap, ALU.add, m_mms + bw2[ci].w)
            bw2[ci].w.append(t_xa)
            for b_ in (ps0a, ps0b, ps0c, ps0d, ps1):
                b_.r.append(t_xa)
            if debug and cc == 0:
                d4 = dmaG(E.dbg_outs["dbg_x3_0"].ap(), w2_ap, [t_xa])
                bw2[ci].r.append(d4)
            c0 = (ci % 2) * 16
            sa = E.t_stat[:, c0:c0 + 1]; sb_ = E.t_stat[:, c0 + 1:c0 + 2]
            sm = E.t_stat[:, c0 + 2:c0 + 3]; sq = E.t_stat[:, c0 + 3:c0 + 4]
            sv = E.t_stat[:, c0 + 4:c0 + 5]; sl_ = E.t_stat[:, c0 + 5:c0 + 6]
            sr = E.t_stat[:, c0 + 6:c0 + 7]
            st = bstat[ci % 2]
            st_old = st.reset()
            t1 = dump_act(E.t_sqd[:, 0:D], w2_ap, AF.Copy, [t_xa] + st_old, sa)
            t2 = dump_act(E.t_sqd[:, 0:D], w2_ap, AF.Square, [t_xa], sb_)
            t3 = vts(sm, sa, 1.0 / D, None, ALU.mult, None, [t1])
            t4 = vtt(sq, sm, sm, ALU.mult, [t3])
            t5 = vts(sv, sb_, 1.0 / D, sq, ALU.mult, ALU.subtract, [t2, t4])
            t6 = act(sl_, sv, AF.Ln, [t5], bias=EPS)
            t7 = act(sr, sl_, AF.Exp, [t6], scale=-0.5)
            st.w += [t1, t2, t3, t4, t5, t6, t7]
            tfs = bxtf[ci % 2]
            tf_old = tfs.reset()
            tf_ap = E.t_xtf[:, (ci % 2) * D:(ci % 2 + 1) * D]
            t_x3 = vts(tf_ap, w2_ap, sm, sr, ALU.subtract, ALU.mult, [t7] + tf_old)
            tfs.w.append(t_x3)
            st.r.append(t_x3)
            bw2[ci].r.append(t_x3)
            if cc == 0:
                t_r0 = vcopy(E.t_x3n0[:], tf_ap[0:1, :], [t_x3])
                tfs.r.append(t_r0)
                t_a0 = vcopy(E.t_xa0[:], w2_ap[0:1, :], [t_xa])
                bw2[ci].r.append(t_a0)
            if cc == T // 128 - 1:
                t_re = dmaS(E.d_row1.ap(), tf_ap[127:128, :], [t_x3])
                tfs.r.append(t_re)
                t_rb = dmaG(E.t_halo[:], E.d_row1.ap(), [t_re])
                t_msk = vts(E.t_big[0:1, 0:D], E.t_halo[:], E.t_cm[0:1, 0:1], None,
                            ALU.mult, None, [t_rb, ld[4]] + bbig.reset())
                bbig.w.append(t_msk)
                t_wc2 = dmaS(E.d_cc2i.ap(), E.t_big[0:1, 0:D], [t_msk])
                bbig.r.append(t_wc2)
                cc2_tok["t"] = s.emit_cc(lambda e: e.collective_compute(
                    "AllReduce", ALU.add, replica_groups=CC_GROUPS,
                    ins=[E.d_cc2i.ap().opt()], outs=[E.d_cc2o.ap().opt()]), [t_wc2])
            t_ = dmaT(E.b_x3ft[:].rearrange("p (d q) -> p d q", q=528)
                      [:, :, 16 + ci * 128:16 + ci * 128 + 128],
                      tf_ap[:].rearrange("p (d f) -> p d f", d=8),
                      [t_x3] + bx3ft.w[:1])
            bx3ft.w.append(t_)
            tfs.r.append(t_)

        xk_old = byft.reset()
        xk_toks = []
        for d in range(8):
            dst = E.b_yft[:, d * 512:(d + 1) * 512]
            t_t = vts(dst, x3v[:, d:d + 1, 16:528].squeeze(1), E.t_mukp[:, d:d + 1],
                      None, ALU.mult, None,
                      bx3ft.w + [ld[7]] + (xk_old if d == 0 else []))
            t_xk = vstt(dst, x3v[:, d:d + 1, 15:527].squeeze(1), E.t_omkp[:, d:d + 1],
                        dst, ALU.mult, ALU.add, [t_t, tok_omk])
            byft.w += [t_t, t_xk]
            bx3ft.r += [t_t, t_xk]
            xk_toks.append(t_xk)
        ka_old = bka.reset()
        for o in range(8):
            fs = bfstr[o % 4]
            fs_old = fs.reset()
            fs_ap = E.b_fstr[:, (o % 4) * 1024:(o % 4 + 1) * 1024]
            t_wl = dmaG(fs_ap, E.d_wks.ap()[:, o * 1024:(o + 1) * 1024],
                        bdram["wks"].w + fs_old)
            fs.w.append(t_wl)
            reg = [ps0a, ps0b][o % 2]
            regap = PS0[:, (o % 2) * 512:(o % 2) * 512 + 512]
            r_old = reg.reset()
            lastm = None
            for d in range(8):
                lastm = mm(regap, fs_ap[:, d * 128:(d + 1) * 128],
                           E.b_yft[:, d * 512:(d + 1) * 512], d == 0, d == 7,
                           [t_wl, xk_toks[d]] + (r_old if d == 0 else []))
                fs.r.append(lastm)
                byft.r.append(lastm)
            reg.w.append(lastm)
            rk_old = brk.reset()
            t_rl = act(E.b_rk[:], regap, AF.Relu, [lastm] + rk_old)
            brk.w.append(t_rl)
            reg.r.append(t_rl)
            t_ka = act(E.b_ka[:, o * 512:(o + 1) * 512], E.b_rk[:], AF.Square,
                       [t_rl] + (ka_old if o == 0 else []))
            bka.w.append(t_ka)
            brk.r.append(t_ka)
        xr_old = byft.reset()
        xr_toks = []
        for d in range(8):
            dst = E.b_yft[:, d * 512:(d + 1) * 512]
            t_t = vts(dst, x3v[:, d:d + 1, 16:528].squeeze(1), E.t_murp[:, d:d + 1],
                      None, ALU.mult, None,
                      bx3ft.w + [ld[6]] + (xr_old if d == 0 else []))
            t_xr = vstt(dst, x3v[:, d:d + 1, 15:527].squeeze(1), E.t_omrp[:, d:d + 1],
                        dst, ALU.mult, ALU.add, [t_t, tok_omr])
            byft.w += [t_t, t_xr]
            bx3ft.r += [t_t, t_xr]
            xr_toks.append(t_xr)
        for pp in range(2):
            r_mms = {0: [], 1: []}
            pr_old = ps1.reset()
            for d in range(8):
                ws = bwstr[d % 4]
                ws_old = ws.reset()
                ws_ap = E.b_wstr[:, (d % 4) * D:(d % 4 + 1) * D]
                t_wl = dmaG(ws_ap, E.d_wrs.ap()[d * 128:(d + 1) * 128, :],
                            bdram["wrs"].w + ws_old)
                ws.w.append(t_wl)
                for co in range(2):
                    ci = pp * 2 + co
                    for n in range(2):
                        t_ = mm(PS1[:, co * 1024 + n * 512: co * 1024 + (n + 1) * 512],
                                E.b_yft[:, d * 512 + ci * 128: d * 512 + ci * 128 + 128],
                                ws_ap[:, n * 512:(n + 1) * 512],
                                d == 0, d == 7,
                                [t_wl, xr_toks[d]] + (pr_old if d == 0 else []))
                        r_mms[co].append(t_)
                        ws.r.append(t_)
                        byft.r.append(t_)
            ps1.w += r_mms[0] + r_mms[1]
            r_toks = []
            for co in range(2):
                rs = brst[co]
                rs_old = rs.reset()
                rs_ap = E.b_rst[:, co * D:(co + 1) * D]
                t_sig = act(rs_ap, PS1[:, co * 1024:(co + 1) * 1024], AF.Sigmoid,
                            r_mms[co] + rs_old)
                rs.w.append(t_sig)
                ps1.r.append(t_sig)
                r_toks.append(t_sig)
            v_mms = {0: [], 1: []}
            pv_old = ps0a.reset() + ps0b.reset() + ps0c.reset() + ps0d.reset()
            for kc in range(8):
                ws = bwstr[kc % 4]
                ws_old = ws.reset()
                ws_ap = E.b_wstr[:, (kc % 4) * D:(kc % 4 + 1) * D]
                t_wl = dmaG(ws_ap, E.d_wvs.ap()[kc * 128:(kc + 1) * 128, :],
                            bdram["wvs"].w + ws_old)
                ws.w.append(t_wl)
                for co in range(2):
                    ci = pp * 2 + co
                    for n in range(2):
                        t_ = mm(PS0[:, co * 1024 + n * 512: co * 1024 + (n + 1) * 512],
                                E.b_ka[:, kc * 512 + ci * 128: kc * 512 + ci * 128 + 128],
                                ws_ap[:, n * 512:(n + 1) * 512],
                                kc == 0, kc == 7,
                                [t_wl] + bka.w + (pv_old if kc == 0 else []))
                        v_mms[co].append(t_)
                        ws.r.append(t_)
                        bka.r.append(t_)
            ps0a.w += v_mms[0] + v_mms[1]
            ps0b.w += v_mms[0] + v_mms[1]
            ps0c.w += v_mms[0] + v_mms[1]
            ps0d.w += v_mms[0] + v_mms[1]
            for co in range(2):
                ci = pp * 2 + co
                cc = g * GRP + ci
                big_old = bbig.reset()
                t_tv = vtt(E.t_big[:, 0:1024], E.b_rst[:, co * D:(co + 1) * D],
                           PS0[:, co * 1024:(co + 1) * 1024], ALU.mult,
                           v_mms[co] + r_toks + big_old)
                bbig.w.append(t_tv)
                brst[co].r.append(t_tv)
                for b_ in (ps0a, ps0b, ps0c, ps0d):
                    b_.r.append(t_tv)
                x2s = bx2[ci % 2]
                x2_old = x2s.reset()
                x4_ap = E.b_x2[:, (ci % 2) * D:(ci % 2 + 1) * D]
                t_x4 = vtt(x4_ap, E.b_w2[:, ci * D:(ci + 1) * D], E.t_big[:, 0:1024],
                           ALU.add, [t_tv] + bw2[ci].w + x2_old)
                x2s.w.append(t_x4)
                bw2[ci].r.append(t_x4)
                bbig.r.append(t_x4)
                t_out = dmaS(E.h_out.ap()[cc * 128:(cc + 1) * 128, :], x4_ap, [t_x4])
                x2s.r.append(t_out)
                out_toks.append(t_out)

    if stop_after == "p2":
        return finalize_early()

    join_all()

    t_mr = dmaS(E.p_murrow[:], E.h_murrow.ap(), [])
    t_mk = dmaS(E.p_mukrow[:], E.h_mukrow.ap(), [])
    t_o1 = vts(E.p_omr[:], E.p_murrow[:], -1.0, 1.0, ALU.mult, ALU.add, [t_mr])
    t_o2 = vts(E.p_omk[:], E.p_mukrow[:], -1.0, 1.0, ALU.mult, ALU.add, [t_mk])
    t_rc2 = dmaG(E.t_big[0:1, 0:D], E.d_cc2o.ap(), [cc2_tok["t"]] + bbig.reset())
    bbig.w.append(t_rc2)
    t_hal = vts(E.t_halo[:], E.t_big[0:1, 0:D], E.t_um[0:1, 0:1], None,
                ALU.mult, None, [t_rc2, ld[5]])
    bbig.r.append(t_hal)

    def patch_rows(x3n0, halo, murow, omrow, outC, extra):
        tA = vtt(E.p_rowA[:], x3n0, murow, ALU.mult,
                 [t_hal, t_mr, t_mk, t_o1, t_o2] + extra)
        tB = vtt(E.p_rowB[:], halo, omrow, ALU.mult, [tA])
        tC = vtt(outC, E.p_rowA[:], E.p_rowB[:], ALU.add, [tA, tB])
        return tC

    t_xr0 = patch_rows(E.t_x3n0[:], E.t_halo[:], E.p_murrow[:], E.p_omr[:],
                       E.p_rowC[:], [])
    t_xk0 = patch_rows(E.t_x3n0[:], E.t_halo[:], E.p_mukrow[:], E.p_omk[:],
                       E.p_rowD[:], [t_xr0])

    def row_to_cols_bf(row_ap, col_base, dep):
        toks = []
        for d in range(8):
            tm = mm(PS0[0:128, 1960 + d % 2:1961 + d % 2],
                    row_ap[:, d * 128:(d + 1) * 128],
                    E.t_eyebf[0:1, 0:1], True, True, [dep, tok_eyebf] + ps0d.reset())
            ps0d.w.append(tm)
            te_ = scopy(E.p_wcol[:, col_base + d:col_base + d + 1],
                        PS0[0:128, 1960 + d % 2:1961 + d % 2], [tm])
            ps0d.r.append(te_)
            toks.append(te_)
        return toks

    xr0c = row_to_cols_bf(E.p_rowC[:], 0, t_xr0)
    xk0c = row_to_cols_bf(E.p_rowD[:], 8, t_xk0)

    def patch_mm(col_base, wsrc, wbuf_d, psreg, psbuf, deps):
        mms = []
        for d in range(8):
            ws = bwstr[d % 4]
            ws_old = ws.reset()
            ws_ap = E.b_wstr[:, (d % 4) * D:(d % 4 + 1) * D]
            t_wl = dmaG(ws_ap, wsrc.ap()[d * 128:(d + 1) * 128, :],
                        wbuf_d.w + ws_old)
            ws.w.append(t_wl)
            for n in range(2):
                t_ = mm(psreg[:, n * 512:(n + 1) * 512],
                        E.p_wcol[:, col_base + d:col_base + d + 1],
                        ws_ap[:, n * 512:(n + 1) * 512],
                        d == 0, d == 7,
                        [t_wl] + deps + (psbuf.reset() if d == 0 and n == 0 else []))
                mms.append(t_)
                ws.r.append(t_)
        psbuf.w += mms
        return mms

    r0_mms = patch_mm(0, E.h_wr, bdram["wrs"], PS1[0:1, 0:1024], ps1, xr0c)
    t_r0 = act(E.p_rowC[:], PS1[0:1, 0:1024], AF.Sigmoid, r0_mms)
    ps1.r.append(t_r0)
    bwk2 = Buf()
    k0_mms = patch_mm(8, E.h_wkt, bwk2, PS1[0:1, 1024:2048], ps1, xk0c)
    t_rl0 = act(E.p_rowD[:], PS1[0:1, 1024:2048], AF.Relu, k0_mms)
    ps1.r.append(t_rl0)
    t_ka0 = vtt(E.p_rowE[:], E.p_rowD[:], E.p_rowD[:], ALU.mult, [t_rl0])
    ka0c = row_to_cols_bf(E.p_rowE[:], 16, t_ka0)
    v0_mms = patch_mm(16, E.h_wv, bdram["wvs"], PS0[0:1, 0:1024],
                      ps0a, ka0c)
    t_tv0 = vtt(E.p_rowA[:], E.p_rowC[:], PS0[0:1, 0:1024], ALU.mult,
                v0_mms + [t_r0])
    ps0a.r.append(t_tv0)
    t_x40 = vtt(E.p_rowB[:], E.t_xa0[:], E.p_rowA[:], ALU.add, [t_tv0])
    t_out0 = dmaS(E.h_out.ap()[0:1, :], E.p_rowB[:], [t_x40])
    out_toks.append(t_out0)

    n = s.dcnt[SY]
    fw = []
    for sidx in range(s.POOL):
        uses = (n - sidx + s.POOL - 1) // s.POOL
        if uses > 0:
            fw.append((s.dsem[SY][sidx], 16 * uses))
    s.ops[SY].append((fw, None, None, 0))

    with nc.Block() as block:
        s.replay(block)
    return nc, E.dbg_outs



_NC_CACHE = {}


def _get_nc(debug=False):
    key = bool(debug)
    if key not in _NC_CACHE:
        _NC_CACHE[key] = build_nc(debug)
    return _NC_CACHE[key]


def kernel(**inputs):
    in_maps = _host_layout(inputs)
    nc, _ = _get_nc(False)
    res = run_bass_kernel_spmd(nc, in_maps, core_ids=list(range(NCORE)))
    out = np.empty((4, 4096, 1024), np.float32)
    for c in range(NCORE):
        b, half = c // 2, c % 2
        out[b, half * T:(half + 1) * T] = res.results[c]["out"]
    return out


# version 25
# speedup vs baseline: 1.0061x; 1.0061x over previous
import numpy as np
import concourse.bass as bass
import concourse.mybir as mybir
from concourse.bass_utils import run_bass_kernel_spmd

F32 = mybir.dt.float32
BF16 = mybir.dt.bfloat16
AF = mybir.ActivationFunctionType
ALU = mybir.AluOpType

NCORE = 8
T = 2048
Q = 128
NCH = 16
GRP = 4
NGRP = 4
D = 1024
DIN = 2048
NH, HD, DS = 32, 64, 64
CONVD = 2176
FF = 8192
EPS = 1e-5
NEG = -1e30

CC_GROUPS = [[0, 1], [2, 3], [4, 5], [6, 7]]



def _host_layout(inputs):
    ip = np.ascontiguousarray(inputs["L0_attn_in_proj_w"], np.float32)
    shared = {}
    wxbc = ip[2048:4224]
    shared["wxbc_p"] = np.ascontiguousarray(
        wxbc.reshape(17, 128, 8, 128).transpose(3, 0, 2, 1)).reshape(128, 17 * 8 * 128)
    wdt = ip[4224:4256]
    shared["wdt_p"] = np.ascontiguousarray(
        wdt.reshape(32, 8, 128).transpose(2, 1, 0)).reshape(128, 8 * 32)
    shared["wz_t"] = np.ascontiguousarray(ip[0:2048].T)
    shared["wop_t"] = np.ascontiguousarray(
        np.asarray(inputs["L0_attn_out_proj_w"], np.float32).T)
    fc1 = np.asarray(inputs["L0_attn_fc1_w"], np.float32)
    shared["fc1_p"] = np.ascontiguousarray(
        fc1.reshape(64, 128, 8, 128).transpose(3, 0, 2, 1)).reshape(128, 64 * 8 * 128)
    shared["fc2_t"] = np.ascontiguousarray(
        np.asarray(inputs["L0_attn_fc2_w"], np.float32).T)
    wk = np.asarray(inputs["L0_ffn_W_k"], np.float32)
    shared["wk_p"] = np.ascontiguousarray(
        wk.reshape(8, 128, 8, 128).transpose(3, 0, 2, 1)).reshape(128, 8 * 8 * 128)
    shared["wr_t"] = np.ascontiguousarray(np.asarray(inputs["L0_ffn_W_r"], np.float32).T)
    shared["wkt_t"] = np.ascontiguousarray(wk.T)
    shared["wv_t"] = np.ascontiguousarray(np.asarray(inputs["L0_ffn_W_v"], np.float32).T)
    cw = np.asarray(inputs["L0_attn_conv_w"], np.float32)
    shared["conv_w_p"] = np.ascontiguousarray(
        cw.reshape(17, 128, 4).transpose(1, 0, 2)).reshape(128, 68)
    cb = np.asarray(inputs["L0_attn_conv_b"], np.float32)
    shared["conv_b_p"] = np.ascontiguousarray(cb.reshape(17, 128).T)
    shared["dt_bias_c"] = np.asarray(inputs["L0_attn_dt_bias"], np.float32).reshape(32, 1)
    shared["a_log_c"] = np.asarray(inputs["L0_attn_A_log"], np.float32).reshape(32, 1)
    shared["d_row"] = np.asarray(inputs["L0_attn_D"], np.float32).reshape(1, 32)
    shared["ssm_w_row"] = np.asarray(
        inputs["L0_attn_ssm_norm_w"], np.float32).reshape(1, 2048)
    shared["rms_w_row"] = np.asarray(inputs["L0_attn_rms_w"], np.float32).reshape(1, 1024)
    shared["mu_r_p"] = np.ascontiguousarray(
        np.asarray(inputs["L0_ffn_mu_r"], np.float32).reshape(8, 128).T)
    shared["mu_k_p"] = np.ascontiguousarray(
        np.asarray(inputs["L0_ffn_mu_k"], np.float32).reshape(8, 128).T)
    shared["mu_r_row"] = np.asarray(inputs["L0_ffn_mu_r"], np.float32).reshape(1, 1024)
    shared["mu_k_row"] = np.asarray(inputs["L0_ffn_mu_k"], np.float32).reshape(1, 1024)

    x = np.asarray(inputs["L0_attn_x"], np.float32)
    in_maps = []
    for c in range(NCORE):
        b, half = c // 2, c % 2
        m = dict(shared)
        m["x"] = np.ascontiguousarray(x[b, half * T:(half + 1) * T])
        if half == 1:
            m["xh"] = np.ascontiguousarray(x[b, T - 3:T])
        else:
            m["xh"] = np.zeros((3, 1024), np.float32)
        m["cmcol"] = np.full((128, 1), 1.0 if half == 0 else 0.0, np.float32)
        m["umcol"] = np.full((128, 1), 0.0 if half == 0 else 1.0, np.float32)
        in_maps.append(m)
    return in_maps



class Slot:
    __slots__ = ("writer", "readers")

    def __init__(self):
        self.writer = None
        self.readers = []


class Ring:

    def __init__(self, n):
        self.slots = [Slot() for _ in range(n)]
        self.n = n

    def slot(self, i):
        return self.slots[i % self.n]


class Sched:

    ENGINES = ("tensor", "vector", "scalar", "sync", "gpsimd")
    QUEUES = ("sync", "gpsimd", "scalar")
    POOL = 8

    def __init__(self, nc):
        self.nc = nc
        self.sem = {e: nc.alloc_semaphore(name=f"sem_{e}") for e in self.ENGINES}
        self.cnt = {e: 0 for e in self.ENGINES}
        self.dsem = {q: [nc.alloc_semaphore(name=f"dma_{q}_{i}")
                         for i in range(self.POOL)] for q in self.QUEUES}
        self.dcnt = {q: 0 for q in self.QUEUES}
        self.ccsem = nc.alloc_semaphore(name="ccsem")
        self.cccnt = 0
        self.ops = {e: [] for e in self.ENGINES}
        self.mark = {e: {} for e in self.ENGINES}

    def _resolve(self, eng, deps):
        waits = []
        for d in deps:
            if d is None:
                continue
            if d[0] == "D":
                _, q, sidx, target = d
                key = ("D", q, sidx)
                sem = self.dsem[q][sidx]
            elif d[0] == "C":
                key = ("C",)
                sem = self.ccsem
                target = d[1]
            else:
                se, target = d
                if se == eng and se == "tensor":
                    continue
                key = ("E", se)
                sem = self.sem[se]
            if self.mark[eng].get(key, 0) >= target:
                continue
            self.mark[eng][key] = target
            waits.append((sem, target))
        return waits

    def emit(self, eng, fn, deps=(), inc=1):
        waits = self._resolve(eng, deps)
        self.cnt[eng] += inc
        self.ops[eng].append((waits, fn, self.sem[eng], inc))
        return (eng, self.cnt[eng])

    def dma(self, eng, fn, deps=()):
        q = eng
        j = self.dcnt[q]
        self.dcnt[q] = j + 1
        sidx = j % self.POOL
        use = j // self.POOL
        dd = list(deps)
        if use > 0:
            dd.append(("D", q, sidx, 16 * use))
        waits = self._resolve(eng, dd)
        self.ops[eng].append((waits, fn, self.dsem[q][sidx], 16))
        return ("D", q, sidx, 16 * (use + 1))

    def emit_cc(self, fn, deps=()):
        waits = self._resolve("gpsimd", deps)
        self.cccnt += 1
        self.ops["gpsimd"].append((waits, fn, self.ccsem, 1))
        return ("C", self.cccnt)

    def barrier_tokens(self):
        toks = [(e, self.cnt[e]) for e in self.ENGINES]
        toks += [("C", self.cccnt)]
        for q in self.QUEUES:
            n = self.dcnt[q]
            for sidx in range(self.POOL):
                uses = (n - sidx + self.POOL - 1) // self.POOL
                if uses > 0:
                    toks.append(("D", q, sidx, 16 * uses))
        return toks

    def wait_only(self, eng, deps):
        waits = self._resolve(eng, deps)
        if waits:
            self.ops[eng].append((waits, None, None, 0))

    def replay(self, block):
        def mk(eng_name):
            def body(e):
                for waits, fn, sem, inc in self.ops[eng_name]:
                    for sm, v in waits:
                        e.wait_ge(sm, v)
                    if fn is not None:
                        inst = fn(e)
                        inst.then_inc(sem, inc)
            return body
        block.tensor(mk("tensor"))
        block.vector(mk("vector"))
        block.scalar(mk("scalar"))
        block.sync(mk("sync"))
        block.gpsimd(mk("gpsimd"))


def rd(slot, tok):
    slot.readers.append(tok)
    return tok


def wdeps(slot, *extra):
    dd = list(slot.readers)
    if slot.writer is not None:
        dd.append(slot.writer)
    dd.extend(extra)
    return dd


def wrote(slot, tok):
    slot.writer = tok
    slot.readers = []
    return tok



def build_nc(debug=False, stop_after=None):
    nc = bass.Bass("TRN2", target_bir_lowering=False)

    def inp(name, shp):
        return nc.dram_tensor(name, shp, F32, kind="ExternalInput")

    h_x = inp("x", [T, D])
    h_xh = inp("xh", [3, D])
    h_cm = inp("cmcol", [128, 1])
    h_um = inp("umcol", [128, 1])
    h_wxbc = inp("wxbc_p", [128, 17 * 8 * 128])
    h_wdt = inp("wdt_p", [128, 8 * 32])
    h_wz = inp("wz_t", [D, DIN])
    h_wop = inp("wop_t", [DIN, D])
    h_fc1 = inp("fc1_p", [128, 64 * 8 * 128])
    h_fc2 = inp("fc2_t", [4096, D])
    h_wk = inp("wk_p", [128, 8 * 8 * 128])
    h_wkt = inp("wkt_t", [D, D])
    h_wr = inp("wr_t", [D, D])
    h_wv = inp("wv_t", [D, D])
    h_cw = inp("conv_w_p", [128, 68])
    h_cb = inp("conv_b_p", [128, 17])
    h_dtb = inp("dt_bias_c", [32, 1])
    h_alog = inp("a_log_c", [32, 1])
    h_drow = inp("d_row", [1, 32])
    h_ssmw = inp("ssm_w_row", [1, DIN])
    h_rmsw = inp("rms_w_row", [1, D])
    h_murp = inp("mu_r_p", [128, 8])
    h_mukp = inp("mu_k_p", [128, 8])
    h_murrow = inp("mu_r_row", [1, D])
    h_mukrow = inp("mu_k_row", [1, D])

    h_out = nc.dram_tensor("out", [T, D], F32, kind="ExternalOutput")

    ones_row_np = np.ones((1, 128), np.float32)
    eye_np = np.eye(128, dtype=np.float32)
    mask_np = np.where(np.triu(np.ones((Q, Q), np.float32)) > 0, 0.0, NEG).astype(np.float32)
    c_ones = nc.inline_tensor(ones_row_np, "c_ones")
    c_eye = nc.inline_tensor(eye_np, "c_eye")
    c_mask = nc.inline_tensor(mask_np, "c_mask")
    c_mask01 = nc.inline_tensor(np.triu(np.ones((Q, Q), np.float32)), "c_mask01")

    dram = nc.dram_tensor
    d_zsp = dram("zspill", [T, DIN], BF16, kind="Internal")
    d_ysp = dram("yspill", [T, DIN], BF16, kind="Internal")
    d_hsp = dram("hspill", [NCH, 64, DIN], BF16, kind="Internal")
    d_cs = dram("cs_dram", [NGRP, 32, 512], F32, kind="Internal")
    d_row1 = dram("rowscratch1", [1, D], BF16, kind="Internal")
    d_xft = dram("xftspill", [4, 128, 8 * 512], BF16, kind="Internal")
    d_cc1i = dram("cc1_in", [64, DIN], F32, kind="Internal")
    d_cc1o = dram("cc1_out", [64, DIN], F32, kind="Internal")
    d_cc2i = dram("cc2_in", [1, D], F32, kind="Internal")
    d_cc2o = dram("cc2_out", [1, D], F32, kind="Internal")
    d_fc1s = dram("fc1_s", [128, 64 * 8 * 128], BF16, kind="Internal")
    d_fc2s = dram("fc2_s", [4096, D], BF16, kind="Internal")
    d_wks = dram("wk_s", [128, 8 * 8 * 128], BF16, kind="Internal")
    d_wrs = dram("wr_s", [D, D], BF16, kind="Internal")
    d_wvs = dram("wv_s", [D, D], BF16, kind="Internal")
    d_wops = dram("wop_s", [DIN, D], BF16, kind="Internal")

    dbg_outs = {}
    if debug:
        for nm, shp in [("dbg_u0", [128, D]), ("dbg_xbca0", [128, 512]),
                        ("dbg_dt0", [32, 512]), ("dbg_cs0", [32, 512]),
                        ("dbg_y0", [128, DIN]), ("dbg_h0", [64, DIN]),
                        ("dbg_z0", [128, DIN]), ("dbg_yfull0", [128, DIN]),
                        ("dbg_x2_0", [128, D]), ("dbg_x3_0", [128, D]),
                        ("dbg_gam0", [64, 32]), ("dbg_hbu", [64, DIN])]:
            dbg_outs[nm] = nc.dram_tensor(nm, shp, F32, kind="ExternalOutput")

    SB = nc.alloc_sbuf_tensor

    t_lam = SB("lamstore", [128, NCH * 32], F32)
    t_cstore = SB("cstore", [64, NCH * 128], BF16)
    t_csend = SB("csendstore", [64, NCH * 32], F32)
    t_convw = SB("convw", [128, 68], F32)
    t_convb = SB("convb", [128, 17], F32)
    t_dtb = SB("dtbias", [32, 1], F32)
    t_alog = SB("alog", [32, 1], F32)
    t_expa = SB("expa", [32, 1], F32)
    t_cm = SB("cm", [128, 1], F32)
    t_um = SB("um", [128, 1], F32)
    t_dtile = SB("dtile", [128, 32], F32)
    t_ssmw = SB("ssmwrep", [128, DIN], BF16)
    t_omrp = SB("omrp", [128, 8], F32)
    t_omkp = SB("omkp", [128, 8], F32)
    t_zero32 = SB("zero32", [32, 128], F32)
    t_eyebf = SB("eyebf", [128, 128], BF16)
    t_rmsw = SB("rmswrep", [128, D], F32)
    t_murp = SB("murp", [128, 8], F32)
    t_mukp = SB("mukp", [128, 8], F32)
    t_ones = SB("ones", [1, 128], F32)
    t_eye = SB("eye", [128, 128], F32)
    t_mask = SB("maskneg", [128, 128], F32)
    t_wdtw = SB("wdtw", [128, 8 * 32], BF16)
    t_stat = SB("stat", [128, 64], F32)
    t_sqd = SB("sqdump", [128, DIN], BF16)
    t_hbu = SB("hbu", [64, DIN], F32)
    t_xr = SB("xring", [128, 2 * D], F32)
    t_xtf = SB("xtfring", [128, 2 * D], BF16)
    t_big = SB("bigf32", [128, DIN], F32)
    t_gamt = SB("gamtile", [64, 8 * 32], F32)
    t_tfdt8 = SB("tfdt8", [128, 8 * 96], F32)
    t_g = SB("gtile", [128, 2 * 128], BF16)
    t_mask01 = SB("mask01", [128, 128], F32)
    t_const = SB("constap", [128, 2], F32)
    nc.const_aps.register(F32, EPS, t_const.ap()[:, 0:1])
    nc.const_aps.register(F32, 1.0, t_const.ap()[:, 1:2])
    t_x3n0 = SB("x3n0", [1, D], BF16)
    t_xa0 = SB("xa0", [1, D], F32)
    t_halo = SB("halorow", [1, D], BF16)

    AR_BYTES = 146112
    t_arena = SB("ARENA", [128, AR_BYTES], mybir.dt.uint8)
    base = nc.lookup_mloc(t_arena).addr

    def at(name, shape, dtype, off):
        return nc.alloc_sbuf_tensor_at(name, shape, dtype, offset=base + off)

    a_wz = at("wzres", [128, 8 * DIN], BF16, 0)
    a_wxbc = at("wxbc", [128, 17 * 8 * 128], BF16, 0)
    a_zbuf = at("zbuf", [128, DIN], BF16, 34816)
    a_zbuf2 = at("zbuf2", [128, DIN], BF16, 38912)
    a_pre = at("xbcpre", [128, 17 * 515], BF16, 38912)
    a_xbca = at("xbca", [128, 17 * 512], BF16, 56448)
    a_xft = at("xftgrp", [128, 8 * 512], BF16, 73856)
    a_cwk = at("convwork", [128, 2 * 512], F32, 82048)
    a_xs = at("xstf", [128, DIN], BF16, 86144)
    a_xs2 = at("xstf2", [128, DIN], BF16, 34816)
    a_bctf = at("bctf", [128, 2 * 128], BF16, 90240)
    a_cskh = at("cskh", [32, 2 * 512], BF16, 90752)
    a_cskl = at("cskl", [32, 2 * 512], BF16, 92800)
    a_arg = at("argbuf", [128, 2 * 512], F32, 94848)
    a_gamma = at("gammabuf", [128, 2 * 512], BF16, 98944)
    a_mt = at("mtbuf", [128, 2 * 512], BF16, 100992)
    a_wdxs = at("wdxs", [128, DIN], BF16, 103040)
    a_yp = at("ypartring", [128, 2 * DIN], BF16, 107136)
    a_stack = at("stack", [96, 512], F32, 115328)
    a_csft = at("csft", [32, 512], F32, 117376)
    a_dtft = at("dtft", [32, 512], F32, 119424)
    a_lndt = at("lndtft", [32, 512], F32, 121472)
    a_w32 = at("work32", [32, 512], F32, 123520)
    a_tfdt = at("tfdt", [128, 4 * 96], F32, 125568)
    a_hprev = at("hprev", [64, DIN], F32, 127104)
    a_hnew = at("hnew", [64, DIN], F32, 135296)
    a_scr = at("scrbuf", [128, 2 * 512], BF16, 143488)
    a_xhft = at("xhft", [128, 8 * 3], BF16, 145600)
    a_hsave = at("hsave", [128, 17 * 3], BF16, 145664)

    b_gated = at("gated", [128, 32 * 512], BF16, 0)
    b_yft = at("yftpair", [128, 16 * 256], BF16, 32768)
    b_x3ft = at("x3ftgrp", [128, 8 * 528], BF16, 40960)
    b_fstr = at("fstream", [128, 4 * 1024], BF16, 49408)
    b_yrd = at("yread", [128, 2 * DIN], BF16, 57600)
    b_zrd = at("zread", [128, 2 * DIN], BF16, 65792)
    b_yg = at("ygbuf", [128, DIN], BF16, 73984)
    b_yhat = at("yhat", [128, DIN], BF16, 78080)
    b_u = at("uring", [128, 2 * D], F32, 82176)
    b_x2 = at("x2ring", [128, 2 * D], F32, 90368)
    b_w2 = at("w2ring", [128, 4 * D], F32, 98560)
    b_hin = at("hin", [64, DIN], BF16, 114944)
    b_hloc = at("hlocrd", [64, DIN], BF16, 119040)
    b_wstr = at("wstream", [128, 4 * D], BF16, 123136)
    b_pigs = at("pigsum", [64, 64], F32, 131328)
    b_pigt = at("pigtile", [64, 32], F32, 131616)
    b_rst = at("rstore", [128, 2 * D], BF16, 131776)
    b_rk = at("rkscr", [128, 512], BF16, 135872)
    b_sg = at("sgscr", [128, 512], BF16, 136896)
    b_ka = at("kaft", [128, 8 * 512], BF16, 137920)

    p_murrow = at("murrow", [1, D], F32, 0)
    p_mukrow = at("mukrow", [1, D], F32, 4096)
    p_omr = at("onemur", [1, D], F32, 8192)
    p_omk = at("onemuk", [1, D], F32, 12288)
    p_rowA = at("rowA", [1, D], F32, 16384)
    p_rowB = at("rowB", [1, D], F32, 20480)
    p_rowC = at("rowC", [1, D], BF16, 24576)
    p_rowD = at("rowD", [1, D], BF16, 26624)
    p_rowE = at("rowE", [1, D], BF16, 28672)
    p_wcol = at("wcol", [128, 24], BF16, 30720)

    PSA = nc.alloc_psum_tensor
    PS0 = PSA("ps0", [128, 2048], F32)
    PS1 = PSA("ps1", [128, 2048], F32)

    s = Sched(nc)
    return _emit(nc, s, locals(), debug, stop_after)



class Buf:

    def __init__(self):
        self.w = []
        self.r = []

    def reset(self):
        old = self.r + self.w
        self.w = []
        self.r = []
        return old


def _emit(nc, s, L, debug, stop_after=None):
    from types import SimpleNamespace
    E = SimpleNamespace(**{k: v for k, v in L.items() if not k.startswith("__")})
    TE, V, SC, SY, GP = "tensor", "vector", "scalar", "sync", "gpsimd"

    def vtt(out, a, b, op, deps):
        return s.emit(V, lambda e, o=out, x=a, y=b, p=op: e.tensor_tensor(o, x, y, p), deps)

    def vts(out, a, s1, s2, op0, op1, deps):
        if s2 is None:
            return s.emit(V, lambda e, o=out, x=a, u=s1, p0=op0:
                          e.tensor_scalar(o, x, u, None, p0), deps)
        return s.emit(V, lambda e, o=out, x=a, u=s1, w=s2, p0=op0, p1=op1:
                      e.tensor_scalar(o, x, u, w, p0, p1), deps)

    def vstt(out, in0, sc, in1, op0, op1, deps):
        return s.emit(V, lambda e, o=out, x=in0, u=sc, y=in1, p0=op0, p1=op1:
                      e.scalar_tensor_tensor(o, x, u, y, p0, p1), deps)

    def pstt(out, in0, sc, in1, op0, op1, deps):
        return s.emit(GP, lambda e, o=out, x=in0, u=sc, y=in1, p0=op0, p1=op1:
                      e.scalar_tensor_tensor(o, x, u, y, p0, p1), deps)

    def pts(out, a, s1, s2, op0, op1, deps):
        if s2 is None:
            return s.emit(GP, lambda e, o=out, x=a, u=s1, p0=op0:
                          e.tensor_scalar(o, x, u, None, p0), deps)
        return s.emit(GP, lambda e, o=out, x=a, u=s1, w=s2, p0=op0, p1=op1:
                      e.tensor_scalar(o, x, u, w, p0, p1), deps)

    def vcopy(out, a, deps):
        return s.emit(V, lambda e, o=out, x=a: e.tensor_copy(o, x), deps)

    def vmemset(out, val, deps):
        return s.emit(V, lambda e, o=out, v=val: e.memset(o, v), deps)

    def vscan(out, d0, d1, init, op0, op1, deps):
        return s.emit(V, lambda e, o=out, a=d0, b=d1, i=init, p0=op0, p1=op1:
                      e.tensor_tensor_scan(o, a, b, i, p0, p1), deps)

    def act(out, in_, func, deps, bias=0.0, scale=1.0, accum=None):
        dd = list(deps)
        if isinstance(bias, float) and bias != 0.0:
            dd += [tok_c1, tok_c2]
        return s.emit(SC, lambda e, o=out, x=in_, f=func, b=bias, sc=scale, ac=accum:
                      e.activation(o, x, f, bias=b, scale=sc, accum_out=ac), dd)

    def scopy(out, in_, deps):
        return s.emit(SC, lambda e, o=out, x=in_: e.copy(o, x), deps)

    def mm(out, l, r, st, sp, deps, tp=None):
        return s.emit(TE, lambda e, o=out, a=l, b=r, s1=st, s2=sp, t=tp:
                      e.matmul(o, a, b, start=s1, stop=s2, tile_position=t), deps)

    def ptr(out, in_, ident, deps):
        return s.emit(TE, lambda e, o=out, x=in_, i=ident: e.transpose(o, x, i), deps)

    def dmaS(out, in_, deps):
        return s.dma(SY, lambda e, o=out, x=in_: e.dma_start(out=o, in_=x), deps)

    def dmaT(out, in_, deps):
        return s.dma(SY, lambda e, o=out, x=in_: e.dma_start(out=o, in_=x, transpose=True), deps)

    def dmaT2(out, in_, deps):
        return s.dma(SC, lambda e, o=out, x=in_: e.dma_start(out=o, in_=x, transpose=True), deps)

    def dmaS2(out, in_, deps):
        return s.dma(SC, lambda e, o=out, x=in_: e.dma_start(out=o, in_=x), deps)

    def dmaG(out, in_, deps):
        return s.dma(GP, lambda e, o=out, x=in_: e.dma_start(out=o, in_=x), deps)

    def join_all():
        toks = s.barrier_tokens()
        for e in s.ENGINES:
            s.wait_only(e, toks)

    def bcast64(ap2d, nblk):
        return ap2d[:, :, None].broadcast_to([ap2d.shape[0], nblk, 64])

    bx = [Buf(), Buf()]
    bxtf = [Buf(), Buf()]
    bstat = [Buf(), Buf()]
    bxft = Buf()
    bpre = Buf()
    bxbca = Buf()
    bcwk = [Buf(), Buf()]
    bxs = [Buf(), Buf()]
    bbctf = [Buf(), Buf()]
    bcsk = [Buf(), Buf()]
    barg = [Buf(), Buf()]
    bgamma = [Buf(), Buf()]
    bmt = [Buf(), Buf()]
    bwdxs = Buf()
    byp = [Buf(), Buf()]
    bzbuf = [Buf(), Buf()]
    bstack = Buf()
    bcsft = Buf()
    bdtft = Buf()
    blndt = Buf()
    bw32 = Buf()
    btfdt = [Buf() for _ in range(8)]
    bh = [Buf(), Buf()]
    bbig = Buf()
    bgamt = [Buf() for _ in range(8)]
    bcsend = Buf()
    blam = Buf()
    bcstore = Buf()
    bscr = [Buf(), Buf()]
    brscr = [Buf(), Buf()]
    bg = [Buf(), Buf()]
    bhbu = Buf()
    bhsave = Buf()
    ps0a = Buf()
    ps0b = Buf()
    ps0c = Buf()
    ps0d = Buf()
    ps1 = Buf()
    byrd = [Buf(), Buf()]
    bzrd = [Buf(), Buf()]
    byg = Buf()
    byhat = Buf()
    bu = [Buf(), Buf()]
    bx2 = [Buf(), Buf()]
    bw2 = [Buf(), Buf(), Buf(), Buf()]
    bhin = Buf()
    bhloc = [Buf(), Buf()]
    bwstr = [Buf(), Buf(), Buf(), Buf()]
    bfstr = [Buf(), Buf(), Buf(), Buf()]
    bpigs = Buf()
    bpigt = Buf()
    brst = [Buf(), Buf()]
    brk = Buf()
    bsg = Buf()
    bgated = Buf()
    byft = Buf()
    bx3ft = Buf()
    bka = Buf()
    bdram = {}
    for nm in ("zsp", "ysp", "hsp", "cs", "row1", "cc1i", "cc1o", "cc2i", "cc2o",
               "fc1s", "fc2s", "wks", "wrs", "wvs", "wops", "xft"):
        bdram[nm] = Buf()

    PS0, PS1 = E.PS0, E.PS1
    E.a_tfdt = E.t_tfdt8


    ld = []
    ld.append(dmaG(E.t_convw[:], E.h_cw.ap(), []))
    ld.append(dmaG(E.t_convb[:], E.h_cb.ap(), []))
    ld.append(dmaG(E.t_dtb[:], E.h_dtb.ap(), []))
    ld.append(dmaG(E.t_alog[:], E.h_alog.ap(), []))
    ld.append(dmaG(E.t_cm[:], E.h_cm.ap(), []))
    ld.append(dmaG(E.t_um[:], E.h_um.ap(), []))
    ld.append(dmaG(E.t_murp[:], E.h_murp.ap(), []))
    ld.append(dmaG(E.t_mukp[:], E.h_mukp.ap(), []))
    ld.append(dmaG(E.t_wdtw[:], E.h_wdt.ap(), []))
    def repsrc(hap, n):
        return hap[0:1, :, None].broadcast_to([1, n, 128]).transpose([0, 2, 1]).squeeze(0)
    ld.append(dmaG(E.t_dtile[:], repsrc(E.h_drow.ap(), 32), []))
    ld.append(dmaG(E.t_ssmw[:], repsrc(E.h_ssmw.ap(), DIN), []))
    ld.append(dmaG(E.t_rmsw[:], repsrc(E.h_rmsw.ap(), D), []))
    ld.append(dmaS(E.t_ones[:], E.c_ones.ap(), []))
    ld.append(dmaS(E.t_eye[:], E.c_eye.ap(), []))
    ld.append(dmaS(E.t_mask[:], E.c_mask.ap(), []))
    ld.append(dmaS(E.t_mask01[:], E.c_mask01.ap(), []))
    ld_mask01 = ld[-1]
    tok_c1 = vmemset(E.t_const[:, 0:1], EPS, [])
    tok_c2 = vmemset(E.t_const[:, 1:2], 1.0, [])
    tok_eyebf = vcopy(E.t_eyebf[:], E.t_eye[:], [ld[13]])
    tok_expa = act(E.t_expa[:], E.t_alog[:], AF.Exp, [ld[3]])
    tok_z32 = vmemset(E.t_zero32[:], 0.0, [])
    tok_omr = vts(E.t_omrp[:], E.t_murp[:], -1.0, 1.0, ALU.mult, ALU.add, [ld[6]])
    tok_omk = vts(E.t_omkp[:], E.t_mukp[:], -1.0, 1.0, ALU.mult, ALU.add, [ld[7]])
    tok_h0 = vmemset(E.a_hprev[:], 0.0, [])
    bh[0].w.append(tok_h0)

    wz_toks = []
    for d in range(8):
        wz_toks.append(dmaG(E.a_wz[:, d * DIN:(d + 1) * DIN],
                            E.h_wz.ap()[d * 128:(d + 1) * 128, :], []))


    last_dump = [None]

    def dump_act(out, in_, func, deps, accum):
        t = act(out, in_, func, list(deps) + ([last_dump[0]] if last_dump[0] else []),
                accum=accum)
        last_dump[0] = t
        return t

    def emit_ln1(cc, want_bf16_tf, want_f32_u, u_dst=None, u_deps=()):
        xs_ = bx[cc % 2]
        old = xs_.reset()
        x_ap = E.t_xr[:, (cc % 2) * D:(cc % 2 + 1) * D]
        t_x = dmaS(x_ap, E.h_x.ap()[cc * 128:(cc + 1) * 128, :], old)
        xs_.w.append(t_x)
        st = bstat[cc % 2]
        stold = st.reset()
        c0 = (cc % 2) * 16
        sa = E.t_stat[:, c0:c0 + 1]
        sb_ = E.t_stat[:, c0 + 1:c0 + 2]
        sm = E.t_stat[:, c0 + 2:c0 + 3]
        sq = E.t_stat[:, c0 + 3:c0 + 4]
        sv = E.t_stat[:, c0 + 4:c0 + 5]
        sl = E.t_stat[:, c0 + 5:c0 + 6]
        sr = E.t_stat[:, c0 + 6:c0 + 7]
        t1 = dump_act(E.t_sqd[:, 0:D], x_ap, AF.Copy, [t_x] + stold, sa)
        t2 = dump_act(E.t_sqd[:, 0:D], x_ap, AF.Square, [t_x], sb_)
        st.w += [t1, t2]
        t3 = vts(sm, sa, 1.0 / D, None, ALU.mult, None, [t1])
        t4 = vtt(sq, sm, sm, ALU.mult, [t3])
        t5 = vts(sv, sb_, 1.0 / D, sq, ALU.mult, ALU.subtract, [t2, t4])
        t6 = act(sl, sv, AF.Ln, [t5], bias=EPS)
        t7 = act(sr, sl, AF.Exp, [t6], scale=-0.5)
        st.w += [t3, t4, t5, t6, t7]
        u_tok = tf_tok = None
        if want_bf16_tf:
            tfs = bxtf[cc % 2]
            tfold = tfs.reset()
            tf_ap = E.t_xtf[:, (cc % 2) * D:(cc % 2 + 1) * D]
            tf_tok = vts(tf_ap, x_ap, sm, sr, ALU.subtract, ALU.mult, [t7, t_x] + tfold)
            tfs.w.append(tf_tok)
            xs_.r.append(tf_tok)
            st.r.append(tf_tok)
        if want_f32_u:
            u_tok = vts(u_dst, x_ap, sm, sr, ALU.subtract, ALU.mult,
                        [t7, t_x] + list(u_deps))
            xs_.r.append(u_tok)
            st.r.append(u_tok)
        return t_x, x_ap, u_tok, tf_tok

    def emit_xft(cc, tf_tok):
        ci = cc % 4
        xft_first = bxft.reset() if ci == 0 else []
        t = dmaT(E.a_xft[:].rearrange("p (d q) -> p d q", q=512)
                 [:, :, ci * 128:ci * 128 + 128],
                 E.t_xtf[:, (cc % 2) * D:(cc % 2 + 1) * D]
                 .rearrange("p (d f) -> p d f", d=8),
                 [tf_tok] + xft_first)
        bxft.w.append(t)
        bxtf[cc % 2].r.append(t)
        return [t]

    zctx = {}

    def p1a_front(cc):
        t_x, x_ap, _, tf_tok = emit_ln1(cc, True, False)
        zctx[cc] = emit_xft(cc, tf_tok)
        if cc % GRP == GRP - 1:
            t_xsp = dmaS(E.d_xft.ap()[cc // GRP], E.a_xft[:], list(bxft.w))
            bxft.r.append(t_xsp)
            bdram["xft"].w.append(t_xsp)

    def p1a_back(cc):
        ci = cc % GRP
        xtoks = zctx.pop(cc)
        zPS = PS1 if cc % 2 == 0 else PS0
        zbuf_ = ps1 if cc % 2 == 0 else ps0a
        zdeps = xtoks + wz_toks + zbuf_.reset() + (
            [] if cc % 2 == 0 else ps0b.reset() + ps0c.reset() + ps0d.reset())
        last = None
        for d in range(8):
            lhs = E.a_xft[:, d * 512 + ci * 128: d * 512 + ci * 128 + 128]
            for sl in range(4):
                last = mm(zPS[:, sl * 512:(sl + 1) * 512], lhs,
                          E.a_wz[:, d * DIN + sl * 512: d * DIN + (sl + 1) * 512],
                          d == 0, d == 7, zdeps)
        zbuf_.w.append(last)
        if cc % 2 == 1:
            ps0b.w.append(last); ps0c.w.append(last); ps0d.w.append(last)
        zb = bzbuf[cc % 2]
        zold = zb.reset()
        z_ap = E.a_zbuf[:, 0:DIN] if cc % 2 == 0 else E.a_zbuf2[:, 0:DIN]
        t_silu = act(z_ap, zPS[:, :], AF.Silu, [last] + zold)
        zb.w.append(t_silu)
        zbuf_.r.append(t_silu)
        if cc % 2 == 1:
            ps0b.r.append(t_silu); ps0c.r.append(t_silu); ps0d.r.append(t_silu)
        t_sp = dmaS(E.d_zsp.ap()[cc * 128:(cc + 1) * 128, :], z_ap, [t_silu])
        zb.r.append(t_sp)
        bdram["zsp"].w.append(t_sp)

    p1a_front(0)
    for cc in range(NCH):
        if cc + 1 < NCH:
            p1a_front(cc + 1)
        p1a_back(cc)

    def finalize_early():
        join_all()
        n = s.dcnt[SY]
        fw = []
        for sidx in range(s.POOL):
            uses = (n - sidx + s.POOL - 1) // s.POOL
            if uses > 0:
                fw.append((s.dsem[SY][sidx], 16 * uses))
        s.ops[SY].append((fw, None, None, 0))
        with nc.Block() as block:
            s.replay(block)
        return nc, E.dbg_outs

    if stop_after == "prologue":
        return finalize_early()

    join_all()

    if stop_after == "1a":
        return finalize_early()

    tok_wxbc = dmaG(E.a_wxbc[:], E.h_wxbc.ap(), [])
    if stop_after == "1bwx0":
        return finalize_early()

    t_xh = dmaG(E.t_big[0:3, 0:D], E.h_xh.ap(), bbig.reset())
    bbig.w.append(t_xh)
    sa = E.t_stat[0:3, 20:21]; sb_ = E.t_stat[0:3, 21:22]
    sm = E.t_stat[0:3, 22:23]; sq = E.t_stat[0:3, 23:24]
    sv = E.t_stat[0:3, 24:25]; sl_ = E.t_stat[0:3, 25:26]; sr = E.t_stat[0:3, 26:27]
    th1 = dump_act(E.t_sqd[0:3, 0:D], E.t_big[0:3, 0:D], AF.Copy, [t_xh], sa)
    th2 = dump_act(E.t_sqd[0:3, 0:D], E.t_big[0:3, 0:D], AF.Square, [t_xh], sb_)
    th3 = vts(sm, sa, 1.0 / D, None, ALU.mult, None, [th1])
    th4 = vtt(sq, sm, sm, ALU.mult, [th3])
    th5 = vts(sv, sb_, 1.0 / D, sq, ALU.mult, ALU.subtract, [th2, th4])
    th6 = act(sl_, sv, AF.Ln, [th5], bias=EPS)
    th7 = act(sr, sl_, AF.Exp, [th6], scale=-0.5)
    if stop_after == "1bh1":
        return finalize_early()
    uh_ap = E.t_big[0:3, D:2 * D]
    th8 = vts(uh_ap, E.t_big[0:3, 0:D], sm, sr, ALU.subtract, ALU.mult, [th7])
    bbig.w.append(th8)
    if stop_after == "1bh1a":
        return finalize_early()
    xhft_toks = []
    for d in range(8):
        tt_ = ptr(PS0[0:128, 1536 + d * 4:1536 + d * 4 + 3],
                  uh_ap[:, d * 128:(d + 1) * 128], E.t_eye[0:3, 0:3],
                  [th8] + ps0d.reset())
        ps0d.w.append(tt_)
        if stop_after != "1bh1b":
            if stop_after == "1bh2alt":
                dst = E.t_sqd[:, d * 3:(d + 1) * 3]
            elif stop_after == "1bh2alt2":
                dst = E.a_zbuf[:, d * 3:(d + 1) * 3]
            else:
                dst = E.a_xhft[:, d * 3:(d + 1) * 3]
            te_ = scopy(dst, PS0[0:128, 1536 + d * 4:1536 + d * 4 + 3], [tt_])
            ps0d.r.append(te_)
            xhft_toks.append(te_)
    if stop_after == "1bh1b":
        return finalize_early()
    if False:
        pass
    if stop_after in ("1bh2", "1bh2alt", "1bh2alt2"):
        return finalize_early()
    pre_old = bpre.reset()
    for o in range(17):
        lastm = None
        hdeps = [tok_wxbc] + xhft_toks + ps0d.reset() + pre_old
        for d in range(8):
            lastm = mm(PS0[0:128, 1960:1963],
                       E.a_wxbc[:, o * 1024 + d * 128: o * 1024 + d * 128 + 128],
                       E.a_xhft[:, d * 3:(d + 1) * 3],
                       d == 0, d == 7, hdeps)
        ps0d.w.append(lastm)
        th_e = scopy(E.a_pre[:, o * 515:o * 515 + 3], PS0[0:128, 1960:1963], [lastm])
        ps0d.r.append(th_e)
        bpre.w.append(th_e)
    if stop_after == "1bhalo":
        return finalize_early()

    if stop_after == "1bwx":
        return finalize_early()

    ctxs = {}

    def p1b_prep(g):
        ctx = {}
        ctxs[g] = ctx
        t_xrd = dmaS(E.a_xft[:], E.d_xft.ap()[g], bdram["xft"].w + bxft.reset())
        bxft.w.append(t_xrd)
        yield
        if g > 0:
            pre3 = E.a_pre[:, :].rearrange("p (o c) -> p o c", c=515)
            hs3 = E.a_hsave[:, :].rearrange("p (o c) -> p o c", c=3)
            t_halo_cp = vcopy(pre3[:, :, 0:3], hs3[:, :, :], bhsave.w + bpre.r + bpre.w)
            pre_old = bpre.reset()
            bpre.w.append(t_halo_cp)
            bhsave.r.append(t_halo_cp)
        else:
            pre_old = []
        ctx["pre_old"] = pre_old

        def io_chunk(o):
            reg = [ps0a, ps0b][o % 2]
            regap = PS0[:, (o % 2) * 512:(o % 2) * 512 + 512]
            lastm = None
            rdeps = bxft.w + [tok_wxbc] + reg.reset() + ctx["pre_old"]
            for d in range(8):
                lastm = mm(regap,
                           E.a_wxbc[:, o * 1024 + d * 128: o * 1024 + d * 128 + 128],
                           E.a_xft[:, d * 512:(d + 1) * 512],
                           d == 0, d == 7, rdeps)
            reg.w.append(lastm)
            t_e = scopy(E.a_pre[:, o * 515 + 3:o * 515 + 515], regap,
                        [lastm] + ctx["pre_old"] + list(bpre.w))
            reg.r.append(t_e)
            bpre.w.append(t_e)

        for o in range(9):
            io_chunk(o)
        yield
        for o in range(9, 17):
            io_chunk(o)
        pre3 = E.a_pre[:, :].rearrange("p (o c) -> p o c", c=515)
        hs3 = E.a_hsave[:, :].rearrange("p (o c) -> p o c", c=3)
        t_hs = vcopy(hs3[:, :, :], pre3[:, :, 512:515], bpre.w + bhsave.reset())
        bhsave.w.append(t_hs)
        bpre.r.append(t_hs)

        dt_old = ps0c.reset()
        lastm = None
        for d in range(8):
            lastm = mm(PS0[0:32, 1024:1536],
                       E.t_wdtw[:, d * 32:(d + 1) * 32],
                       E.a_xft[:, d * 512:(d + 1) * 512],
                       d == 0, d == 7, bxft.w + [ld[8]] + dt_old)
        ps0c.w.append(lastm)
        w32old = bw32.reset()
        dtftold = bdtft.reset()
        lndtold = blndt.reset()
        csftold = bcsft.reset()
        ta = vts(E.a_w32[:], PS0[0:32, 1024:1536], E.t_dtb[:], None, ALU.add, None,
                 [lastm, ld[2]] + w32old)
        ps0c.r.append(ta)
        tb = act(E.a_csft[:], E.a_w32[:], AF.Abs, [ta] + csftold)
        tc = act(E.a_lndt[:], E.a_csft[:], AF.Exp, [tb] + lndtold, scale=-1.0)
        td = act(E.a_csft[:], E.a_lndt[:], AF.Ln, [tc], bias=1.0)
        te = vts(E.a_dtft[:], E.a_w32[:], 0.0, None, ALU.max, None, [ta] + dtftold)
        tf_ = vtt(E.a_w32[:], E.a_dtft[:], E.a_csft[:], ALU.add, [td, te])
        bw32.w.append(tf_)
        tg = act(E.a_lndt[:], E.a_w32[:], AF.Ln, [tf_])
        blndt.w.append(tg)
        tlogda = vts(E.a_dtft[:], E.a_w32[:], E.t_expa[:], -1.0, ALU.mult, ALU.mult,
                     [tf_, tok_expa])
        bdtft.w.append(tlogda)
        cs_toks = []
        for ci in range(GRP):
            tcs = vscan(E.a_csft[:, ci * 128:(ci + 1) * 128],
                        E.a_dtft[:, ci * 128:(ci + 1) * 128],
                        E.t_zero32[:], 0.0, ALU.add, ALU.add, [tlogda, tok_z32, td])
            cs_toks.append(tcs)
            bcsft.w.append(tcs)
        stold = bstack.reset()
        tbias = vtt(E.a_stack[0:32, :], E.a_lndt[:], E.a_csft[:], ALU.subtract,
                    cs_toks + [tg] + stold)
        tlamf = act(E.a_stack[64:96, :], E.a_csft[:], AF.Exp, cs_toks + stold)
        bstack.w += [tbias, tlamf]
        bcsft.r += [tbias, tlamf]
        ksl = (g % 2) * 512
        t_kh = vcopy(E.a_cskh[:, ksl:ksl + 512], E.a_csft[:],
                     cs_toks + bcsk[g % 2].reset())
        t_kl = vtt(E.a_cskl[:, ksl:ksl + 512], E.a_csft[:],
                   E.a_cskh[:, ksl:ksl + 512], ALU.subtract, [t_kh])
        bcsk[g % 2].w += [t_kh, t_kl]
        bcsft.r += [t_kh, t_kl]
        ctx["csk"] = [t_kh, t_kl]
        for ci in range(GRP):
            cc = g * GRP + ci
            sl8 = (g % 2) * 4 + ci
            csend = E.a_csft[:, ci * 128 + 127:ci * 128 + 128]
            t1_ = vts(E.a_arg[0:32, 0:128], E.a_csft[:, ci * 128:(ci + 1) * 128],
                      csend, -1.0, ALU.subtract, ALU.mult,
                      cs_toks + barg[0].reset())
            t2_ = act(E.a_arg[0:32, 128:256], E.a_arg[0:32, 0:128], AF.Exp, [t1_])
            t3_ = vtt(E.a_stack[32:64, ci * 128:(ci + 1) * 128],
                      E.a_arg[0:32, 128:256], E.a_w32[:, ci * 128:(ci + 1) * 128],
                      ALU.mult, [t2_, tf_])
            barg[0].w += [t1_, t2_]
            bstack.w.append(t3_)
            bw32.r.append(t3_)
            tt1 = ptr(PS0[0:1, 1632:1664], csend, E.t_eye[0:32, 0:32],
                      ps0d.reset() + cs_toks)
            ps0d.w.append(tt1)
            tt2 = scopy(E.t_stat[0:1, 32:64], PS0[0:1, 1632:1664], [tt1])
            ps0d.r.append(tt2)
            tt3 = mm(PS0[0:64, 1664:1696], E.t_ones[0:1, 0:64], E.t_stat[0:1, 32:64],
                     True, True, [tt2, ld[12]])
            ps0d.w.append(tt3)
            tt4 = scopy(E.t_csend[:, cc * 32:(cc + 1) * 32], PS0[0:64, 1664:1696], [tt3])
            bcsend.w.append(tt4)
            ps0d.r.append(tt4)
            tt5 = ptr(PS0[0:128, 1536:1632], E.a_stack[:, ci * 128:(ci + 1) * 128],
                      E.t_eye[0:96, 0:96], bstack.w + ps0d.reset())
            ps0d.w.append(tt5)
            bstack.r.append(tt5)
            slot = btfdt[sl8]
            tfold = slot.reset()
            tt6 = scopy(E.a_tfdt[:, sl8 * 96:sl8 * 96 + 96],
                        PS0[0:128, 1536:1632], [tt5] + tfold)
            slot.w.append(tt6)
            ps0d.r.append(tt6)
            tt7 = vcopy(E.t_lam[:, cc * 32:(cc + 1) * 32],
                        E.a_tfdt[:, sl8 * 96 + 64:sl8 * 96 + 96], [tt6])
            blam.w.append(tt7)
            slot.r.append(tt7)
            tt8 = act(E.t_gamt[:, sl8 * 32:(sl8 + 1) * 32], PS0[0:64, 1664:1696],
                      AF.Exp, [tt3] + bgamt[sl8].reset())
            bgamt[sl8].w.append(tt8)
            ps0d.r.append(tt8)
        yield
        conv_toks = []
        for o in range(17):
            base = o * 515
            TS_, STT_ = (vts, vstt)
            half = E.a_cwk[:, (o % 2) * 512:(o % 2) * 512 + 512]
            wA = half[:, 0:512]
            d0 = bcwk[o % 2].reset()
            c1 = TS_(wA, E.a_pre[:, base + 0:base + 512], E.t_convw[:, o * 4:o * 4 + 1],
                     None, ALU.mult, None, bpre.w + [ld[0]] + d0)
            c2 = STT_(wA, E.a_pre[:, base + 1:base + 513], E.t_convw[:, o * 4 + 1:o * 4 + 2],
                      wA, ALU.mult, ALU.add, [c1])
            c3 = STT_(wA, E.a_pre[:, base + 2:base + 514], E.t_convw[:, o * 4 + 2:o * 4 + 3],
                      wA, ALU.mult, ALU.add, [c2])
            c4 = STT_(wA, E.a_pre[:, base + 3:base + 515], E.t_convw[:, o * 4 + 3:o * 4 + 4],
                      wA, ALU.mult, ALU.add, [c3])
            bcwk[o % 2].w.append(c4)
            bpre.r += [c1, c2, c3, c4]
            c5 = act(E.a_xbca[:].rearrange("p (c q) -> p c q", q=2176)
                     [:, :, o * 128:(o + 1) * 128], wA, AF.Silu,
                     [c4, ld[1]] + (bxbca.reset() if o == 0 else []),
                     bias=E.t_convb[:, o:o + 1])
            bcwk[o % 2].r.append(c5)
            bxbca.w.append(c5)
            conv_toks.append(c5)
            if o == 8:
                yield
        ctx["conv"] = conv_toks
        yield

    def p1b_scan_T(g, ci):
        ctx = ctxs[g]
        conv_toks = ctx["conv"]
        cc = g * GRP + ci
        slot = cc % 2
        xs_t = E.a_xs if slot == 0 else E.a_xs2
        xs_old = bxs[slot].reset()
        cdep = conv_toks[15] if ci == 0 else conv_toks[16]
        t = dmaT(xs_t[:].rearrange("p (c f) -> p c f", c=16),
                 E.a_xbca[:, ci * 2176:ci * 2176 + 2048],
                 [cdep] + xs_old)
        bxs[slot].w.append(t)
        bxbca.r.append(t)
        xs_toks = [t]
        bslot = bbctf[ci % 2]
        b_old = bslot.reset()
        t_bc = dmaT(E.a_bctf[:, (ci % 2) * 128:(ci % 2) * 128 + 128],
                    E.a_xbca[:, ci * 2176 + 2048: ci * 2176 + 2176],
                    [conv_toks[16]] + b_old)
        bslot.w.append(t_bc)
        bxbca.r.append(t_bc)
        cm_old = ps0d.reset()
        t_cmm = mm(PS0[0:64, 1920:2048], E.t_eyebf[:, 64:128],
                   E.a_xbca[:, ci * 2176 + 2048: ci * 2176 + 2176],
                   True, True, [conv_toks[16], tok_eyebf] + cm_old)
        ps0d.w.append(t_cmm)
        bxbca.r.append(t_cmm)
        t_cst = scopy(E.t_cstore[:, cc * 128:(cc + 1) * 128], PS0[0:64, 1920:2048],
                      [t_cmm] + (bcstore.reset() if cc == 0 else []))
        ps0d.r.append(t_cst)
        bcstore.w.append(t_cst)
        g_old = ps0d.reset()
        t_gmm = mm(PS0[:, 1792:1920],
                   E.a_xbca[0:64, ci * 2176 + 2048: ci * 2176 + 2176],
                   E.t_cstore[:, cc * 128:(cc + 1) * 128],
                   True, True, [conv_toks[16], t_cst] + g_old)
        ps0d.w.append(t_gmm)
        t_gev = vtt(E.t_g[:, slot * 128:(slot + 1) * 128], PS0[:, 1792:1920],
                    E.t_mask01[:], ALU.mult, [t_gmm, ld_mask01] + bg[slot].reset())
        bg[slot].w.append(t_gev)
        ps0d.r.append(t_gev)
        ctx[("sT", ci)] = (xs_toks, t_bc, t_gev)

    def p1b_scan_C(g, ci):
        ctx = ctxs[g]
        cc = g * GRP + ci
        slot = cc % 2
        xs_t = E.a_xs if slot == 0 else E.a_xs2
        sl8 = (g % 2) * 4 + ci
        xs_toks, t_bc, t_gev = ctx[("sT", ci)]
        bslot = bbctf[ci % 2]
        gslot = g % 2
        wdap = bcast64(E.a_tfdt[:, sl8 * 96 + 32:sl8 * 96 + 64], 32)
        gamap = bcast64(E.t_gamt[:, sl8 * 32:(sl8 + 1) * 32], 32)
        ps1_old = ps1.reset()
        y_mms = []

        def j_stage_a(j):
            pb = ps0c if j % 2 == 0 else ps0d
            bank_ap = PS0[:, 1024:1536] if j % 2 == 0 else PS0[:, 1536:2048]
            pb_old = pb.reset()
            cs_mms = []
            for hh in range(4):
                h = 4 * j + hh
                selap = E.t_eyebf[0:32, h:h + 1, None].broadcast_to([32, 1, 128]).squeeze(1)
                ccol = gslot * 512 + ci * 128
                t_h = mm(bank_ap[:, hh * 128:(hh + 1) * 128], selap,
                         E.a_cskh[0:32, ccol:ccol + 128],
                         True, False,
                         ctx["csk"] + [tok_eyebf] + (pb_old if hh == 0 else []))
                t_l = mm(bank_ap[:, hh * 128:(hh + 1) * 128], selap,
                         E.a_cskl[0:32, ccol:ccol + 128],
                         False, True, [t_h])
                pb.w += [t_h, t_l]
                bcsk[gslot].r += [t_h, t_l]
                cs_mms.append(t_l)
            ar_ = barg[j % 2]
            ar_old = ar_.reset()
            arg_ap = E.a_arg[:, (j % 2) * 512:(j % 2) * 512 + 512]
            stt_toks = []
            for hh in range(4):
                h = 4 * j + hh
                t_ = vts(arg_ap[:, hh * 128:(hh + 1) * 128],
                         bank_ap[:, hh * 128:(hh + 1) * 128],
                         E.a_tfdt[:, sl8 * 96 + h:sl8 * 96 + h + 1], 20.0,
                         ALU.add, ALU.min,
                         [cs_mms[-1]] + btfdt[sl8].w + ar_old)
                stt_toks.append(t_)
                pb.r.append(t_)
            ar_.w += stt_toks
            return ar_, arg_ap, stt_toks

        def j_stage_c(j, ar_, arg_ap, stt_toks):
            gm = bgamma[j % 2]
            gm_old = gm.reset()
            gam_ap = E.a_gamma[:, (j % 2) * 512:(j % 2) * 512 + 512]
            t_exp = act(gam_ap, arg_ap, AF.Exp, stt_toks + gm_old)
            gm.w.append(t_exp)
            ar_.r.append(t_exp)
            return gm, gam_ap, t_exp

        def j_stage_de(j, gm, gam_ap, t_exp):
            mt_ = bmt[j % 2]
            mt_old = mt_.reset()
            mt_ap = E.a_mt[:, (j % 2) * 512:(j % 2) * 512 + 512]
            mt_toks = []
            for hh in range(4):
                t_ = vtt(mt_ap[:, hh * 128:(hh + 1) * 128],
                         gam_ap[:, hh * 128:(hh + 1) * 128],
                         E.t_g[:, slot * 128:(slot + 1) * 128],
                         ALU.mult, [t_exp, t_gev] + mt_old)
                mt_toks.append(t_)
            mt_.w += mt_toks
            gm.r += mt_toks
            bg[slot].r += mt_toks
            for hh in range(4):
                h = 4 * j + hh
                t_ = mm(PS1[:, h * 64:(h + 1) * 64],
                        mt_ap[:, hh * 128:(hh + 1) * 128],
                        xs_t[:, h * 64:(h + 1) * 64],
                        True, True,
                        [mt_toks[hh]] + xs_toks + (ps1_old if j == 0 and hh == 0 else []))
                mt_.r.append(t_)
                bxs[slot].r.append(t_)
                y_mms.append(t_)

        pend = None
        for j in range(8):
            abc = j_stage_a(j)
            if pend is not None:
                j_stage_de(j - 1, *pend)
            pend = j_stage_c(j, *abc)
        j_stage_de(7, *pend)
        ps1.w += y_mms
        wd_old = bwdxs.reset()
        t_d = vtt(E.a_wdxs[:], xs_t[:], bcast64(E.t_dtile[:], 32), ALU.mult,
                  xs_toks + [ld[9]] + wd_old)
        bwdxs.w.append(t_d)
        bxs[slot].r.append(t_d)
        yp_ = byp[ci % 2]
        yp_old = yp_.reset()
        yp_ap = E.a_yp[:, (ci % 2) * DIN:(ci % 2 + 1) * DIN]
        t_yp = vtt(yp_ap, E.a_wdxs[:], PS1[:, :], ALU.add,
                   [t_d] + y_mms + yp_old)
        yp_.w.append(t_yp)
        bwdxs.r.append(t_yp)
        ps1.r.append(t_yp)
        t_ysp = dmaS(E.d_ysp.ap()[cc * 128:(cc + 1) * 128, :], yp_ap, [t_yp])
        yp_.r.append(t_ysp)
        bdram["ysp"].w.append(t_ysp)
        wd_old = bwdxs.reset()
        t_wdxs = vtt(E.a_wdxs[:], xs_t[:], wdap, ALU.mult,
                     xs_toks + btfdt[sl8].w + wd_old)
        bwdxs.w.append(t_wdxs)
        bxs[slot].r.append(t_wdxs)
        d_old = ps1.reset()
        d_mms = []
        for h in range(NH):
            t_ = mm(PS1[0:64, h * 64:(h + 1) * 64],
                    E.a_bctf[:, (ci % 2) * 128:(ci % 2) * 128 + 64],
                    E.a_wdxs[:, h * 64:(h + 1) * 64],
                    True, True, [t_bc, t_wdxs] + d_old)
            d_mms.append(t_)
        ps1.w += d_mms
        bwdxs.r += d_mms
        bslot.r += d_mms
        hp = bh[cc % 2]
        hn = bh[(cc + 1) % 2]
        hn_old = hn.reset()
        big_old = bbig.reset()
        t_s1 = vtt(E.t_big[0:64, :], E.a_hprev[:] if cc % 2 == 0 else E.a_hnew[:],
                   gamap, ALU.mult, hp.w + bgamt[sl8].w + big_old)
        hp.r.append(t_s1)
        bgamt[sl8].r.append(t_s1)
        bbig.w.append(t_s1)
        t_s2 = vtt(E.a_hnew[:] if cc % 2 == 0 else E.a_hprev[:],
                   E.t_big[0:64, :], PS1[0:64, :], ALU.add,
                   [t_s1] + d_mms + hn_old)
        hn.w.append(t_s2)
        bbig.r.append(t_s2)
        ps1.r.append(t_s2)
        t_hsp = dmaG(E.d_hsp.ap()[cc],
                     E.a_hnew[:] if cc % 2 == 0 else E.a_hprev[:],
                     [t_s2])
        hn.r.append(t_hsp)
        bdram["hsp"].w.append(t_hsp)
        if debug and cc == 0:
            dbg_y = dmaG(E.dbg_outs["dbg_y0"].ap(), yp_ap, [t_yp])
            yp_.r.append(dbg_y)

    scratch_jobs = []

    def add_jobs(src_h, dst_d, tag, nsplit):
        R = src_h.shape[0]
        step = R // nsplit
        for r0 in range(0, R, step):
            scratch_jobs.append((src_h, dst_d, tag, r0, step))

    add_jobs(E.h_fc1, E.d_fc1s, "fc1s", 4)
    add_jobs(E.h_fc2, E.d_fc2s, "fc2s", 4)
    add_jobs(E.h_wop, E.d_wops, "wops", 2)
    add_jobs(E.h_wk, E.d_wks, "wks", 1)
    add_jobs(E.h_wr, E.d_wrs, "wrs", 1)
    add_jobs(E.h_wv, E.d_wvs, "wvs", 1)

    def emit_scratch_jobs(n):
        for _ in range(n):
            if not scratch_jobs:
                return
            src_h, dst_d, tag, r0, step = scratch_jobs.pop(0)
            t = dmaG(dst_d.ap()[r0:r0 + step, :], src_h.ap()[r0:r0 + step, :], [])
            bdram[tag].w.append(t)

    gen = p1b_prep(0)
    for _ in gen:
        pass
    p1b_scan_T(0, 0)
    for g in range(NGRP):
        nxt = p1b_prep(g + 1) if g + 1 < NGRP else None
        for ci in range(GRP):
            if ci < GRP - 1:
                p1b_scan_T(g, ci + 1)
            if nxt is not None and ci < 3:
                next(nxt, None)
            emit_scratch_jobs(1)
            p1b_scan_C(g, ci)
            if ci == 2 and nxt is not None:
                next(nxt, None)
        if nxt is not None:
            next(nxt, None)
            p1b_scan_T(g + 1, 0)
        if isinstance(stop_after, str) and stop_after == f"1b:g{g}":
            return finalize_early()

    if stop_after == "1b":
        return finalize_early()


    hfin_ap = E.a_hnew[:] if (NCH - 1) % 2 == 0 else E.a_hprev[:]
    hfin = bh[NCH % 2]
    t_m1 = vtt(E.t_big[0:64, :], hfin_ap, E.t_cm[0:64, 0:1, None].broadcast_to([64, 1, DIN]).squeeze(1),
               ALU.mult, hfin.w + [ld[4]] + bbig.reset())
    bbig.w.append(t_m1)
    hfin.r.append(t_m1)
    t_w1 = dmaG(E.d_cc1i.ap(), E.t_big[0:64, :], [t_m1])
    bbig.r.append(t_w1)
    t_cc1 = s.emit_cc(lambda e: e.collective_compute(
        "AllReduce", ALU.add, replica_groups=CC_GROUPS,
        ins=[E.d_cc1i.ap().opt()], outs=[E.d_cc1o.ap().opt()]), [t_w1])
    t_r1 = dmaG(E.t_big[0:64, :], E.d_cc1o.ap(), [t_cc1] + bbig.reset())
    bbig.w.append(t_r1)
    t_hbu_ = vtt(E.t_hbu[:], E.t_big[0:64, :],
                 E.t_um[0:64, 0:1, None].broadcast_to([64, 1, DIN]).squeeze(1),
                 ALU.mult, [t_r1, ld[5]] + bhbu.reset())
    bhbu.w.append(t_hbu_)
    bbig.r.append(t_hbu_)
    if debug:
        t_dbh = dmaG(E.dbg_outs["dbg_hbu"].ap(), E.t_hbu[:], [t_hbu_])
        bhbu.r.append(t_dbh)

    join_all()

    if stop_after == "cc1":
        return finalize_early()

    t_pig0 = vmemset(E.b_pigs[:, 0:32], 0.0, [])
    bpigs.w.append(t_pig0)
    cc2_tok = {}
    out_toks = []

    for g in range(NGRP):
        if isinstance(stop_after, str) and stop_after.startswith("p2:g") and g > int(stop_after[4:]):
            return finalize_early()
        def p2_top(ci):
            cc = g * GRP + ci
            u_ap = E.b_u[:, (ci % 2) * D:(ci % 2 + 1) * D]
            us = bu[ci % 2]
            u_old = us.reset()
            t_x, x_ap, u_tok, _ = emit_ln1(cc, False, True, u_dst=u_ap, u_deps=u_old)
            us.w.append(u_tok)
            yr = byrd[ci % 2]
            yr_old = yr.reset()
            yr_ap = E.b_yrd[:, (ci % 2) * DIN:(ci % 2 + 1) * DIN]
            t_yr = dmaS(yr_ap, E.d_ysp.ap()[cc * 128:(cc + 1) * 128, :],
                        [bdram["ysp"].w[cc]] + yr_old)
            yr.w.append(t_yr)
            zr = bzrd[ci % 2]
            zr_old = zr.reset()
            zr_ap = E.b_zrd[:, (ci % 2) * DIN:(ci % 2 + 1) * DIN]
            t_zr = dmaS(zr_ap, E.d_zsp.ap()[cc * 128:(cc + 1) * 128, :],
                        [bdram["zsp"].w[cc]] + zr_old)
            zr.w.append(t_zr)
            pg_old = bpigt.reset()
            t_pig = act(E.b_pigt[:], E.b_pigs[:, (cc % 2) * 32:(cc % 2) * 32 + 32],
                        AF.Exp, bpigs.w + pg_old)
            bpigt.w.append(t_pig)
            bpigs.r.append(t_pig)
            t_ps = vtt(E.b_pigs[:, ((cc + 1) % 2) * 32:((cc + 1) % 2) * 32 + 32],
                       E.b_pigs[:, (cc % 2) * 32:(cc % 2) * 32 + 32],
                       E.t_csend[:, cc * 32:(cc + 1) * 32], ALU.add,
                       bpigs.w + bcsend.w)
            bpigs.w.append(t_ps)
            hin_old = bhin.reset()
            if cc == 0:
                t_hin = vtt(E.b_hin[:], E.t_hbu[:], bcast64(E.b_pigt[:], 32),
                            ALU.mult, bhbu.w + [t_pig] + hin_old)
            else:
                hl = bhloc[0]
                hl_old = hl.reset()
                hl_ap = E.b_hloc[:, 0:DIN]
                t_hl = dmaS(hl_ap, E.d_hsp.ap()[cc - 1],
                            [bdram["hsp"].w[cc - 1]] + hl_old)
                hl.w.append(t_hl)
                big_old = bbig.reset()
                t_tmp = vtt(E.t_big[0:64, :], E.t_hbu[:], bcast64(E.b_pigt[:], 32),
                            ALU.mult, bhbu.w + [t_pig] + big_old)
                bbig.w.append(t_tmp)
                t_hin = vtt(E.b_hin[:], E.t_big[0:64, :], hl_ap, ALU.add,
                            [t_tmp, t_hl] + hin_old)
                bbig.r.append(t_hin)
                hl.r.append(t_hin)
            bpigt.r.append(t_hin)
            bhin.w.append(t_hin)
            ps1_old = ps1.reset()
            yi_mms = []
            for h in range(NH):
                t_ = mm(PS1[:, h * 64:(h + 1) * 64],
                        E.t_cstore[:, cc * 128:(cc + 1) * 128],
                        E.b_hin[:, h * 64:(h + 1) * 64],
                        True, True, [t_hin] + bcstore.w + (ps1_old if h == 0 else []))
                yi_mms.append(t_)
                bhin.r.append(t_)
                bcstore.r.append(t_)
            ps1.w += yi_mms
            big_old = bbig.reset()
            t_l = vtt(E.t_big[:, :], PS1[:, :],
                      bcast64(E.t_lam[:, cc * 32:(cc + 1) * 32], 32), ALU.mult,
                      yi_mms + blam.w + big_old)
            bbig.w.append(t_l)
            ps1.r.append(t_l)
            blam.r.append(t_l)
            yg_old = byg.reset()
            t_yf = vtt(E.b_yg[:], E.t_big[:, :], yr_ap, ALU.add, [t_l, t_yr] + yg_old)
            byg.w.append(t_yf)
            bbig.r.append(t_yf)
            yr.r.append(t_yf)
            yh_old = byhat.reset()
            t_ygt = vtt(E.b_yhat[:], E.b_yg[:], zr_ap, ALU.mult, [t_yf, t_zr] + yh_old)
            byhat.w.append(t_ygt)
            byg.r.append(t_ygt)
            zr.r.append(t_ygt)
            c0 = (ci % 2) * 16
            sb2 = E.t_stat[:, c0 + 8:c0 + 9]
            sv2 = E.t_stat[:, c0 + 9:c0 + 10]
            sl2 = E.t_stat[:, c0 + 10:c0 + 11]
            sr2 = E.t_stat[:, c0 + 11:c0 + 12]
            t_q = dump_act(E.t_sqd[:, :], E.b_yhat[:], AF.Square, [t_ygt], sb2)
            t_v = vts(sv2, sb2, 1.0 / DIN, None, ALU.mult, None, [t_q])
            t_ln = act(sl2, sv2, AF.Ln, [t_v], bias=EPS)
            t_rs = act(sr2, sl2, AF.Exp, [t_ln], scale=-0.5)
            bstat[ci % 2].w += [t_q, t_v, t_ln, t_rs]
            t_yh = vstt(E.b_yg[:], E.b_yhat[:], sr2, E.t_ssmw[:], ALU.mult, ALU.mult,
                        [t_rs, ld[10]] + byg.reset())
            byg.w.append(t_yh)
            byhat.r.append(t_yh)
            bstat[ci % 2].r.append(t_yh)
            yft_first = byft.reset() if ci % 2 == 0 else []
            t_ = dmaT(E.b_yft[:].rearrange("p (k q) -> p k q", q=256)
                      [:, :, (ci % 2) * 128:(ci % 2) * 128 + 128],
                      E.b_yg[:].rearrange("p (k f) -> p k f", k=16),
                      [t_yh] + yft_first)
            byft.w.append(t_)
            byg.r.append(t_)
            yft_toks = [t_]

            if debug and cc == 0:
                d1 = dmaG(E.dbg_outs["dbg_z0"].ap(), zr_ap, [t_zr])
                zr.r.append(d1)
                d2 = dmaG(E.dbg_outs["dbg_yfull0"].ap(), E.b_yg[:], [t_yh])
                byg.r.append(d2)


        def p2_pair(ci):
            cc = g * GRP + ci
            pr_old = (ps0a.reset() + ps0b.reset() + ps0c.reset() + ps0d.reset())
            mo_mms = []
            for k in range(16):
                ws = bwstr[k % 4]
                ws_old = ws.reset()
                ws_ap = E.b_wstr[:, (k % 4) * D:(k % 4 + 1) * D]
                t_wl = dmaG(ws_ap, E.d_wops.ap()[k * 128:(k + 1) * 128, :],
                            ws_old)
                ws.w.append(t_wl)
                for co in range(2):
                    for n in range(2):
                        t_ = mm(PS0[:, co * 1024 + n * 512: co * 1024 + (n + 1) * 512],
                                E.b_yft[:, k * 256 + co * 128: k * 256 + co * 128 + 128],
                                ws_ap[:, n * 512:(n + 1) * 512],
                                k == 0, k == 15,
                                [t_wl] + byft.w + (pr_old if k == 0 else []))
                        mo_mms.append(t_)
                        ws.r.append(t_)
                        byft.r.append(t_)
            ps0a.w += mo_mms
            ps0b.w += mo_mms
            ps0c.w += mo_mms
            ps0d.w += mo_mms
            for co in range(2):
                ccp = cc - 1 + co
                cip = ccp % 4
                x2s = bx2[ccp % 2]
                x2_old = x2s.reset()
                x2_ap = E.b_x2[:, (ccp % 2) * D:(ccp % 2 + 1) * D]
                t_x2 = vtt(x2_ap, E.b_u[:, (ccp % 2) * D:(ccp % 2 + 1) * D],
                           PS0[:, co * 1024:co * 1024 + 1024], ALU.add,
                           mo_mms + bu[ccp % 2].w + x2_old)
                x2s.w.append(t_x2)
                bu[ccp % 2].r.append(t_x2)
                for b_ in (ps0a, ps0b, ps0c, ps0d):
                    b_.r.append(t_x2)
                w2s = bw2[cip]
                w2_old = w2s.reset()
                w2_ap = E.b_w2[:, cip * D:(cip + 1) * D]
                xslot = bx[ccp % 2]
                t_w2 = vtt(w2_ap, x2_ap,
                           E.t_xr[:, (ccp % 2) * D:(ccp % 2 + 1) * D], ALU.add,
                           [t_x2] + xslot.w + w2_old)
                w2s.w.append(t_w2)
                xslot.r.append(t_w2)
                x2s.r.append(t_w2)
                c0p = (ccp % 2) * 16
                sb3 = E.t_stat[:, c0p + 12:c0p + 13]
                sv3 = E.t_stat[:, c0p + 13:c0p + 14]
                sl3 = E.t_stat[:, c0p + 14:c0p + 15]
                sr3 = E.t_stat[:, c0p + 15:c0p + 16]
                t_q3 = dump_act(E.t_sqd[:, 0:D], x2_ap, AF.Square, [t_x2], sb3)
                t_v3 = vts(sv3, sb3, 1.0 / D, None, ALU.mult, None, [t_q3])
                t_l3 = act(sl3, sv3, AF.Ln, [t_v3], bias=EPS)
                t_r3 = act(sr3, sl3, AF.Exp, [t_l3], scale=-0.5)
                tfs = bxtf[ccp % 2]
                tf_old = tfs.reset()
                tf_ap = E.t_xtf[:, (ccp % 2) * D:(ccp % 2 + 1) * D]
                bstat[ccp % 2].w += [t_q3, t_v3, t_l3, t_r3]
                t_x2h = vstt(tf_ap, x2_ap, sr3, E.t_rmsw[:], ALU.mult, ALU.mult,
                             [t_r3, ld[11]] + tf_old)
                tfs.w.append(t_x2h)
                x2s.r.append(t_x2h)
                bstat[ccp % 2].r.append(t_x2h)
                if debug and ccp == 0:
                    d3 = dmaG(E.dbg_outs["dbg_x2_0"].ap(), x2_ap, [t_x2])
                    x2s.r.append(d3)
                if cip == 0:
                    x2ft_first_deps = bx3ft.reset()
                t_ = dmaT(E.b_x3ft[:].rearrange("p (d q) -> p d q", q=528)
                          [:, :, cip * 128:cip * 128 + 128],
                          tf_ap[:].rearrange("p (d f) -> p d f", d=8),
                          [t_x2h] + (x2ft_first_deps if cip == 0 else []))
                bx3ft.w.append(t_)
                tfs.r.append(t_)


        p2_top(0)
        p2_top(1)
        p2_pair(1)
        p2_top(2)
        p2_top(3)
        p2_pair(3)

        fc1_deps = bx3ft.w + bdram["fc1s"].w
        gt_old = bgated.reset()
        gated_toks = []
        fc1ps = [(ps0a, 0), (ps0b, 512), (ps0c, 1024), (ps0d, 1536)]
        for i in range(32):
            fa = bfstr[(2 * i) % 4]
            fg = bfstr[(2 * i + 1) % 4]
            fa_old = fa.reset()
            fg_old = fg.reset()
            fa_ap = E.b_fstr[:, ((2 * i) % 4) * 1024:((2 * i) % 4 + 1) * 1024]
            fg_ap = E.b_fstr[:, ((2 * i + 1) % 4) * 1024:((2 * i + 1) % 4 + 1) * 1024]
            t_fa = dmaG(fa_ap, E.d_fc1s.ap()[:, i * 1024:(i + 1) * 1024],
                        bdram["fc1s"].w + fa_old)
            fa.w.append(t_fa)
            t_fg = dmaG(fg_ap, E.d_fc1s.ap()[:, (32 + i) * 1024:(33 + i) * 1024],
                        bdram["fc1s"].w + fg_old)
            fg.w.append(t_fg)
            pa, ca = fc1ps[(i % 2) * 2]
            pg, cg = fc1ps[(i % 2) * 2 + 1]
            a_old = pa.reset()
            g_old2 = pg.reset()
            lastA = lastG = None
            for d in range(8):
                lastA = mm(PS0[:, ca:ca + 512], fa_ap[:, d * 128:(d + 1) * 128],
                           E.b_x3ft[:, d * 528:d * 528 + 512], d == 0, d == 7,
                           [t_fa] + fc1_deps + (a_old if d == 0 else []))
                fa.r.append(lastA)
            for d in range(8):
                lastG = mm(PS0[:, cg:cg + 512], fg_ap[:, d * 128:(d + 1) * 128],
                           E.b_x3ft[:, d * 528:d * 528 + 512], d == 0, d == 7,
                           [t_fg] + fc1_deps + (g_old2 if d == 0 else []))
                fg.r.append(lastG)
            pa.w.append(lastA)
            pg.w.append(lastG)
            sg_old = bsg.reset()
            t_sg = act(E.b_sg[:], PS0[:, cg:cg + 512], AF.Silu, [lastG] + sg_old)
            bsg.w.append(t_sg)
            pg.r.append(t_sg)
            t_gt = vtt(E.b_gated[:, i * 512:(i + 1) * 512], E.b_sg[:], PS0[:, ca:ca + 512],
                       ALU.mult, [t_sg, lastA] + (gt_old if i == 0 else []))
            bgated.w.append(t_gt)
            bsg.r.append(t_gt)
            pa.r.append(t_gt)
            gated_toks.append(t_gt)
        bx3ft.r.append(gated_toks[-1])

        q_old = (ps0a.reset() + ps0b.reset() + ps0c.reset() + ps0d.reset()
                 + ps1.reset())
        QAP = [PS0[:, 0:1024], PS0[:, 1024:2048], PS1[:, 0:1024], PS1[:, 1024:2048]]
        m_mms = []
        for k in range(32):
            ws = bwstr[k % 4]
            ws_old = ws.reset()
            ws_ap = E.b_wstr[:, (k % 4) * D:(k % 4 + 1) * D]
            t_wl = dmaG(ws_ap, E.d_fc2s.ap()[k * 128:(k + 1) * 128, :],
                        bdram["fc2s"].w + ws_old)
            ws.w.append(t_wl)
            for cq in range(4):
                for n in range(2):
                    t_ = mm(QAP[cq][:, n * 512:(n + 1) * 512],
                            E.b_gated[:, k * 512 + cq * 128: k * 512 + cq * 128 + 128],
                            ws_ap[:, n * 512:(n + 1) * 512],
                            k == 0, k == 31,
                            [t_wl] + gated_toks + (q_old if k == 0 else []))
                    m_mms.append(t_)
                    ws.r.append(t_)
                    bgated.r.append(t_)
        for b_ in (ps0a, ps0b, ps0c, ps0d, ps1):
            b_.w += m_mms

        x3v = E.b_x3ft[:, :].rearrange("p (d c) -> p d c", c=528)
        if g == 0:
            t_hz = vmemset(x3v[:, :, 15:16], 0.0, bx3ft.reset())
        else:
            prev3 = bx3ft.reset()
            t_hz = vcopy(x3v[:, :, 15:16], x3v[:, :, 527:528], prev3)
        bx3ft.w.append(t_hz)
        for ci in range(GRP):
            cc = g * GRP + ci
            w2_ap = E.b_w2[:, ci * D:(ci + 1) * D]
            t_xa = vtt(w2_ap, QAP[ci], w2_ap, ALU.add, m_mms + bw2[ci].w)
            bw2[ci].w.append(t_xa)
            for b_ in (ps0a, ps0b, ps0c, ps0d, ps1):
                b_.r.append(t_xa)
            if debug and cc == 0:
                d4 = dmaG(E.dbg_outs["dbg_x3_0"].ap(), w2_ap, [t_xa])
                bw2[ci].r.append(d4)
            c0 = (ci % 2) * 16
            sa = E.t_stat[:, c0:c0 + 1]; sb_ = E.t_stat[:, c0 + 1:c0 + 2]
            sm = E.t_stat[:, c0 + 2:c0 + 3]; sq = E.t_stat[:, c0 + 3:c0 + 4]
            sv = E.t_stat[:, c0 + 4:c0 + 5]; sl_ = E.t_stat[:, c0 + 5:c0 + 6]
            sr = E.t_stat[:, c0 + 6:c0 + 7]
            st = bstat[ci % 2]
            st_old = st.reset()
            t1 = dump_act(E.t_sqd[:, 0:D], w2_ap, AF.Copy, [t_xa] + st_old, sa)
            t2 = dump_act(E.t_sqd[:, 0:D], w2_ap, AF.Square, [t_xa], sb_)
            t3 = vts(sm, sa, 1.0 / D, None, ALU.mult, None, [t1])
            t4 = vtt(sq, sm, sm, ALU.mult, [t3])
            t5 = vts(sv, sb_, 1.0 / D, sq, ALU.mult, ALU.subtract, [t2, t4])
            t6 = act(sl_, sv, AF.Ln, [t5], bias=EPS)
            t7 = act(sr, sl_, AF.Exp, [t6], scale=-0.5)
            st.w += [t1, t2, t3, t4, t5, t6, t7]
            tfs = bxtf[ci % 2]
            tf_old = tfs.reset()
            tf_ap = E.t_xtf[:, (ci % 2) * D:(ci % 2 + 1) * D]
            t_x3 = vts(tf_ap, w2_ap, sm, sr, ALU.subtract, ALU.mult, [t7] + tf_old)
            tfs.w.append(t_x3)
            st.r.append(t_x3)
            bw2[ci].r.append(t_x3)
            if cc == 0:
                t_r0 = vcopy(E.t_x3n0[:], tf_ap[0:1, :], [t_x3])
                tfs.r.append(t_r0)
                t_a0 = vcopy(E.t_xa0[:], w2_ap[0:1, :], [t_xa])
                bw2[ci].r.append(t_a0)
            if cc == T // 128 - 1:
                t_re = dmaS(E.d_row1.ap(), tf_ap[127:128, :], [t_x3])
                tfs.r.append(t_re)
                t_rb = dmaG(E.t_halo[:], E.d_row1.ap(), [t_re])
                t_msk = vts(E.t_big[0:1, 0:D], E.t_halo[:], E.t_cm[0:1, 0:1], None,
                            ALU.mult, None, [t_rb, ld[4]] + bbig.reset())
                bbig.w.append(t_msk)
                t_wc2 = dmaS(E.d_cc2i.ap(), E.t_big[0:1, 0:D], [t_msk])
                bbig.r.append(t_wc2)
                cc2_tok["t"] = s.emit_cc(lambda e: e.collective_compute(
                    "AllReduce", ALU.add, replica_groups=CC_GROUPS,
                    ins=[E.d_cc2i.ap().opt()], outs=[E.d_cc2o.ap().opt()]), [t_wc2])
            t_ = dmaT(E.b_x3ft[:].rearrange("p (d q) -> p d q", q=528)
                      [:, :, 16 + ci * 128:16 + ci * 128 + 128],
                      tf_ap[:].rearrange("p (d f) -> p d f", d=8),
                      [t_x3] + bx3ft.w[:1])
            bx3ft.w.append(t_)
            tfs.r.append(t_)

        xk_old = byft.reset()
        xk_toks = []
        for d in range(8):
            dst = E.b_yft[:, d * 512:(d + 1) * 512]
            t_t = vts(dst, x3v[:, d:d + 1, 16:528].squeeze(1), E.t_mukp[:, d:d + 1],
                      None, ALU.mult, None,
                      bx3ft.w + [ld[7]] + (xk_old if d == 0 else []))
            t_xk = vstt(dst, x3v[:, d:d + 1, 15:527].squeeze(1), E.t_omkp[:, d:d + 1],
                        dst, ALU.mult, ALU.add, [t_t, tok_omk])
            byft.w += [t_t, t_xk]
            bx3ft.r += [t_t, t_xk]
            xk_toks.append(t_xk)
        ka_old = bka.reset()
        for o in range(8):
            fs = bfstr[o % 4]
            fs_old = fs.reset()
            fs_ap = E.b_fstr[:, (o % 4) * 1024:(o % 4 + 1) * 1024]
            t_wl = dmaG(fs_ap, E.d_wks.ap()[:, o * 1024:(o + 1) * 1024],
                        bdram["wks"].w + fs_old)
            fs.w.append(t_wl)
            reg = [ps0a, ps0b][o % 2]
            regap = PS0[:, (o % 2) * 512:(o % 2) * 512 + 512]
            r_old = reg.reset()
            lastm = None
            for d in range(8):
                lastm = mm(regap, fs_ap[:, d * 128:(d + 1) * 128],
                           E.b_yft[:, d * 512:(d + 1) * 512], d == 0, d == 7,
                           [t_wl, xk_toks[d]] + (r_old if d == 0 else []))
                fs.r.append(lastm)
                byft.r.append(lastm)
            reg.w.append(lastm)
            rk_old = brk.reset()
            t_rl = act(E.b_rk[:], regap, AF.Relu, [lastm] + rk_old)
            brk.w.append(t_rl)
            reg.r.append(t_rl)
            t_ka = act(E.b_ka[:, o * 512:(o + 1) * 512], E.b_rk[:], AF.Square,
                       [t_rl] + (ka_old if o == 0 else []))
            bka.w.append(t_ka)
            brk.r.append(t_ka)
        xr_old = byft.reset()
        xr_toks = []
        for d in range(8):
            dst = E.b_yft[:, d * 512:(d + 1) * 512]
            t_t = vts(dst, x3v[:, d:d + 1, 16:528].squeeze(1), E.t_murp[:, d:d + 1],
                      None, ALU.mult, None,
                      bx3ft.w + [ld[6]] + (xr_old if d == 0 else []))
            t_xr = vstt(dst, x3v[:, d:d + 1, 15:527].squeeze(1), E.t_omrp[:, d:d + 1],
                        dst, ALU.mult, ALU.add, [t_t, tok_omr])
            byft.w += [t_t, t_xr]
            bx3ft.r += [t_t, t_xr]
            xr_toks.append(t_xr)
        for pp in range(2):
            r_mms = {0: [], 1: []}
            pr_old = ps1.reset()
            for d in range(8):
                ws = bwstr[d % 4]
                ws_old = ws.reset()
                ws_ap = E.b_wstr[:, (d % 4) * D:(d % 4 + 1) * D]
                t_wl = dmaG(ws_ap, E.d_wrs.ap()[d * 128:(d + 1) * 128, :],
                            bdram["wrs"].w + ws_old)
                ws.w.append(t_wl)
                for co in range(2):
                    ci = pp * 2 + co
                    for n in range(2):
                        t_ = mm(PS1[:, co * 1024 + n * 512: co * 1024 + (n + 1) * 512],
                                E.b_yft[:, d * 512 + ci * 128: d * 512 + ci * 128 + 128],
                                ws_ap[:, n * 512:(n + 1) * 512],
                                d == 0, d == 7,
                                [t_wl, xr_toks[d]] + (pr_old if d == 0 else []))
                        r_mms[co].append(t_)
                        ws.r.append(t_)
                        byft.r.append(t_)
            ps1.w += r_mms[0] + r_mms[1]
            r_toks = []
            for co in range(2):
                rs = brst[co]
                rs_old = rs.reset()
                rs_ap = E.b_rst[:, co * D:(co + 1) * D]
                t_sig = act(rs_ap, PS1[:, co * 1024:(co + 1) * 1024], AF.Sigmoid,
                            r_mms[co] + rs_old)
                rs.w.append(t_sig)
                ps1.r.append(t_sig)
                r_toks.append(t_sig)
            v_mms = {0: [], 1: []}
            pv_old = ps0a.reset() + ps0b.reset() + ps0c.reset() + ps0d.reset()
            for kc in range(8):
                ws = bwstr[kc % 4]
                ws_old = ws.reset()
                ws_ap = E.b_wstr[:, (kc % 4) * D:(kc % 4 + 1) * D]
                t_wl = dmaG(ws_ap, E.d_wvs.ap()[kc * 128:(kc + 1) * 128, :],
                            bdram["wvs"].w + ws_old)
                ws.w.append(t_wl)
                for co in range(2):
                    ci = pp * 2 + co
                    for n in range(2):
                        t_ = mm(PS0[:, co * 1024 + n * 512: co * 1024 + (n + 1) * 512],
                                E.b_ka[:, kc * 512 + ci * 128: kc * 512 + ci * 128 + 128],
                                ws_ap[:, n * 512:(n + 1) * 512],
                                kc == 0, kc == 7,
                                [t_wl] + bka.w + (pv_old if kc == 0 else []))
                        v_mms[co].append(t_)
                        ws.r.append(t_)
                        bka.r.append(t_)
            ps0a.w += v_mms[0] + v_mms[1]
            ps0b.w += v_mms[0] + v_mms[1]
            ps0c.w += v_mms[0] + v_mms[1]
            ps0d.w += v_mms[0] + v_mms[1]
            for co in range(2):
                ci = pp * 2 + co
                cc = g * GRP + ci
                big_old = bbig.reset()
                t_tv = vtt(E.t_big[:, 0:1024], E.b_rst[:, co * D:(co + 1) * D],
                           PS0[:, co * 1024:(co + 1) * 1024], ALU.mult,
                           v_mms[co] + r_toks + big_old)
                bbig.w.append(t_tv)
                brst[co].r.append(t_tv)
                for b_ in (ps0a, ps0b, ps0c, ps0d):
                    b_.r.append(t_tv)
                x2s = bx2[ci % 2]
                x2_old = x2s.reset()
                x4_ap = E.b_x2[:, (ci % 2) * D:(ci % 2 + 1) * D]
                t_x4 = vtt(x4_ap, E.b_w2[:, ci * D:(ci + 1) * D], E.t_big[:, 0:1024],
                           ALU.add, [t_tv] + bw2[ci].w + x2_old)
                x2s.w.append(t_x4)
                bw2[ci].r.append(t_x4)
                bbig.r.append(t_x4)
                t_out = dmaS(E.h_out.ap()[cc * 128:(cc + 1) * 128, :], x4_ap, [t_x4])
                x2s.r.append(t_out)
                out_toks.append(t_out)

    if stop_after == "p2":
        return finalize_early()

    join_all()

    t_mr = dmaS(E.p_murrow[:], E.h_murrow.ap(), [])
    t_mk = dmaS(E.p_mukrow[:], E.h_mukrow.ap(), [])
    t_o1 = vts(E.p_omr[:], E.p_murrow[:], -1.0, 1.0, ALU.mult, ALU.add, [t_mr])
    t_o2 = vts(E.p_omk[:], E.p_mukrow[:], -1.0, 1.0, ALU.mult, ALU.add, [t_mk])
    t_rc2 = dmaG(E.t_big[0:1, 0:D], E.d_cc2o.ap(), [cc2_tok["t"]] + bbig.reset())
    bbig.w.append(t_rc2)
    t_hal = vts(E.t_halo[:], E.t_big[0:1, 0:D], E.t_um[0:1, 0:1], None,
                ALU.mult, None, [t_rc2, ld[5]])
    bbig.r.append(t_hal)

    def patch_rows(x3n0, halo, murow, omrow, outC, extra):
        tA = vtt(E.p_rowA[:], x3n0, murow, ALU.mult,
                 [t_hal, t_mr, t_mk, t_o1, t_o2] + extra)
        tB = vtt(E.p_rowB[:], halo, omrow, ALU.mult, [tA])
        tC = vtt(outC, E.p_rowA[:], E.p_rowB[:], ALU.add, [tA, tB])
        return tC

    t_xr0 = patch_rows(E.t_x3n0[:], E.t_halo[:], E.p_murrow[:], E.p_omr[:],
                       E.p_rowC[:], [])
    t_xk0 = patch_rows(E.t_x3n0[:], E.t_halo[:], E.p_mukrow[:], E.p_omk[:],
                       E.p_rowD[:], [t_xr0])

    def row_to_cols_bf(row_ap, col_base, dep):
        toks = []
        for d in range(8):
            tm = mm(PS0[0:128, 1960 + d % 2:1961 + d % 2],
                    row_ap[:, d * 128:(d + 1) * 128],
                    E.t_eyebf[0:1, 0:1], True, True, [dep, tok_eyebf] + ps0d.reset())
            ps0d.w.append(tm)
            te_ = scopy(E.p_wcol[:, col_base + d:col_base + d + 1],
                        PS0[0:128, 1960 + d % 2:1961 + d % 2], [tm])
            ps0d.r.append(te_)
            toks.append(te_)
        return toks

    xr0c = row_to_cols_bf(E.p_rowC[:], 0, t_xr0)
    xk0c = row_to_cols_bf(E.p_rowD[:], 8, t_xk0)

    def patch_mm(col_base, wsrc, wbuf_d, psreg, psbuf, deps):
        mms = []
        for d in range(8):
            ws = bwstr[d % 4]
            ws_old = ws.reset()
            ws_ap = E.b_wstr[:, (d % 4) * D:(d % 4 + 1) * D]
            t_wl = dmaG(ws_ap, wsrc.ap()[d * 128:(d + 1) * 128, :],
                        wbuf_d.w + ws_old)
            ws.w.append(t_wl)
            for n in range(2):
                t_ = mm(psreg[:, n * 512:(n + 1) * 512],
                        E.p_wcol[:, col_base + d:col_base + d + 1],
                        ws_ap[:, n * 512:(n + 1) * 512],
                        d == 0, d == 7,
                        [t_wl] + deps + (psbuf.reset() if d == 0 and n == 0 else []))
                mms.append(t_)
                ws.r.append(t_)
        psbuf.w += mms
        return mms

    r0_mms = patch_mm(0, E.h_wr, bdram["wrs"], PS1[0:1, 0:1024], ps1, xr0c)
    t_r0 = act(E.p_rowC[:], PS1[0:1, 0:1024], AF.Sigmoid, r0_mms)
    ps1.r.append(t_r0)
    bwk2 = Buf()
    k0_mms = patch_mm(8, E.h_wkt, bwk2, PS1[0:1, 1024:2048], ps1, xk0c)
    t_rl0 = act(E.p_rowD[:], PS1[0:1, 1024:2048], AF.Relu, k0_mms)
    ps1.r.append(t_rl0)
    t_ka0 = vtt(E.p_rowE[:], E.p_rowD[:], E.p_rowD[:], ALU.mult, [t_rl0])
    ka0c = row_to_cols_bf(E.p_rowE[:], 16, t_ka0)
    v0_mms = patch_mm(16, E.h_wv, bdram["wvs"], PS0[0:1, 0:1024],
                      ps0a, ka0c)
    t_tv0 = vtt(E.p_rowA[:], E.p_rowC[:], PS0[0:1, 0:1024], ALU.mult,
                v0_mms + [t_r0])
    ps0a.r.append(t_tv0)
    t_x40 = vtt(E.p_rowB[:], E.t_xa0[:], E.p_rowA[:], ALU.add, [t_tv0])
    t_out0 = dmaS(E.h_out.ap()[0:1, :], E.p_rowB[:], [t_x40])
    out_toks.append(t_out0)

    n = s.dcnt[SY]
    fw = []
    for sidx in range(s.POOL):
        uses = (n - sidx + s.POOL - 1) // s.POOL
        if uses > 0:
            fw.append((s.dsem[SY][sidx], 16 * uses))
    s.ops[SY].append((fw, None, None, 0))

    with nc.Block() as block:
        s.replay(block)
    return nc, E.dbg_outs



_NC_CACHE = {}


def _get_nc(debug=False):
    key = bool(debug)
    if key not in _NC_CACHE:
        _NC_CACHE[key] = build_nc(debug)
    return _NC_CACHE[key]


def kernel(**inputs):
    in_maps = _host_layout(inputs)
    nc, _ = _get_nc(False)
    res = run_bass_kernel_spmd(nc, in_maps, core_ids=list(range(NCORE)))
    out = np.empty((4, 4096, 1024), np.float32)
    for c in range(NCORE):
        b, half = c // 2, c % 2
        out[b, half * T:(half + 1) * T] = res.results[c]["out"]
    return out


# version 26
# speedup vs baseline: 1.0102x; 1.0041x over previous
import numpy as np
import concourse.bass as bass
import concourse.mybir as mybir
from concourse.bass_utils import run_bass_kernel_spmd

F32 = mybir.dt.float32
BF16 = mybir.dt.bfloat16
AF = mybir.ActivationFunctionType
ALU = mybir.AluOpType

NCORE = 8
T = 2048
Q = 128
NCH = 16
GRP = 4
NGRP = 4
D = 1024
DIN = 2048
NH, HD, DS = 32, 64, 64
CONVD = 2176
FF = 8192
EPS = 1e-5
NEG = -1e30

CC_GROUPS = [[0, 1], [2, 3], [4, 5], [6, 7]]



def _host_layout(inputs):
    ip = np.ascontiguousarray(inputs["L0_attn_in_proj_w"], np.float32)
    shared = {}
    wxbc = ip[2048:4224]
    shared["wxbc_p"] = np.ascontiguousarray(
        wxbc.reshape(17, 128, 8, 128).transpose(3, 0, 2, 1)).reshape(128, 17 * 8 * 128)
    wdt = ip[4224:4256]
    shared["wdt_p"] = np.ascontiguousarray(
        wdt.reshape(32, 8, 128).transpose(2, 1, 0)).reshape(128, 8 * 32)
    shared["wz_t"] = np.ascontiguousarray(ip[0:2048].T)
    shared["wop_t"] = np.ascontiguousarray(
        np.asarray(inputs["L0_attn_out_proj_w"], np.float32).T)
    fc1 = np.asarray(inputs["L0_attn_fc1_w"], np.float32)
    shared["fc1_p"] = np.ascontiguousarray(
        fc1.reshape(64, 128, 8, 128).transpose(3, 0, 2, 1)).reshape(128, 64 * 8 * 128)
    shared["fc2_t"] = np.ascontiguousarray(
        np.asarray(inputs["L0_attn_fc2_w"], np.float32).T)
    wk = np.asarray(inputs["L0_ffn_W_k"], np.float32)
    shared["wk_p"] = np.ascontiguousarray(
        wk.reshape(8, 128, 8, 128).transpose(3, 0, 2, 1)).reshape(128, 8 * 8 * 128)
    shared["wr_t"] = np.ascontiguousarray(np.asarray(inputs["L0_ffn_W_r"], np.float32).T)
    shared["wkt_t"] = np.ascontiguousarray(wk.T)
    shared["wv_t"] = np.ascontiguousarray(np.asarray(inputs["L0_ffn_W_v"], np.float32).T)
    cw = np.asarray(inputs["L0_attn_conv_w"], np.float32)
    shared["conv_w_p"] = np.ascontiguousarray(
        cw.reshape(17, 128, 4).transpose(1, 0, 2)).reshape(128, 68)
    cb = np.asarray(inputs["L0_attn_conv_b"], np.float32)
    shared["conv_b_p"] = np.ascontiguousarray(cb.reshape(17, 128).T)
    shared["dt_bias_c"] = np.asarray(inputs["L0_attn_dt_bias"], np.float32).reshape(32, 1)
    shared["a_log_c"] = np.asarray(inputs["L0_attn_A_log"], np.float32).reshape(32, 1)
    shared["d_row"] = np.asarray(inputs["L0_attn_D"], np.float32).reshape(1, 32)
    shared["ssm_w_row"] = np.asarray(
        inputs["L0_attn_ssm_norm_w"], np.float32).reshape(1, 2048)
    shared["rms_w_row"] = np.asarray(inputs["L0_attn_rms_w"], np.float32).reshape(1, 1024)
    shared["mu_r_p"] = np.ascontiguousarray(
        np.asarray(inputs["L0_ffn_mu_r"], np.float32).reshape(8, 128).T)
    shared["mu_k_p"] = np.ascontiguousarray(
        np.asarray(inputs["L0_ffn_mu_k"], np.float32).reshape(8, 128).T)
    shared["mu_r_row"] = np.asarray(inputs["L0_ffn_mu_r"], np.float32).reshape(1, 1024)
    shared["mu_k_row"] = np.asarray(inputs["L0_ffn_mu_k"], np.float32).reshape(1, 1024)

    x = np.asarray(inputs["L0_attn_x"], np.float32)
    in_maps = []
    for c in range(NCORE):
        b, half = c // 2, c % 2
        m = dict(shared)
        m["x"] = np.ascontiguousarray(x[b, half * T:(half + 1) * T])
        if half == 1:
            m["xh"] = np.ascontiguousarray(x[b, T - 3:T])
        else:
            m["xh"] = np.zeros((3, 1024), np.float32)
        m["cmcol"] = np.full((128, 1), 1.0 if half == 0 else 0.0, np.float32)
        m["umcol"] = np.full((128, 1), 0.0 if half == 0 else 1.0, np.float32)
        in_maps.append(m)
    return in_maps



class Slot:
    __slots__ = ("writer", "readers")

    def __init__(self):
        self.writer = None
        self.readers = []


class Ring:

    def __init__(self, n):
        self.slots = [Slot() for _ in range(n)]
        self.n = n

    def slot(self, i):
        return self.slots[i % self.n]


class Sched:

    ENGINES = ("tensor", "vector", "scalar", "sync", "gpsimd")
    QUEUES = ("sync", "gpsimd", "scalar")
    POOL = 8

    def __init__(self, nc):
        self.nc = nc
        self.sem = {e: nc.alloc_semaphore(name=f"sem_{e}") for e in self.ENGINES}
        self.cnt = {e: 0 for e in self.ENGINES}
        self.dsem = {q: [nc.alloc_semaphore(name=f"dma_{q}_{i}")
                         for i in range(self.POOL)] for q in self.QUEUES}
        self.dcnt = {q: 0 for q in self.QUEUES}
        self.ccsem = nc.alloc_semaphore(name="ccsem")
        self.cccnt = 0
        self.ops = {e: [] for e in self.ENGINES}
        self.mark = {e: {} for e in self.ENGINES}

    def _resolve(self, eng, deps):
        waits = []
        for d in deps:
            if d is None:
                continue
            if d[0] == "D":
                _, q, sidx, target = d
                key = ("D", q, sidx)
                sem = self.dsem[q][sidx]
            elif d[0] == "C":
                key = ("C",)
                sem = self.ccsem
                target = d[1]
            else:
                se, target = d
                if se == eng and se == "tensor":
                    continue
                key = ("E", se)
                sem = self.sem[se]
            if self.mark[eng].get(key, 0) >= target:
                continue
            self.mark[eng][key] = target
            waits.append((sem, target))
        return waits

    def emit(self, eng, fn, deps=(), inc=1):
        waits = self._resolve(eng, deps)
        self.cnt[eng] += inc
        self.ops[eng].append((waits, fn, self.sem[eng], inc))
        return (eng, self.cnt[eng])

    def dma(self, eng, fn, deps=()):
        q = eng
        j = self.dcnt[q]
        self.dcnt[q] = j + 1
        sidx = j % self.POOL
        use = j // self.POOL
        dd = list(deps)
        if use > 0:
            dd.append(("D", q, sidx, 16 * use))
        waits = self._resolve(eng, dd)
        self.ops[eng].append((waits, fn, self.dsem[q][sidx], 16))
        return ("D", q, sidx, 16 * (use + 1))

    def emit_cc(self, fn, deps=()):
        waits = self._resolve("gpsimd", deps)
        self.cccnt += 1
        self.ops["gpsimd"].append((waits, fn, self.ccsem, 1))
        return ("C", self.cccnt)

    def barrier_tokens(self):
        toks = [(e, self.cnt[e]) for e in self.ENGINES]
        toks += [("C", self.cccnt)]
        for q in self.QUEUES:
            n = self.dcnt[q]
            for sidx in range(self.POOL):
                uses = (n - sidx + self.POOL - 1) // self.POOL
                if uses > 0:
                    toks.append(("D", q, sidx, 16 * uses))
        return toks

    def wait_only(self, eng, deps):
        waits = self._resolve(eng, deps)
        if waits:
            self.ops[eng].append((waits, None, None, 0))

    def replay(self, block):
        def mk(eng_name):
            def body(e):
                for waits, fn, sem, inc in self.ops[eng_name]:
                    for sm, v in waits:
                        e.wait_ge(sm, v)
                    if fn is not None:
                        inst = fn(e)
                        inst.then_inc(sem, inc)
            return body
        block.tensor(mk("tensor"))
        block.vector(mk("vector"))
        block.scalar(mk("scalar"))
        block.sync(mk("sync"))
        block.gpsimd(mk("gpsimd"))


def rd(slot, tok):
    slot.readers.append(tok)
    return tok


def wdeps(slot, *extra):
    dd = list(slot.readers)
    if slot.writer is not None:
        dd.append(slot.writer)
    dd.extend(extra)
    return dd


def wrote(slot, tok):
    slot.writer = tok
    slot.readers = []
    return tok



def build_nc(debug=False, stop_after=None):
    nc = bass.Bass("TRN2", target_bir_lowering=False)

    def inp(name, shp):
        return nc.dram_tensor(name, shp, F32, kind="ExternalInput")

    h_x = inp("x", [T, D])
    h_xh = inp("xh", [3, D])
    h_cm = inp("cmcol", [128, 1])
    h_um = inp("umcol", [128, 1])
    h_wxbc = inp("wxbc_p", [128, 17 * 8 * 128])
    h_wdt = inp("wdt_p", [128, 8 * 32])
    h_wz = inp("wz_t", [D, DIN])
    h_wop = inp("wop_t", [DIN, D])
    h_fc1 = inp("fc1_p", [128, 64 * 8 * 128])
    h_fc2 = inp("fc2_t", [4096, D])
    h_wk = inp("wk_p", [128, 8 * 8 * 128])
    h_wkt = inp("wkt_t", [D, D])
    h_wr = inp("wr_t", [D, D])
    h_wv = inp("wv_t", [D, D])
    h_cw = inp("conv_w_p", [128, 68])
    h_cb = inp("conv_b_p", [128, 17])
    h_dtb = inp("dt_bias_c", [32, 1])
    h_alog = inp("a_log_c", [32, 1])
    h_drow = inp("d_row", [1, 32])
    h_ssmw = inp("ssm_w_row", [1, DIN])
    h_rmsw = inp("rms_w_row", [1, D])
    h_murp = inp("mu_r_p", [128, 8])
    h_mukp = inp("mu_k_p", [128, 8])
    h_murrow = inp("mu_r_row", [1, D])
    h_mukrow = inp("mu_k_row", [1, D])

    h_out = nc.dram_tensor("out", [T, D], F32, kind="ExternalOutput")

    ones_row_np = np.ones((1, 128), np.float32)
    eye_np = np.eye(128, dtype=np.float32)
    mask_np = np.where(np.triu(np.ones((Q, Q), np.float32)) > 0, 0.0, NEG).astype(np.float32)
    c_ones = nc.inline_tensor(ones_row_np, "c_ones")
    c_eye = nc.inline_tensor(eye_np, "c_eye")
    c_mask = nc.inline_tensor(mask_np, "c_mask")
    c_mask01 = nc.inline_tensor(np.triu(np.ones((Q, Q), np.float32)), "c_mask01")

    dram = nc.dram_tensor
    d_zsp = dram("zspill", [T, DIN], BF16, kind="Internal")
    d_ysp = dram("yspill", [T, DIN], BF16, kind="Internal")
    d_hsp = dram("hspill", [NCH, 64, DIN], BF16, kind="Internal")
    d_cs = dram("cs_dram", [NGRP, 32, 512], F32, kind="Internal")
    d_row1 = dram("rowscratch1", [1, D], BF16, kind="Internal")
    d_xft = dram("xftspill", [4, 128, 8 * 512], BF16, kind="Internal")
    d_cc1i = dram("cc1_in", [64, DIN], F32, kind="Internal")
    d_cc1o = dram("cc1_out", [64, DIN], F32, kind="Internal")
    d_cc2i = dram("cc2_in", [1, D], F32, kind="Internal")
    d_cc2o = dram("cc2_out", [1, D], F32, kind="Internal")
    d_fc1s = dram("fc1_s", [128, 64 * 8 * 128], BF16, kind="Internal")
    d_fc2s = dram("fc2_s", [4096, D], BF16, kind="Internal")
    d_wks = dram("wk_s", [128, 8 * 8 * 128], BF16, kind="Internal")
    d_wrs = dram("wr_s", [D, D], BF16, kind="Internal")
    d_wvs = dram("wv_s", [D, D], BF16, kind="Internal")
    d_wops = dram("wop_s", [DIN, D], BF16, kind="Internal")

    dbg_outs = {}
    if debug:
        for nm, shp in [("dbg_u0", [128, D]), ("dbg_xbca0", [128, 512]),
                        ("dbg_dt0", [32, 512]), ("dbg_cs0", [32, 512]),
                        ("dbg_y0", [128, DIN]), ("dbg_h0", [64, DIN]),
                        ("dbg_z0", [128, DIN]), ("dbg_yfull0", [128, DIN]),
                        ("dbg_x2_0", [128, D]), ("dbg_x3_0", [128, D]),
                        ("dbg_gam0", [64, 32]), ("dbg_hbu", [64, DIN])]:
            dbg_outs[nm] = nc.dram_tensor(nm, shp, F32, kind="ExternalOutput")

    SB = nc.alloc_sbuf_tensor

    t_lam = SB("lamstore", [128, NCH * 32], F32)
    t_cstore = SB("cstore", [64, NCH * 128], BF16)
    t_csend = SB("csendstore", [64, NCH * 32], F32)
    t_convw = SB("convw", [128, 68], F32)
    t_convb = SB("convb", [128, 17], F32)
    t_dtb = SB("dtbias", [32, 1], F32)
    t_alog = SB("alog", [32, 1], F32)
    t_expa = SB("expa", [32, 1], F32)
    t_cm = SB("cm", [128, 1], F32)
    t_um = SB("um", [128, 1], F32)
    t_dtile = SB("dtile", [128, 32], F32)
    t_ssmw = SB("ssmwrep", [128, DIN], BF16)
    t_omrp = SB("omrp", [128, 8], F32)
    t_omkp = SB("omkp", [128, 8], F32)
    t_zero32 = SB("zero32", [32, 128], F32)
    t_eyebf = SB("eyebf", [128, 128], BF16)
    t_rmsw = SB("rmswrep", [128, D], F32)
    t_murp = SB("murp", [128, 8], F32)
    t_mukp = SB("mukp", [128, 8], F32)
    t_ones = SB("ones", [1, 128], F32)
    t_eye = SB("eye", [128, 128], F32)
    t_mask = SB("maskneg", [128, 128], F32)
    t_wdtw = SB("wdtw", [128, 8 * 32], BF16)
    t_stat = SB("stat", [128, 64], F32)
    t_sqd = SB("sqdump", [128, DIN], BF16)
    t_hbu = SB("hbu", [64, DIN], F32)
    t_xr = SB("xring", [128, 2 * D], F32)
    t_xtf = SB("xtfring", [128, 2 * D], BF16)
    t_big = SB("bigf32", [128, DIN], F32)
    t_gamt = SB("gamtile", [64, 8 * 32], F32)
    t_tfdt8 = SB("tfdt8", [128, 8 * 96], F32)
    t_g = SB("gtile", [128, 2 * 128], BF16)
    t_mask01 = SB("mask01", [128, 128], F32)
    t_const = SB("constap", [128, 2], F32)
    nc.const_aps.register(F32, EPS, t_const.ap()[:, 0:1])
    nc.const_aps.register(F32, 1.0, t_const.ap()[:, 1:2])
    t_x3n0 = SB("x3n0", [1, D], BF16)
    t_xa0 = SB("xa0", [1, D], F32)
    t_halo = SB("halorow", [1, D], BF16)

    AR_BYTES = 146112
    t_arena = SB("ARENA", [128, AR_BYTES], mybir.dt.uint8)
    base = nc.lookup_mloc(t_arena).addr

    def at(name, shape, dtype, off):
        return nc.alloc_sbuf_tensor_at(name, shape, dtype, offset=base + off)

    a_wz = at("wzres", [128, 8 * DIN], BF16, 0)
    a_wxbc = at("wxbc", [128, 17 * 8 * 128], BF16, 0)
    a_zbuf = at("zbuf", [128, DIN], BF16, 34816)
    a_zbuf2 = at("zbuf2", [128, DIN], BF16, 38912)
    a_pre = at("xbcpre", [128, 17 * 515], BF16, 38912)
    a_xbca = at("xbca", [128, 17 * 512], BF16, 56448)
    a_xft = at("xftgrp", [128, 8 * 512], BF16, 73856)
    a_cwk = at("convwork", [128, 2 * 512], F32, 82048)
    a_xs = at("xstf", [128, DIN], BF16, 86144)
    a_xs2 = at("xstf2", [128, DIN], BF16, 34816)
    a_bctf = at("bctf", [128, 2 * 128], BF16, 90240)
    a_cskh = at("cskh", [32, 2 * 512], BF16, 90752)
    a_cskl = at("cskl", [32, 2 * 512], BF16, 92800)
    a_arg = at("argbuf", [128, 2 * 512], F32, 94848)
    a_gamma = at("gammabuf", [128, 2 * 512], BF16, 98944)
    a_mt = at("mtbuf", [128, 2 * 512], BF16, 100992)
    a_wdxs = at("wdxs", [128, DIN], BF16, 103040)
    a_yp = at("ypartring", [128, 2 * DIN], BF16, 107136)
    a_stack = at("stack", [96, 512], F32, 115328)
    a_csft = at("csft", [32, 512], F32, 117376)
    a_dtft = at("dtft", [32, 512], F32, 119424)
    a_lndt = at("lndtft", [32, 512], F32, 121472)
    a_w32 = at("work32", [32, 512], F32, 123520)
    a_tfdt = at("tfdt", [128, 4 * 96], F32, 125568)
    a_hprev = at("hprev", [64, DIN], F32, 127104)
    a_hnew = at("hnew", [64, DIN], F32, 135296)
    a_scr = at("scrbuf", [128, 2 * 512], BF16, 143488)
    a_xhft = at("xhft", [128, 8 * 3], BF16, 145600)
    a_hsave = at("hsave", [128, 17 * 3], BF16, 145664)

    b_gated = at("gated", [128, 32 * 512], BF16, 0)
    b_yft = at("yftpair", [128, 16 * 256], BF16, 32768)
    b_x3ft = at("x3ftgrp", [128, 8 * 528], BF16, 40960)
    b_fstr = at("fstream", [128, 4 * 1024], BF16, 49408)
    b_yrd = at("yread", [128, 2 * DIN], BF16, 57600)
    b_zrd = at("zread", [128, 2 * DIN], BF16, 65792)
    b_yg = at("ygbuf", [128, DIN], BF16, 73984)
    b_yhat = at("yhat", [128, DIN], BF16, 78080)
    b_u = at("uring", [128, 2 * D], F32, 82176)
    b_x2 = at("x2ring", [128, 2 * D], F32, 90368)
    b_w2 = at("w2ring", [128, 4 * D], F32, 98560)
    b_hin = at("hin", [64, DIN], BF16, 114944)
    b_hloc = at("hlocrd", [64, DIN], BF16, 119040)
    b_wstr = at("wstream", [128, 4 * D], BF16, 123136)
    b_pigs = at("pigsum", [64, 64], F32, 131328)
    b_pigt = at("pigtile", [64, 32], F32, 131616)
    b_rst = at("rstore", [128, 2 * D], BF16, 131776)
    b_rk = at("rkscr", [128, 512], BF16, 135872)
    b_sg = at("sgscr", [128, 512], BF16, 136896)
    b_ka = at("kaft", [128, 8 * 512], BF16, 137920)

    p_murrow = at("murrow", [1, D], F32, 0)
    p_mukrow = at("mukrow", [1, D], F32, 4096)
    p_omr = at("onemur", [1, D], F32, 8192)
    p_omk = at("onemuk", [1, D], F32, 12288)
    p_rowA = at("rowA", [1, D], F32, 16384)
    p_rowB = at("rowB", [1, D], F32, 20480)
    p_rowC = at("rowC", [1, D], BF16, 24576)
    p_rowD = at("rowD", [1, D], BF16, 26624)
    p_rowE = at("rowE", [1, D], BF16, 28672)
    p_wcol = at("wcol", [128, 24], BF16, 30720)

    PSA = nc.alloc_psum_tensor
    PS0 = PSA("ps0", [128, 2048], F32)
    PS1 = PSA("ps1", [128, 2048], F32)

    s = Sched(nc)
    return _emit(nc, s, locals(), debug, stop_after)



class Buf:

    def __init__(self):
        self.w = []
        self.r = []

    def reset(self):
        old = self.r + self.w
        self.w = []
        self.r = []
        return old


def _emit(nc, s, L, debug, stop_after=None):
    from types import SimpleNamespace
    E = SimpleNamespace(**{k: v for k, v in L.items() if not k.startswith("__")})
    TE, V, SC, SY, GP = "tensor", "vector", "scalar", "sync", "gpsimd"

    def vtt(out, a, b, op, deps):
        return s.emit(V, lambda e, o=out, x=a, y=b, p=op: e.tensor_tensor(o, x, y, p), deps)

    def vts(out, a, s1, s2, op0, op1, deps):
        if s2 is None:
            return s.emit(V, lambda e, o=out, x=a, u=s1, p0=op0:
                          e.tensor_scalar(o, x, u, None, p0), deps)
        return s.emit(V, lambda e, o=out, x=a, u=s1, w=s2, p0=op0, p1=op1:
                      e.tensor_scalar(o, x, u, w, p0, p1), deps)

    def vstt(out, in0, sc, in1, op0, op1, deps):
        return s.emit(V, lambda e, o=out, x=in0, u=sc, y=in1, p0=op0, p1=op1:
                      e.scalar_tensor_tensor(o, x, u, y, p0, p1), deps)

    def pstt(out, in0, sc, in1, op0, op1, deps):
        return s.emit(GP, lambda e, o=out, x=in0, u=sc, y=in1, p0=op0, p1=op1:
                      e.scalar_tensor_tensor(o, x, u, y, p0, p1), deps)

    def pts(out, a, s1, s2, op0, op1, deps):
        if s2 is None:
            return s.emit(GP, lambda e, o=out, x=a, u=s1, p0=op0:
                          e.tensor_scalar(o, x, u, None, p0), deps)
        return s.emit(GP, lambda e, o=out, x=a, u=s1, w=s2, p0=op0, p1=op1:
                      e.tensor_scalar(o, x, u, w, p0, p1), deps)

    def vcopy(out, a, deps):
        return s.emit(V, lambda e, o=out, x=a: e.tensor_copy(o, x), deps)

    def vmemset(out, val, deps):
        return s.emit(V, lambda e, o=out, v=val: e.memset(o, v), deps)

    def vscan(out, d0, d1, init, op0, op1, deps):
        return s.emit(V, lambda e, o=out, a=d0, b=d1, i=init, p0=op0, p1=op1:
                      e.tensor_tensor_scan(o, a, b, i, p0, p1), deps)

    def act(out, in_, func, deps, bias=0.0, scale=1.0, accum=None):
        dd = list(deps)
        if isinstance(bias, float) and bias != 0.0:
            dd += [tok_c1, tok_c2]
        return s.emit(SC, lambda e, o=out, x=in_, f=func, b=bias, sc=scale, ac=accum:
                      e.activation(o, x, f, bias=b, scale=sc, accum_out=ac), dd)

    def scopy(out, in_, deps):
        return s.emit(SC, lambda e, o=out, x=in_: e.copy(o, x), deps)

    def mm(out, l, r, st, sp, deps, tp=None):
        return s.emit(TE, lambda e, o=out, a=l, b=r, s1=st, s2=sp, t=tp:
                      e.matmul(o, a, b, start=s1, stop=s2, tile_position=t), deps)

    def ptr(out, in_, ident, deps):
        return s.emit(TE, lambda e, o=out, x=in_, i=ident: e.transpose(o, x, i), deps)

    def dmaS(out, in_, deps):
        return s.dma(SY, lambda e, o=out, x=in_: e.dma_start(out=o, in_=x), deps)

    def dmaT(out, in_, deps):
        return s.dma(SY, lambda e, o=out, x=in_: e.dma_start(out=o, in_=x, transpose=True), deps)

    def dmaT2(out, in_, deps):
        return s.dma(SC, lambda e, o=out, x=in_: e.dma_start(out=o, in_=x, transpose=True), deps)

    def dmaS2(out, in_, deps):
        return s.dma(SC, lambda e, o=out, x=in_: e.dma_start(out=o, in_=x), deps)

    def dmaG(out, in_, deps):
        return s.dma(GP, lambda e, o=out, x=in_: e.dma_start(out=o, in_=x), deps)

    def join_all():
        toks = s.barrier_tokens()
        for e in s.ENGINES:
            s.wait_only(e, toks)

    def bcast64(ap2d, nblk):
        return ap2d[:, :, None].broadcast_to([ap2d.shape[0], nblk, 64])

    bx = [Buf(), Buf()]
    bxtf = [Buf(), Buf()]
    bstat = [Buf(), Buf()]
    bxft = Buf()
    bpre = Buf()
    bxbca = Buf()
    bcwk = [Buf(), Buf()]
    bxs = [Buf(), Buf()]
    bbctf = [Buf(), Buf()]
    bcsk = [Buf(), Buf()]
    barg = [Buf(), Buf()]
    bgamma = [Buf(), Buf()]
    bmt = [Buf(), Buf()]
    bwdxs = Buf()
    byp = [Buf(), Buf()]
    bzbuf = [Buf(), Buf()]
    bstack = Buf()
    bcsft = Buf()
    bdtft = Buf()
    blndt = Buf()
    bw32 = Buf()
    btfdt = [Buf() for _ in range(8)]
    bh = [Buf(), Buf()]
    bbig = Buf()
    bgamt = [Buf() for _ in range(8)]
    bcsend = Buf()
    blam = Buf()
    bcstore = Buf()
    bscr = [Buf(), Buf()]
    brscr = [Buf(), Buf()]
    bg = [Buf(), Buf()]
    bhbu = Buf()
    bhsave = Buf()
    ps0a = Buf()
    ps0b = Buf()
    ps0c = Buf()
    ps0d = Buf()
    ps1 = Buf()
    byrd = [Buf(), Buf()]
    bzrd = [Buf(), Buf()]
    byg = Buf()
    byhat = Buf()
    bu = [Buf(), Buf()]
    bx2 = [Buf(), Buf()]
    bw2 = [Buf(), Buf(), Buf(), Buf()]
    bhin = Buf()
    bhloc = [Buf(), Buf()]
    bwstr = [Buf(), Buf(), Buf(), Buf()]
    bfstr = [Buf(), Buf(), Buf(), Buf()]
    bpigs = Buf()
    bpigt = Buf()
    brst = [Buf(), Buf()]
    brk = Buf()
    bsg = Buf()
    bgated = Buf()
    byft = Buf()
    bx3ft = Buf()
    bka = Buf()
    bdram = {}
    for nm in ("zsp", "ysp", "hsp", "cs", "row1", "cc1i", "cc1o", "cc2i", "cc2o",
               "fc1s", "fc2s", "wks", "wrs", "wvs", "wops", "xft"):
        bdram[nm] = Buf()

    PS0, PS1 = E.PS0, E.PS1
    E.a_tfdt = E.t_tfdt8


    ld = []
    ld.append(dmaG(E.t_convw[:], E.h_cw.ap(), []))
    ld.append(dmaG(E.t_convb[:], E.h_cb.ap(), []))
    ld.append(dmaG(E.t_dtb[:], E.h_dtb.ap(), []))
    ld.append(dmaG(E.t_alog[:], E.h_alog.ap(), []))
    ld.append(dmaG(E.t_cm[:], E.h_cm.ap(), []))
    ld.append(dmaG(E.t_um[:], E.h_um.ap(), []))
    ld.append(dmaG(E.t_murp[:], E.h_murp.ap(), []))
    ld.append(dmaG(E.t_mukp[:], E.h_mukp.ap(), []))
    ld.append(dmaG(E.t_wdtw[:], E.h_wdt.ap(), []))
    def repsrc(hap, n):
        return hap[0:1, :, None].broadcast_to([1, n, 128]).transpose([0, 2, 1]).squeeze(0)
    ld.append(dmaG(E.t_dtile[:], repsrc(E.h_drow.ap(), 32), []))
    ld.append(dmaG(E.t_ssmw[:], repsrc(E.h_ssmw.ap(), DIN), []))
    ld.append(dmaG(E.t_rmsw[:], repsrc(E.h_rmsw.ap(), D), []))
    ld.append(dmaS(E.t_ones[:], E.c_ones.ap(), []))
    ld.append(dmaS(E.t_eye[:], E.c_eye.ap(), []))
    ld.append(dmaS(E.t_mask[:], E.c_mask.ap(), []))
    ld.append(dmaS(E.t_mask01[:], E.c_mask01.ap(), []))
    ld_mask01 = ld[-1]
    tok_c1 = vmemset(E.t_const[:, 0:1], EPS, [])
    tok_c2 = vmemset(E.t_const[:, 1:2], 1.0, [])
    tok_eyebf = vcopy(E.t_eyebf[:], E.t_eye[:], [ld[13]])
    tok_expa = act(E.t_expa[:], E.t_alog[:], AF.Exp, [ld[3]])
    tok_z32 = vmemset(E.t_zero32[:], 0.0, [])
    tok_omr = vts(E.t_omrp[:], E.t_murp[:], -1.0, 1.0, ALU.mult, ALU.add, [ld[6]])
    tok_omk = vts(E.t_omkp[:], E.t_mukp[:], -1.0, 1.0, ALU.mult, ALU.add, [ld[7]])
    tok_h0 = vmemset(E.a_hprev[:], 0.0, [])
    bh[0].w.append(tok_h0)

    wz_toks = []
    for d in range(8):
        wz_toks.append(dmaG(E.a_wz[:, d * DIN:(d + 1) * DIN],
                            E.h_wz.ap()[d * 128:(d + 1) * 128, :], []))


    last_dump = [None]

    def dump_act(out, in_, func, deps, accum):
        t = act(out, in_, func, list(deps) + ([last_dump[0]] if last_dump[0] else []),
                accum=accum)
        last_dump[0] = t
        return t

    def emit_ln1(cc, want_bf16_tf, want_f32_u, u_dst=None, u_deps=()):
        xs_ = bx[cc % 2]
        old = xs_.reset()
        x_ap = E.t_xr[:, (cc % 2) * D:(cc % 2 + 1) * D]
        t_x = dmaS(x_ap, E.h_x.ap()[cc * 128:(cc + 1) * 128, :], old)
        xs_.w.append(t_x)
        st = bstat[cc % 2]
        stold = st.reset()
        c0 = (cc % 2) * 16
        sa = E.t_stat[:, c0:c0 + 1]
        sb_ = E.t_stat[:, c0 + 1:c0 + 2]
        sm = E.t_stat[:, c0 + 2:c0 + 3]
        sq = E.t_stat[:, c0 + 3:c0 + 4]
        sv = E.t_stat[:, c0 + 4:c0 + 5]
        sl = E.t_stat[:, c0 + 5:c0 + 6]
        sr = E.t_stat[:, c0 + 6:c0 + 7]
        t1 = dump_act(E.t_sqd[:, 0:D], x_ap, AF.Copy, [t_x] + stold, sa)
        t2 = dump_act(E.t_sqd[:, 0:D], x_ap, AF.Square, [t_x], sb_)
        st.w += [t1, t2]
        t3 = vts(sm, sa, 1.0 / D, None, ALU.mult, None, [t1])
        t4 = vtt(sq, sm, sm, ALU.mult, [t3])
        t5 = vts(sv, sb_, 1.0 / D, sq, ALU.mult, ALU.subtract, [t2, t4])
        t6 = act(sl, sv, AF.Ln, [t5], bias=EPS)
        t7 = act(sr, sl, AF.Exp, [t6], scale=-0.5)
        st.w += [t3, t4, t5, t6, t7]
        u_tok = tf_tok = None
        if want_bf16_tf:
            tfs = bxtf[cc % 2]
            tfold = tfs.reset()
            tf_ap = E.t_xtf[:, (cc % 2) * D:(cc % 2 + 1) * D]
            tf_tok = vts(tf_ap, x_ap, sm, sr, ALU.subtract, ALU.mult, [t7, t_x] + tfold)
            tfs.w.append(tf_tok)
            xs_.r.append(tf_tok)
            st.r.append(tf_tok)
        if want_f32_u:
            u_tok = vts(u_dst, x_ap, sm, sr, ALU.subtract, ALU.mult,
                        [t7, t_x] + list(u_deps))
            xs_.r.append(u_tok)
            st.r.append(u_tok)
        return t_x, x_ap, u_tok, tf_tok

    def emit_xft(cc, tf_tok):
        ci = cc % 4
        xft_first = bxft.reset() if ci == 0 else []
        t = dmaT(E.a_xft[:].rearrange("p (d q) -> p d q", q=512)
                 [:, :, ci * 128:ci * 128 + 128],
                 E.t_xtf[:, (cc % 2) * D:(cc % 2 + 1) * D]
                 .rearrange("p (d f) -> p d f", d=8),
                 [tf_tok] + xft_first)
        bxft.w.append(t)
        bxtf[cc % 2].r.append(t)
        return [t]

    zctx = {}

    def p1a_front(cc):
        t_x, x_ap, _, tf_tok = emit_ln1(cc, True, False)
        zctx[cc] = emit_xft(cc, tf_tok)
        if cc % GRP == GRP - 1:
            t_xsp = dmaS(E.d_xft.ap()[cc // GRP], E.a_xft[:], list(bxft.w))
            bxft.r.append(t_xsp)
            bdram["xft"].w.append(t_xsp)

    def p1a_back(cc):
        ci = cc % GRP
        xtoks = zctx.pop(cc)
        zPS = PS1 if cc % 2 == 0 else PS0
        zbuf_ = ps1 if cc % 2 == 0 else ps0a
        zdeps = xtoks + wz_toks + zbuf_.reset() + (
            [] if cc % 2 == 0 else ps0b.reset() + ps0c.reset() + ps0d.reset())
        last = None
        for d in range(8):
            lhs = E.a_xft[:, d * 512 + ci * 128: d * 512 + ci * 128 + 128]
            for sl in range(4):
                last = mm(zPS[:, sl * 512:(sl + 1) * 512], lhs,
                          E.a_wz[:, d * DIN + sl * 512: d * DIN + (sl + 1) * 512],
                          d == 0, d == 7, zdeps)
        zbuf_.w.append(last)
        if cc % 2 == 1:
            ps0b.w.append(last); ps0c.w.append(last); ps0d.w.append(last)
        zb = bzbuf[cc % 2]
        zold = zb.reset()
        z_ap = E.a_zbuf[:, 0:DIN] if cc % 2 == 0 else E.a_zbuf2[:, 0:DIN]
        t_silu = act(z_ap, zPS[:, :], AF.Silu, [last] + zold)
        zb.w.append(t_silu)
        zbuf_.r.append(t_silu)
        if cc % 2 == 1:
            ps0b.r.append(t_silu); ps0c.r.append(t_silu); ps0d.r.append(t_silu)
        t_sp = dmaS(E.d_zsp.ap()[cc * 128:(cc + 1) * 128, :], z_ap, [t_silu])
        zb.r.append(t_sp)
        bdram["zsp"].w.append(t_sp)

    p1a_front(0)
    for cc in range(NCH):
        if cc + 1 < NCH:
            p1a_front(cc + 1)
        p1a_back(cc)

    def finalize_early():
        join_all()
        n = s.dcnt[SY]
        fw = []
        for sidx in range(s.POOL):
            uses = (n - sidx + s.POOL - 1) // s.POOL
            if uses > 0:
                fw.append((s.dsem[SY][sidx], 16 * uses))
        s.ops[SY].append((fw, None, None, 0))
        with nc.Block() as block:
            s.replay(block)
        return nc, E.dbg_outs

    if stop_after == "prologue":
        return finalize_early()

    join_all()

    if stop_after == "1a":
        return finalize_early()

    tok_wxbc = dmaG(E.a_wxbc[:], E.h_wxbc.ap(), [])
    if stop_after == "1bwx0":
        return finalize_early()

    t_xh = dmaG(E.t_big[0:3, 0:D], E.h_xh.ap(), bbig.reset())
    bbig.w.append(t_xh)
    sa = E.t_stat[0:3, 20:21]; sb_ = E.t_stat[0:3, 21:22]
    sm = E.t_stat[0:3, 22:23]; sq = E.t_stat[0:3, 23:24]
    sv = E.t_stat[0:3, 24:25]; sl_ = E.t_stat[0:3, 25:26]; sr = E.t_stat[0:3, 26:27]
    th1 = dump_act(E.t_sqd[0:3, 0:D], E.t_big[0:3, 0:D], AF.Copy, [t_xh], sa)
    th2 = dump_act(E.t_sqd[0:3, 0:D], E.t_big[0:3, 0:D], AF.Square, [t_xh], sb_)
    th3 = vts(sm, sa, 1.0 / D, None, ALU.mult, None, [th1])
    th4 = vtt(sq, sm, sm, ALU.mult, [th3])
    th5 = vts(sv, sb_, 1.0 / D, sq, ALU.mult, ALU.subtract, [th2, th4])
    th6 = act(sl_, sv, AF.Ln, [th5], bias=EPS)
    th7 = act(sr, sl_, AF.Exp, [th6], scale=-0.5)
    if stop_after == "1bh1":
        return finalize_early()
    uh_ap = E.t_big[0:3, D:2 * D]
    th8 = vts(uh_ap, E.t_big[0:3, 0:D], sm, sr, ALU.subtract, ALU.mult, [th7])
    bbig.w.append(th8)
    if stop_after == "1bh1a":
        return finalize_early()
    xhft_toks = []
    for d in range(8):
        tt_ = ptr(PS0[0:128, 1536 + d * 4:1536 + d * 4 + 3],
                  uh_ap[:, d * 128:(d + 1) * 128], E.t_eye[0:3, 0:3],
                  [th8] + ps0d.reset())
        ps0d.w.append(tt_)
        if stop_after != "1bh1b":
            if stop_after == "1bh2alt":
                dst = E.t_sqd[:, d * 3:(d + 1) * 3]
            elif stop_after == "1bh2alt2":
                dst = E.a_zbuf[:, d * 3:(d + 1) * 3]
            else:
                dst = E.a_xhft[:, d * 3:(d + 1) * 3]
            te_ = scopy(dst, PS0[0:128, 1536 + d * 4:1536 + d * 4 + 3], [tt_])
            ps0d.r.append(te_)
            xhft_toks.append(te_)
    if stop_after == "1bh1b":
        return finalize_early()
    if False:
        pass
    if stop_after in ("1bh2", "1bh2alt", "1bh2alt2"):
        return finalize_early()
    pre_old = bpre.reset()
    for o in range(17):
        lastm = None
        hdeps = [tok_wxbc] + xhft_toks + ps0d.reset() + pre_old
        for d in range(8):
            lastm = mm(PS0[0:128, 1960:1963],
                       E.a_wxbc[:, o * 1024 + d * 128: o * 1024 + d * 128 + 128],
                       E.a_xhft[:, d * 3:(d + 1) * 3],
                       d == 0, d == 7, hdeps)
        ps0d.w.append(lastm)
        th_e = scopy(E.a_pre[:, o * 515:o * 515 + 3], PS0[0:128, 1960:1963], [lastm])
        ps0d.r.append(th_e)
        bpre.w.append(th_e)
    if stop_after == "1bhalo":
        return finalize_early()

    if stop_after == "1bwx":
        return finalize_early()

    ctxs = {}

    def p1b_prep(g):
        ctx = {}
        ctxs[g] = ctx
        t_xrd = dmaS(E.a_xft[:], E.d_xft.ap()[g], bdram["xft"].w + bxft.reset())
        bxft.w.append(t_xrd)
        yield
        if g > 0:
            pre3 = E.a_pre[:, :].rearrange("p (o c) -> p o c", c=515)
            hs3 = E.a_hsave[:, :].rearrange("p (o c) -> p o c", c=3)
            t_halo_cp = vcopy(pre3[:, :, 0:3], hs3[:, :, :], bhsave.w + bpre.r + bpre.w)
            pre_old = bpre.reset()
            bpre.w.append(t_halo_cp)
            bhsave.r.append(t_halo_cp)
        else:
            pre_old = []
        ctx["pre_old"] = pre_old

        def io_chunk(o):
            reg = [ps0a, ps0b][o % 2]
            regap = PS0[:, (o % 2) * 512:(o % 2) * 512 + 512]
            lastm = None
            rdeps = bxft.w + [tok_wxbc] + reg.reset() + ctx["pre_old"]
            for d in range(8):
                lastm = mm(regap,
                           E.a_wxbc[:, o * 1024 + d * 128: o * 1024 + d * 128 + 128],
                           E.a_xft[:, d * 512:(d + 1) * 512],
                           d == 0, d == 7, rdeps)
            reg.w.append(lastm)
            t_e = scopy(E.a_pre[:, o * 515 + 3:o * 515 + 515], regap,
                        [lastm] + ctx["pre_old"] + list(bpre.w))
            reg.r.append(t_e)
            bpre.w.append(t_e)

        for o in range(9):
            io_chunk(o)
        yield
        for o in range(9, 17):
            io_chunk(o)
        pre3 = E.a_pre[:, :].rearrange("p (o c) -> p o c", c=515)
        hs3 = E.a_hsave[:, :].rearrange("p (o c) -> p o c", c=3)
        t_hs = vcopy(hs3[:, :, :], pre3[:, :, 512:515], bpre.w + bhsave.reset())
        bhsave.w.append(t_hs)
        bpre.r.append(t_hs)

        dt_old = ps0c.reset()
        lastm = None
        for d in range(8):
            lastm = mm(PS0[0:32, 1024:1536],
                       E.t_wdtw[:, d * 32:(d + 1) * 32],
                       E.a_xft[:, d * 512:(d + 1) * 512],
                       d == 0, d == 7, bxft.w + [ld[8]] + dt_old)
        ps0c.w.append(lastm)
        w32old = bw32.reset()
        dtftold = bdtft.reset()
        lndtold = blndt.reset()
        csftold = bcsft.reset()
        ta = vts(E.a_w32[:], PS0[0:32, 1024:1536], E.t_dtb[:], None, ALU.add, None,
                 [lastm, ld[2]] + w32old)
        ps0c.r.append(ta)
        tb = act(E.a_csft[:], E.a_w32[:], AF.Abs, [ta] + csftold)
        tc = act(E.a_lndt[:], E.a_csft[:], AF.Exp, [tb] + lndtold, scale=-1.0)
        td = act(E.a_csft[:], E.a_lndt[:], AF.Ln, [tc], bias=1.0)
        te = vts(E.a_dtft[:], E.a_w32[:], 0.0, None, ALU.max, None, [ta] + dtftold)
        tf_ = vtt(E.a_w32[:], E.a_dtft[:], E.a_csft[:], ALU.add, [td, te])
        bw32.w.append(tf_)
        tg = act(E.a_lndt[:], E.a_w32[:], AF.Ln, [tf_])
        blndt.w.append(tg)
        tlogda = vts(E.a_dtft[:], E.a_w32[:], E.t_expa[:], -1.0, ALU.mult, ALU.mult,
                     [tf_, tok_expa])
        bdtft.w.append(tlogda)
        cs_toks = []
        for ci in range(GRP):
            tcs = vscan(E.a_csft[:, ci * 128:(ci + 1) * 128],
                        E.a_dtft[:, ci * 128:(ci + 1) * 128],
                        E.t_zero32[:], 0.0, ALU.add, ALU.add, [tlogda, tok_z32, td])
            cs_toks.append(tcs)
            bcsft.w.append(tcs)
        stold = bstack.reset()
        tbias = vtt(E.a_stack[0:32, :], E.a_lndt[:], E.a_csft[:], ALU.subtract,
                    cs_toks + [tg] + stold)
        tlamf = act(E.a_stack[64:96, :], E.a_csft[:], AF.Exp, cs_toks + stold)
        bstack.w += [tbias, tlamf]
        bcsft.r += [tbias, tlamf]
        ksl = (g % 2) * 512
        t_kh = vcopy(E.a_cskh[:, ksl:ksl + 512], E.a_csft[:],
                     cs_toks + bcsk[g % 2].reset())
        t_kl = vtt(E.a_cskl[:, ksl:ksl + 512], E.a_csft[:],
                   E.a_cskh[:, ksl:ksl + 512], ALU.subtract, [t_kh])
        bcsk[g % 2].w += [t_kh, t_kl]
        bcsft.r += [t_kh, t_kl]
        ctx["csk"] = [t_kh, t_kl]
        for ci in range(GRP):
            cc = g * GRP + ci
            sl8 = (g % 2) * 4 + ci
            csend = E.a_csft[:, ci * 128 + 127:ci * 128 + 128]
            t1_ = vts(E.a_arg[0:32, 0:128], E.a_csft[:, ci * 128:(ci + 1) * 128],
                      csend, -1.0, ALU.subtract, ALU.mult,
                      cs_toks + barg[0].reset())
            t2_ = act(E.a_arg[0:32, 128:256], E.a_arg[0:32, 0:128], AF.Exp, [t1_])
            t3_ = vtt(E.a_stack[32:64, ci * 128:(ci + 1) * 128],
                      E.a_arg[0:32, 128:256], E.a_w32[:, ci * 128:(ci + 1) * 128],
                      ALU.mult, [t2_, tf_])
            barg[0].w += [t1_, t2_]
            bstack.w.append(t3_)
            bw32.r.append(t3_)
            tt1 = ptr(PS0[0:1, 1632:1664], csend, E.t_eye[0:32, 0:32],
                      ps0d.reset() + cs_toks)
            ps0d.w.append(tt1)
            tt2 = scopy(E.t_stat[0:1, 32:64], PS0[0:1, 1632:1664], [tt1])
            ps0d.r.append(tt2)
            tt3 = mm(PS0[0:64, 1664:1696], E.t_ones[0:1, 0:64], E.t_stat[0:1, 32:64],
                     True, True, [tt2, ld[12]])
            ps0d.w.append(tt3)
            tt4 = scopy(E.t_csend[:, cc * 32:(cc + 1) * 32], PS0[0:64, 1664:1696], [tt3])
            bcsend.w.append(tt4)
            ps0d.r.append(tt4)
            tt5 = ptr(PS0[0:128, 1536:1632], E.a_stack[:, ci * 128:(ci + 1) * 128],
                      E.t_eye[0:96, 0:96], bstack.w + ps0d.reset())
            ps0d.w.append(tt5)
            bstack.r.append(tt5)
            slot = btfdt[sl8]
            tfold = slot.reset()
            tt6 = scopy(E.a_tfdt[:, sl8 * 96:sl8 * 96 + 96],
                        PS0[0:128, 1536:1632], [tt5] + tfold)
            slot.w.append(tt6)
            ps0d.r.append(tt6)
            tt7 = vcopy(E.t_lam[:, cc * 32:(cc + 1) * 32],
                        E.a_tfdt[:, sl8 * 96 + 64:sl8 * 96 + 96], [tt6])
            blam.w.append(tt7)
            slot.r.append(tt7)
            tt8 = act(E.t_gamt[:, sl8 * 32:(sl8 + 1) * 32], PS0[0:64, 1664:1696],
                      AF.Exp, [tt3] + bgamt[sl8].reset())
            bgamt[sl8].w.append(tt8)
            ps0d.r.append(tt8)
        yield
        conv_toks = []
        for o in range(17):
            base = o * 515
            TS_, STT_ = (vts, vstt)
            half = E.a_cwk[:, (o % 2) * 512:(o % 2) * 512 + 512]
            wA = half[:, 0:512]
            d0 = bcwk[o % 2].reset()
            c1 = TS_(wA, E.a_pre[:, base + 0:base + 512], E.t_convw[:, o * 4:o * 4 + 1],
                     None, ALU.mult, None, bpre.w + [ld[0]] + d0)
            c2 = STT_(wA, E.a_pre[:, base + 1:base + 513], E.t_convw[:, o * 4 + 1:o * 4 + 2],
                      wA, ALU.mult, ALU.add, [c1])
            c3 = STT_(wA, E.a_pre[:, base + 2:base + 514], E.t_convw[:, o * 4 + 2:o * 4 + 3],
                      wA, ALU.mult, ALU.add, [c2])
            c4 = STT_(wA, E.a_pre[:, base + 3:base + 515], E.t_convw[:, o * 4 + 3:o * 4 + 4],
                      wA, ALU.mult, ALU.add, [c3])
            bcwk[o % 2].w.append(c4)
            bpre.r += [c1, c2, c3, c4]
            c5 = act(E.a_xbca[:].rearrange("p (c q) -> p c q", q=2176)
                     [:, :, o * 128:(o + 1) * 128], wA, AF.Silu,
                     [c4, ld[1]] + (bxbca.reset() if o == 0 else []),
                     bias=E.t_convb[:, o:o + 1])
            bcwk[o % 2].r.append(c5)
            bxbca.w.append(c5)
            conv_toks.append(c5)
        ctx["conv"] = conv_toks
        yield

    def p1b_scan_T(g, ci):
        ctx = ctxs[g]
        conv_toks = ctx["conv"]
        cc = g * GRP + ci
        slot = cc % 2
        xs_t = E.a_xs if slot == 0 else E.a_xs2
        xs_old = bxs[slot].reset()
        cdep = conv_toks[15] if ci == 0 else conv_toks[16]
        t = dmaT(xs_t[:].rearrange("p (c f) -> p c f", c=16),
                 E.a_xbca[:, ci * 2176:ci * 2176 + 2048],
                 [cdep] + xs_old)
        bxs[slot].w.append(t)
        bxbca.r.append(t)
        xs_toks = [t]
        bslot = bbctf[ci % 2]
        b_old = bslot.reset()
        t_bc = dmaT(E.a_bctf[:, (ci % 2) * 128:(ci % 2) * 128 + 128],
                    E.a_xbca[:, ci * 2176 + 2048: ci * 2176 + 2176],
                    [conv_toks[16]] + b_old)
        bslot.w.append(t_bc)
        bxbca.r.append(t_bc)
        cm_old = ps0d.reset()
        t_cmm = mm(PS0[0:64, 1920:2048], E.t_eyebf[:, 64:128],
                   E.a_xbca[:, ci * 2176 + 2048: ci * 2176 + 2176],
                   True, True, [conv_toks[16], tok_eyebf] + cm_old)
        ps0d.w.append(t_cmm)
        bxbca.r.append(t_cmm)
        t_cst = scopy(E.t_cstore[:, cc * 128:(cc + 1) * 128], PS0[0:64, 1920:2048],
                      [t_cmm] + (bcstore.reset() if cc == 0 else []))
        ps0d.r.append(t_cst)
        bcstore.w.append(t_cst)
        g_old = ps0d.reset()
        t_gmm = mm(PS0[:, 1792:1920],
                   E.a_xbca[0:64, ci * 2176 + 2048: ci * 2176 + 2176],
                   E.t_cstore[:, cc * 128:(cc + 1) * 128],
                   True, True, [conv_toks[16], t_cst] + g_old)
        ps0d.w.append(t_gmm)
        t_gev = vtt(E.t_g[:, slot * 128:(slot + 1) * 128], PS0[:, 1792:1920],
                    E.t_mask01[:], ALU.mult, [t_gmm, ld_mask01] + bg[slot].reset())
        bg[slot].w.append(t_gev)
        ps0d.r.append(t_gev)
        ctx[("sT", ci)] = (xs_toks, t_bc, t_gev)

    def p1b_scan_C(g, ci):
        ctx = ctxs[g]
        cc = g * GRP + ci
        slot = cc % 2
        xs_t = E.a_xs if slot == 0 else E.a_xs2
        sl8 = (g % 2) * 4 + ci
        xs_toks, t_bc, t_gev = ctx[("sT", ci)]
        bslot = bbctf[ci % 2]
        gslot = g % 2
        wdap = bcast64(E.a_tfdt[:, sl8 * 96 + 32:sl8 * 96 + 64], 32)
        gamap = bcast64(E.t_gamt[:, sl8 * 32:(sl8 + 1) * 32], 32)
        ps1_old = ps1.reset()
        y_mms = []

        def j_stage_a(j):
            pb = ps0c if j % 2 == 0 else ps0d
            bank_ap = PS0[:, 1024:1536] if j % 2 == 0 else PS0[:, 1536:2048]
            pb_old = pb.reset()
            cs_mms = []
            for hh in range(4):
                h = 4 * j + hh
                selap = E.t_eyebf[0:32, h:h + 1, None].broadcast_to([32, 1, 128]).squeeze(1)
                ccol = gslot * 512 + ci * 128
                t_h = mm(bank_ap[:, hh * 128:(hh + 1) * 128], selap,
                         E.a_cskh[0:32, ccol:ccol + 128],
                         True, False,
                         ctx["csk"] + [tok_eyebf] + (pb_old if hh == 0 else []))
                t_l = mm(bank_ap[:, hh * 128:(hh + 1) * 128], selap,
                         E.a_cskl[0:32, ccol:ccol + 128],
                         False, True, [t_h])
                pb.w += [t_h, t_l]
                bcsk[gslot].r += [t_h, t_l]
                cs_mms.append(t_l)
            ar_ = barg[j % 2]
            ar_old = ar_.reset()
            arg_ap = E.a_arg[:, (j % 2) * 512:(j % 2) * 512 + 512]
            stt_toks = []
            for hh in range(4):
                h = 4 * j + hh
                t_ = vts(arg_ap[:, hh * 128:(hh + 1) * 128],
                         bank_ap[:, hh * 128:(hh + 1) * 128],
                         E.a_tfdt[:, sl8 * 96 + h:sl8 * 96 + h + 1], 20.0,
                         ALU.add, ALU.min,
                         [cs_mms[-1]] + btfdt[sl8].w + ar_old)
                stt_toks.append(t_)
                pb.r.append(t_)
            ar_.w += stt_toks
            return ar_, arg_ap, stt_toks

        def j_stage_c(j, ar_, arg_ap, stt_toks):
            gm = bgamma[j % 2]
            gm_old = gm.reset()
            gam_ap = E.a_gamma[:, (j % 2) * 512:(j % 2) * 512 + 512]
            t_exp = act(gam_ap, arg_ap, AF.Exp, stt_toks + gm_old)
            gm.w.append(t_exp)
            ar_.r.append(t_exp)
            return gm, gam_ap, t_exp

        def j_stage_de(j, gm, gam_ap, t_exp):
            mt_ = bmt[j % 2]
            mt_old = mt_.reset()
            mt_ap = E.a_mt[:, (j % 2) * 512:(j % 2) * 512 + 512]
            mt_toks = []
            for hh in range(4):
                t_ = vtt(mt_ap[:, hh * 128:(hh + 1) * 128],
                         gam_ap[:, hh * 128:(hh + 1) * 128],
                         E.t_g[:, slot * 128:(slot + 1) * 128],
                         ALU.mult, [t_exp, t_gev] + mt_old)
                mt_toks.append(t_)
            mt_.w += mt_toks
            gm.r += mt_toks
            bg[slot].r += mt_toks
            for hh in range(4):
                h = 4 * j + hh
                t_ = mm(PS1[:, h * 64:(h + 1) * 64],
                        mt_ap[:, hh * 128:(hh + 1) * 128],
                        xs_t[:, h * 64:(h + 1) * 64],
                        True, True,
                        [mt_toks[hh]] + xs_toks + (ps1_old if j == 0 and hh == 0 else []))
                mt_.r.append(t_)
                bxs[slot].r.append(t_)
                y_mms.append(t_)

        wd_old = bwdxs.reset()
        t_d = vtt(E.a_wdxs[:], xs_t[:], bcast64(E.t_dtile[:], 32), ALU.mult,
                  xs_toks + [ld[9]] + wd_old)
        bwdxs.w.append(t_d)
        bxs[slot].r.append(t_d)
        hp = bh[cc % 2]
        hn = bh[(cc + 1) % 2]
        hn_old = hn.reset()
        big_old = bbig.reset()
        t_s1 = vtt(E.t_big[0:64, :], E.a_hprev[:] if cc % 2 == 0 else E.a_hnew[:],
                   gamap, ALU.mult, hp.w + bgamt[sl8].w + big_old)
        hp.r.append(t_s1)
        bgamt[sl8].r.append(t_s1)
        bbig.w.append(t_s1)

        pend = None
        for j in range(8):
            abc = j_stage_a(j)
            if pend is not None:
                j_stage_de(j - 1, *pend)
            pend = j_stage_c(j, *abc)
        j_stage_de(7, *pend)
        ps1.w += y_mms
        yp_ = byp[ci % 2]
        yp_old = yp_.reset()
        yp_ap = E.a_yp[:, (ci % 2) * DIN:(ci % 2 + 1) * DIN]
        t_yp = vtt(yp_ap, E.a_wdxs[:], PS1[:, :], ALU.add,
                   [t_d] + y_mms + yp_old)
        yp_.w.append(t_yp)
        bwdxs.r.append(t_yp)
        ps1.r.append(t_yp)
        t_ysp = dmaS(E.d_ysp.ap()[cc * 128:(cc + 1) * 128, :], yp_ap, [t_yp])
        yp_.r.append(t_ysp)
        bdram["ysp"].w.append(t_ysp)
        wd_old = bwdxs.reset()
        t_wdxs = vtt(E.a_wdxs[:], xs_t[:], wdap, ALU.mult,
                     xs_toks + btfdt[sl8].w + wd_old)
        bwdxs.w.append(t_wdxs)
        bxs[slot].r.append(t_wdxs)
        d_old = ps1.reset()
        d_mms = []
        for h in range(NH):
            t_ = mm(PS1[0:64, h * 64:(h + 1) * 64],
                    E.a_bctf[:, (ci % 2) * 128:(ci % 2) * 128 + 64],
                    E.a_wdxs[:, h * 64:(h + 1) * 64],
                    True, True, [t_bc, t_wdxs] + d_old)
            d_mms.append(t_)
        ps1.w += d_mms
        bwdxs.r += d_mms
        bslot.r += d_mms
        t_s2 = vtt(E.a_hnew[:] if cc % 2 == 0 else E.a_hprev[:],
                   E.t_big[0:64, :], PS1[0:64, :], ALU.add,
                   [t_s1] + d_mms + hn_old)
        hn.w.append(t_s2)
        bbig.r.append(t_s2)
        ps1.r.append(t_s2)
        t_hsp = dmaG(E.d_hsp.ap()[cc],
                     E.a_hnew[:] if cc % 2 == 0 else E.a_hprev[:],
                     [t_s2])
        hn.r.append(t_hsp)
        bdram["hsp"].w.append(t_hsp)
        if debug and cc == 0:
            dbg_y = dmaG(E.dbg_outs["dbg_y0"].ap(), yp_ap, [t_yp])
            yp_.r.append(dbg_y)

    scratch_jobs = []

    def add_jobs(src_h, dst_d, tag, nsplit):
        R = src_h.shape[0]
        step = R // nsplit
        for r0 in range(0, R, step):
            scratch_jobs.append((src_h, dst_d, tag, r0, step))

    add_jobs(E.h_fc1, E.d_fc1s, "fc1s", 4)
    add_jobs(E.h_fc2, E.d_fc2s, "fc2s", 4)
    add_jobs(E.h_wop, E.d_wops, "wops", 2)
    add_jobs(E.h_wk, E.d_wks, "wks", 1)
    add_jobs(E.h_wr, E.d_wrs, "wrs", 1)
    add_jobs(E.h_wv, E.d_wvs, "wvs", 1)

    def emit_scratch_jobs(n):
        for _ in range(n):
            if not scratch_jobs:
                return
            src_h, dst_d, tag, r0, step = scratch_jobs.pop(0)
            t = dmaG(dst_d.ap()[r0:r0 + step, :], src_h.ap()[r0:r0 + step, :], [])
            bdram[tag].w.append(t)

    gen = p1b_prep(0)
    for _ in gen:
        pass
    p1b_scan_T(0, 0)
    for g in range(NGRP):
        nxt = p1b_prep(g + 1) if g + 1 < NGRP else None
        for ci in range(GRP):
            if ci < GRP - 1:
                p1b_scan_T(g, ci + 1)
            if nxt is not None and ci < 3:
                next(nxt, None)
            emit_scratch_jobs(1)
            p1b_scan_C(g, ci)
        if nxt is not None:
            next(nxt, None)
            p1b_scan_T(g + 1, 0)
        if isinstance(stop_after, str) and stop_after == f"1b:g{g}":
            return finalize_early()

    if stop_after == "1b":
        return finalize_early()


    hfin_ap = E.a_hnew[:] if (NCH - 1) % 2 == 0 else E.a_hprev[:]
    hfin = bh[NCH % 2]
    t_m1 = vtt(E.t_big[0:64, :], hfin_ap, E.t_cm[0:64, 0:1, None].broadcast_to([64, 1, DIN]).squeeze(1),
               ALU.mult, hfin.w + [ld[4]] + bbig.reset())
    bbig.w.append(t_m1)
    hfin.r.append(t_m1)
    t_w1 = dmaG(E.d_cc1i.ap(), E.t_big[0:64, :], [t_m1])
    bbig.r.append(t_w1)
    t_cc1 = s.emit_cc(lambda e: e.collective_compute(
        "AllReduce", ALU.add, replica_groups=CC_GROUPS,
        ins=[E.d_cc1i.ap().opt()], outs=[E.d_cc1o.ap().opt()]), [t_w1])
    t_r1 = dmaG(E.t_big[0:64, :], E.d_cc1o.ap(), [t_cc1] + bbig.reset())
    bbig.w.append(t_r1)
    t_hbu_ = vtt(E.t_hbu[:], E.t_big[0:64, :],
                 E.t_um[0:64, 0:1, None].broadcast_to([64, 1, DIN]).squeeze(1),
                 ALU.mult, [t_r1, ld[5]] + bhbu.reset())
    bhbu.w.append(t_hbu_)
    bbig.r.append(t_hbu_)
    if debug:
        t_dbh = dmaG(E.dbg_outs["dbg_hbu"].ap(), E.t_hbu[:], [t_hbu_])
        bhbu.r.append(t_dbh)

    join_all()

    if stop_after == "cc1":
        return finalize_early()

    t_pig0 = vmemset(E.b_pigs[:, 0:32], 0.0, [])
    bpigs.w.append(t_pig0)
    cc2_tok = {}
    out_toks = []

    for g in range(NGRP):
        if isinstance(stop_after, str) and stop_after.startswith("p2:g") and g > int(stop_after[4:]):
            return finalize_early()
        def p2_top(ci):
            cc = g * GRP + ci
            u_ap = E.b_u[:, (ci % 2) * D:(ci % 2 + 1) * D]
            us = bu[ci % 2]
            u_old = us.reset()
            t_x, x_ap, u_tok, _ = emit_ln1(cc, False, True, u_dst=u_ap, u_deps=u_old)
            us.w.append(u_tok)
            yr = byrd[ci % 2]
            yr_old = yr.reset()
            yr_ap = E.b_yrd[:, (ci % 2) * DIN:(ci % 2 + 1) * DIN]
            t_yr = dmaS(yr_ap, E.d_ysp.ap()[cc * 128:(cc + 1) * 128, :],
                        [bdram["ysp"].w[cc]] + yr_old)
            yr.w.append(t_yr)
            zr = bzrd[ci % 2]
            zr_old = zr.reset()
            zr_ap = E.b_zrd[:, (ci % 2) * DIN:(ci % 2 + 1) * DIN]
            t_zr = dmaS(zr_ap, E.d_zsp.ap()[cc * 128:(cc + 1) * 128, :],
                        [bdram["zsp"].w[cc]] + zr_old)
            zr.w.append(t_zr)
            pg_old = bpigt.reset()
            t_pig = act(E.b_pigt[:], E.b_pigs[:, (cc % 2) * 32:(cc % 2) * 32 + 32],
                        AF.Exp, bpigs.w + pg_old)
            bpigt.w.append(t_pig)
            bpigs.r.append(t_pig)
            t_ps = vtt(E.b_pigs[:, ((cc + 1) % 2) * 32:((cc + 1) % 2) * 32 + 32],
                       E.b_pigs[:, (cc % 2) * 32:(cc % 2) * 32 + 32],
                       E.t_csend[:, cc * 32:(cc + 1) * 32], ALU.add,
                       bpigs.w + bcsend.w)
            bpigs.w.append(t_ps)
            hin_old = bhin.reset()
            if cc == 0:
                t_hin = vtt(E.b_hin[:], E.t_hbu[:], bcast64(E.b_pigt[:], 32),
                            ALU.mult, bhbu.w + [t_pig] + hin_old)
            else:
                hl = bhloc[0]
                hl_old = hl.reset()
                hl_ap = E.b_hloc[:, 0:DIN]
                t_hl = dmaS(hl_ap, E.d_hsp.ap()[cc - 1],
                            [bdram["hsp"].w[cc - 1]] + hl_old)
                hl.w.append(t_hl)
                big_old = bbig.reset()
                t_tmp = vtt(E.t_big[0:64, :], E.t_hbu[:], bcast64(E.b_pigt[:], 32),
                            ALU.mult, bhbu.w + [t_pig] + big_old)
                bbig.w.append(t_tmp)
                t_hin = vtt(E.b_hin[:], E.t_big[0:64, :], hl_ap, ALU.add,
                            [t_tmp, t_hl] + hin_old)
                bbig.r.append(t_hin)
                hl.r.append(t_hin)
            bpigt.r.append(t_hin)
            bhin.w.append(t_hin)
            ps1_old = ps1.reset()
            yi_mms = []
            for h in range(NH):
                t_ = mm(PS1[:, h * 64:(h + 1) * 64],
                        E.t_cstore[:, cc * 128:(cc + 1) * 128],
                        E.b_hin[:, h * 64:(h + 1) * 64],
                        True, True, [t_hin] + bcstore.w + (ps1_old if h == 0 else []))
                yi_mms.append(t_)
                bhin.r.append(t_)
                bcstore.r.append(t_)
            ps1.w += yi_mms
            big_old = bbig.reset()
            t_l = vtt(E.t_big[:, :], PS1[:, :],
                      bcast64(E.t_lam[:, cc * 32:(cc + 1) * 32], 32), ALU.mult,
                      yi_mms + blam.w + big_old)
            bbig.w.append(t_l)
            ps1.r.append(t_l)
            blam.r.append(t_l)
            yg_old = byg.reset()
            t_yf = vtt(E.b_yg[:], E.t_big[:, :], yr_ap, ALU.add, [t_l, t_yr] + yg_old)
            byg.w.append(t_yf)
            bbig.r.append(t_yf)
            yr.r.append(t_yf)
            yh_old = byhat.reset()
            t_ygt = vtt(E.b_yhat[:], E.b_yg[:], zr_ap, ALU.mult, [t_yf, t_zr] + yh_old)
            byhat.w.append(t_ygt)
            byg.r.append(t_ygt)
            zr.r.append(t_ygt)
            c0 = (ci % 2) * 16
            sb2 = E.t_stat[:, c0 + 8:c0 + 9]
            sv2 = E.t_stat[:, c0 + 9:c0 + 10]
            sl2 = E.t_stat[:, c0 + 10:c0 + 11]
            sr2 = E.t_stat[:, c0 + 11:c0 + 12]
            t_q = dump_act(E.t_sqd[:, :], E.b_yhat[:], AF.Square, [t_ygt], sb2)
            t_v = vts(sv2, sb2, 1.0 / DIN, None, ALU.mult, None, [t_q])
            t_ln = act(sl2, sv2, AF.Ln, [t_v], bias=EPS)
            t_rs = act(sr2, sl2, AF.Exp, [t_ln], scale=-0.5)
            bstat[ci % 2].w += [t_q, t_v, t_ln, t_rs]
            t_yh = vstt(E.b_yg[:], E.b_yhat[:], sr2, E.t_ssmw[:], ALU.mult, ALU.mult,
                        [t_rs, ld[10]] + byg.reset())
            byg.w.append(t_yh)
            byhat.r.append(t_yh)
            bstat[ci % 2].r.append(t_yh)
            yft_first = byft.reset() if ci % 2 == 0 else []
            t_ = dmaT(E.b_yft[:].rearrange("p (k q) -> p k q", q=256)
                      [:, :, (ci % 2) * 128:(ci % 2) * 128 + 128],
                      E.b_yg[:].rearrange("p (k f) -> p k f", k=16),
                      [t_yh] + yft_first)
            byft.w.append(t_)
            byg.r.append(t_)
            yft_toks = [t_]

            if debug and cc == 0:
                d1 = dmaG(E.dbg_outs["dbg_z0"].ap(), zr_ap, [t_zr])
                zr.r.append(d1)
                d2 = dmaG(E.dbg_outs["dbg_yfull0"].ap(), E.b_yg[:], [t_yh])
                byg.r.append(d2)


        def p2_pair(ci):
            cc = g * GRP + ci
            pr_old = (ps0a.reset() + ps0b.reset() + ps0c.reset() + ps0d.reset())
            mo_mms = []
            for k in range(16):
                ws = bwstr[k % 4]
                ws_old = ws.reset()
                ws_ap = E.b_wstr[:, (k % 4) * D:(k % 4 + 1) * D]
                t_wl = dmaG(ws_ap, E.d_wops.ap()[k * 128:(k + 1) * 128, :],
                            ws_old)
                ws.w.append(t_wl)
                for co in range(2):
                    for n in range(2):
                        t_ = mm(PS0[:, co * 1024 + n * 512: co * 1024 + (n + 1) * 512],
                                E.b_yft[:, k * 256 + co * 128: k * 256 + co * 128 + 128],
                                ws_ap[:, n * 512:(n + 1) * 512],
                                k == 0, k == 15,
                                [t_wl] + byft.w + (pr_old if k == 0 else []))
                        mo_mms.append(t_)
                        ws.r.append(t_)
                        byft.r.append(t_)
            ps0a.w += mo_mms
            ps0b.w += mo_mms
            ps0c.w += mo_mms
            ps0d.w += mo_mms
            for co in range(2):
                ccp = cc - 1 + co
                cip = ccp % 4
                x2s = bx2[ccp % 2]
                x2_old = x2s.reset()
                x2_ap = E.b_x2[:, (ccp % 2) * D:(ccp % 2 + 1) * D]
                t_x2 = vtt(x2_ap, E.b_u[:, (ccp % 2) * D:(ccp % 2 + 1) * D],
                           PS0[:, co * 1024:co * 1024 + 1024], ALU.add,
                           mo_mms + bu[ccp % 2].w + x2_old)
                x2s.w.append(t_x2)
                bu[ccp % 2].r.append(t_x2)
                for b_ in (ps0a, ps0b, ps0c, ps0d):
                    b_.r.append(t_x2)
                w2s = bw2[cip]
                w2_old = w2s.reset()
                w2_ap = E.b_w2[:, cip * D:(cip + 1) * D]
                xslot = bx[ccp % 2]
                t_w2 = vtt(w2_ap, x2_ap,
                           E.t_xr[:, (ccp % 2) * D:(ccp % 2 + 1) * D], ALU.add,
                           [t_x2] + xslot.w + w2_old)
                w2s.w.append(t_w2)
                xslot.r.append(t_w2)
                x2s.r.append(t_w2)
                c0p = (ccp % 2) * 16
                sb3 = E.t_stat[:, c0p + 12:c0p + 13]
                sv3 = E.t_stat[:, c0p + 13:c0p + 14]
                sl3 = E.t_stat[:, c0p + 14:c0p + 15]
                sr3 = E.t_stat[:, c0p + 15:c0p + 16]
                t_q3 = dump_act(E.t_sqd[:, 0:D], x2_ap, AF.Square, [t_x2], sb3)
                t_v3 = vts(sv3, sb3, 1.0 / D, None, ALU.mult, None, [t_q3])
                t_l3 = act(sl3, sv3, AF.Ln, [t_v3], bias=EPS)
                t_r3 = act(sr3, sl3, AF.Exp, [t_l3], scale=-0.5)
                tfs = bxtf[ccp % 2]
                tf_old = tfs.reset()
                tf_ap = E.t_xtf[:, (ccp % 2) * D:(ccp % 2 + 1) * D]
                bstat[ccp % 2].w += [t_q3, t_v3, t_l3, t_r3]
                t_x2h = vstt(tf_ap, x2_ap, sr3, E.t_rmsw[:], ALU.mult, ALU.mult,
                             [t_r3, ld[11]] + tf_old)
                tfs.w.append(t_x2h)
                x2s.r.append(t_x2h)
                bstat[ccp % 2].r.append(t_x2h)
                if debug and ccp == 0:
                    d3 = dmaG(E.dbg_outs["dbg_x2_0"].ap(), x2_ap, [t_x2])
                    x2s.r.append(d3)
                if cip == 0:
                    x2ft_first_deps = bx3ft.reset()
                t_ = dmaT(E.b_x3ft[:].rearrange("p (d q) -> p d q", q=528)
                          [:, :, cip * 128:cip * 128 + 128],
                          tf_ap[:].rearrange("p (d f) -> p d f", d=8),
                          [t_x2h] + (x2ft_first_deps if cip == 0 else []))
                bx3ft.w.append(t_)
                tfs.r.append(t_)


        p2_top(0)
        p2_top(1)
        p2_pair(1)
        p2_top(2)
        p2_top(3)
        p2_pair(3)

        fc1_deps = bx3ft.w + bdram["fc1s"].w
        gt_old = bgated.reset()
        gated_toks = []
        fc1ps = [(ps0a, 0), (ps0b, 512), (ps0c, 1024), (ps0d, 1536)]
        for i in range(32):
            fa = bfstr[(2 * i) % 4]
            fg = bfstr[(2 * i + 1) % 4]
            fa_old = fa.reset()
            fg_old = fg.reset()
            fa_ap = E.b_fstr[:, ((2 * i) % 4) * 1024:((2 * i) % 4 + 1) * 1024]
            fg_ap = E.b_fstr[:, ((2 * i + 1) % 4) * 1024:((2 * i + 1) % 4 + 1) * 1024]
            t_fa = dmaG(fa_ap, E.d_fc1s.ap()[:, i * 1024:(i + 1) * 1024],
                        bdram["fc1s"].w + fa_old)
            fa.w.append(t_fa)
            t_fg = dmaG(fg_ap, E.d_fc1s.ap()[:, (32 + i) * 1024:(33 + i) * 1024],
                        bdram["fc1s"].w + fg_old)
            fg.w.append(t_fg)
            pa, ca = fc1ps[(i % 2) * 2]
            pg, cg = fc1ps[(i % 2) * 2 + 1]
            a_old = pa.reset()
            g_old2 = pg.reset()
            lastA = lastG = None
            for d in range(8):
                lastA = mm(PS0[:, ca:ca + 512], fa_ap[:, d * 128:(d + 1) * 128],
                           E.b_x3ft[:, d * 528:d * 528 + 512], d == 0, d == 7,
                           [t_fa] + fc1_deps + (a_old if d == 0 else []))
                fa.r.append(lastA)
            for d in range(8):
                lastG = mm(PS0[:, cg:cg + 512], fg_ap[:, d * 128:(d + 1) * 128],
                           E.b_x3ft[:, d * 528:d * 528 + 512], d == 0, d == 7,
                           [t_fg] + fc1_deps + (g_old2 if d == 0 else []))
                fg.r.append(lastG)
            pa.w.append(lastA)
            pg.w.append(lastG)
            sg_old = bsg.reset()
            t_sg = act(E.b_sg[:], PS0[:, cg:cg + 512], AF.Silu, [lastG] + sg_old)
            bsg.w.append(t_sg)
            pg.r.append(t_sg)
            t_gt = vtt(E.b_gated[:, i * 512:(i + 1) * 512], E.b_sg[:], PS0[:, ca:ca + 512],
                       ALU.mult, [t_sg, lastA] + (gt_old if i == 0 else []))
            bgated.w.append(t_gt)
            bsg.r.append(t_gt)
            pa.r.append(t_gt)
            gated_toks.append(t_gt)
        bx3ft.r.append(gated_toks[-1])

        q_old = (ps0a.reset() + ps0b.reset() + ps0c.reset() + ps0d.reset()
                 + ps1.reset())
        QAP = [PS0[:, 0:1024], PS0[:, 1024:2048], PS1[:, 0:1024], PS1[:, 1024:2048]]
        m_mms = []
        for k in range(32):
            ws = bwstr[k % 4]
            ws_old = ws.reset()
            ws_ap = E.b_wstr[:, (k % 4) * D:(k % 4 + 1) * D]
            t_wl = dmaG(ws_ap, E.d_fc2s.ap()[k * 128:(k + 1) * 128, :],
                        bdram["fc2s"].w + ws_old)
            ws.w.append(t_wl)
            for cq in range(4):
                for n in range(2):
                    t_ = mm(QAP[cq][:, n * 512:(n + 1) * 512],
                            E.b_gated[:, k * 512 + cq * 128: k * 512 + cq * 128 + 128],
                            ws_ap[:, n * 512:(n + 1) * 512],
                            k == 0, k == 31,
                            [t_wl] + gated_toks + (q_old if k == 0 else []))
                    m_mms.append(t_)
                    ws.r.append(t_)
                    bgated.r.append(t_)
        for b_ in (ps0a, ps0b, ps0c, ps0d, ps1):
            b_.w += m_mms

        x3v = E.b_x3ft[:, :].rearrange("p (d c) -> p d c", c=528)
        if g == 0:
            t_hz = vmemset(x3v[:, :, 15:16], 0.0, bx3ft.reset())
        else:
            prev3 = bx3ft.reset()
            t_hz = vcopy(x3v[:, :, 15:16], x3v[:, :, 527:528], prev3)
        bx3ft.w.append(t_hz)
        for ci in range(GRP):
            cc = g * GRP + ci
            w2_ap = E.b_w2[:, ci * D:(ci + 1) * D]
            t_xa = vtt(w2_ap, QAP[ci], w2_ap, ALU.add, m_mms + bw2[ci].w)
            bw2[ci].w.append(t_xa)
            for b_ in (ps0a, ps0b, ps0c, ps0d, ps1):
                b_.r.append(t_xa)
            if debug and cc == 0:
                d4 = dmaG(E.dbg_outs["dbg_x3_0"].ap(), w2_ap, [t_xa])
                bw2[ci].r.append(d4)
            c0 = (ci % 2) * 16
            sa = E.t_stat[:, c0:c0 + 1]; sb_ = E.t_stat[:, c0 + 1:c0 + 2]
            sm = E.t_stat[:, c0 + 2:c0 + 3]; sq = E.t_stat[:, c0 + 3:c0 + 4]
            sv = E.t_stat[:, c0 + 4:c0 + 5]; sl_ = E.t_stat[:, c0 + 5:c0 + 6]
            sr = E.t_stat[:, c0 + 6:c0 + 7]
            st = bstat[ci % 2]
            st_old = st.reset()
            t1 = dump_act(E.t_sqd[:, 0:D], w2_ap, AF.Copy, [t_xa] + st_old, sa)
            t2 = dump_act(E.t_sqd[:, 0:D], w2_ap, AF.Square, [t_xa], sb_)
            t3 = vts(sm, sa, 1.0 / D, None, ALU.mult, None, [t1])
            t4 = vtt(sq, sm, sm, ALU.mult, [t3])
            t5 = vts(sv, sb_, 1.0 / D, sq, ALU.mult, ALU.subtract, [t2, t4])
            t6 = act(sl_, sv, AF.Ln, [t5], bias=EPS)
            t7 = act(sr, sl_, AF.Exp, [t6], scale=-0.5)
            st.w += [t1, t2, t3, t4, t5, t6, t7]
            tfs = bxtf[ci % 2]
            tf_old = tfs.reset()
            tf_ap = E.t_xtf[:, (ci % 2) * D:(ci % 2 + 1) * D]
            t_x3 = vts(tf_ap, w2_ap, sm, sr, ALU.subtract, ALU.mult, [t7] + tf_old)
            tfs.w.append(t_x3)
            st.r.append(t_x3)
            bw2[ci].r.append(t_x3)
            if cc == 0:
                t_r0 = vcopy(E.t_x3n0[:], tf_ap[0:1, :], [t_x3])
                tfs.r.append(t_r0)
                t_a0 = vcopy(E.t_xa0[:], w2_ap[0:1, :], [t_xa])
                bw2[ci].r.append(t_a0)
            if cc == T // 128 - 1:
                t_re = dmaS(E.d_row1.ap(), tf_ap[127:128, :], [t_x3])
                tfs.r.append(t_re)
                t_rb = dmaG(E.t_halo[:], E.d_row1.ap(), [t_re])
                t_msk = vts(E.t_big[0:1, 0:D], E.t_halo[:], E.t_cm[0:1, 0:1], None,
                            ALU.mult, None, [t_rb, ld[4]] + bbig.reset())
                bbig.w.append(t_msk)
                t_wc2 = dmaS(E.d_cc2i.ap(), E.t_big[0:1, 0:D], [t_msk])
                bbig.r.append(t_wc2)
                cc2_tok["t"] = s.emit_cc(lambda e: e.collective_compute(
                    "AllReduce", ALU.add, replica_groups=CC_GROUPS,
                    ins=[E.d_cc2i.ap().opt()], outs=[E.d_cc2o.ap().opt()]), [t_wc2])
            t_ = dmaT(E.b_x3ft[:].rearrange("p (d q) -> p d q", q=528)
                      [:, :, 16 + ci * 128:16 + ci * 128 + 128],
                      tf_ap[:].rearrange("p (d f) -> p d f", d=8),
                      [t_x3] + bx3ft.w[:1])
            bx3ft.w.append(t_)
            tfs.r.append(t_)

        xk_old = byft.reset()
        xk_toks = []
        for d in range(8):
            dst = E.b_yft[:, d * 512:(d + 1) * 512]
            t_t = vts(dst, x3v[:, d:d + 1, 16:528].squeeze(1), E.t_mukp[:, d:d + 1],
                      None, ALU.mult, None,
                      bx3ft.w + [ld[7]] + (xk_old if d == 0 else []))
            t_xk = vstt(dst, x3v[:, d:d + 1, 15:527].squeeze(1), E.t_omkp[:, d:d + 1],
                        dst, ALU.mult, ALU.add, [t_t, tok_omk])
            byft.w += [t_t, t_xk]
            bx3ft.r += [t_t, t_xk]
            xk_toks.append(t_xk)
        ka_old = bka.reset()
        for o in range(8):
            fs = bfstr[o % 4]
            fs_old = fs.reset()
            fs_ap = E.b_fstr[:, (o % 4) * 1024:(o % 4 + 1) * 1024]
            t_wl = dmaG(fs_ap, E.d_wks.ap()[:, o * 1024:(o + 1) * 1024],
                        bdram["wks"].w + fs_old)
            fs.w.append(t_wl)
            reg = [ps0a, ps0b][o % 2]
            regap = PS0[:, (o % 2) * 512:(o % 2) * 512 + 512]
            r_old = reg.reset()
            lastm = None
            for d in range(8):
                lastm = mm(regap, fs_ap[:, d * 128:(d + 1) * 128],
                           E.b_yft[:, d * 512:(d + 1) * 512], d == 0, d == 7,
                           [t_wl, xk_toks[d]] + (r_old if d == 0 else []))
                fs.r.append(lastm)
                byft.r.append(lastm)
            reg.w.append(lastm)
            rk_old = brk.reset()
            t_rl = act(E.b_rk[:], regap, AF.Relu, [lastm] + rk_old)
            brk.w.append(t_rl)
            reg.r.append(t_rl)
            t_ka = act(E.b_ka[:, o * 512:(o + 1) * 512], E.b_rk[:], AF.Square,
                       [t_rl] + (ka_old if o == 0 else []))
            bka.w.append(t_ka)
            brk.r.append(t_ka)
        xr_old = byft.reset()
        xr_toks = []
        for d in range(8):
            dst = E.b_yft[:, d * 512:(d + 1) * 512]
            t_t = vts(dst, x3v[:, d:d + 1, 16:528].squeeze(1), E.t_murp[:, d:d + 1],
                      None, ALU.mult, None,
                      bx3ft.w + [ld[6]] + (xr_old if d == 0 else []))
            t_xr = vstt(dst, x3v[:, d:d + 1, 15:527].squeeze(1), E.t_omrp[:, d:d + 1],
                        dst, ALU.mult, ALU.add, [t_t, tok_omr])
            byft.w += [t_t, t_xr]
            bx3ft.r += [t_t, t_xr]
            xr_toks.append(t_xr)
        for pp in range(2):
            r_mms = {0: [], 1: []}
            pr_old = ps1.reset()
            for d in range(8):
                ws = bwstr[d % 4]
                ws_old = ws.reset()
                ws_ap = E.b_wstr[:, (d % 4) * D:(d % 4 + 1) * D]
                t_wl = dmaG(ws_ap, E.d_wrs.ap()[d * 128:(d + 1) * 128, :],
                            bdram["wrs"].w + ws_old)
                ws.w.append(t_wl)
                for co in range(2):
                    ci = pp * 2 + co
                    for n in range(2):
                        t_ = mm(PS1[:, co * 1024 + n * 512: co * 1024 + (n + 1) * 512],
                                E.b_yft[:, d * 512 + ci * 128: d * 512 + ci * 128 + 128],
                                ws_ap[:, n * 512:(n + 1) * 512],
                                d == 0, d == 7,
                                [t_wl, xr_toks[d]] + (pr_old if d == 0 else []))
                        r_mms[co].append(t_)
                        ws.r.append(t_)
                        byft.r.append(t_)
            ps1.w += r_mms[0] + r_mms[1]
            r_toks = []
            for co in range(2):
                rs = brst[co]
                rs_old = rs.reset()
                rs_ap = E.b_rst[:, co * D:(co + 1) * D]
                t_sig = act(rs_ap, PS1[:, co * 1024:(co + 1) * 1024], AF.Sigmoid,
                            r_mms[co] + rs_old)
                rs.w.append(t_sig)
                ps1.r.append(t_sig)
                r_toks.append(t_sig)
            v_mms = {0: [], 1: []}
            pv_old = ps0a.reset() + ps0b.reset() + ps0c.reset() + ps0d.reset()
            for kc in range(8):
                ws = bwstr[kc % 4]
                ws_old = ws.reset()
                ws_ap = E.b_wstr[:, (kc % 4) * D:(kc % 4 + 1) * D]
                t_wl = dmaG(ws_ap, E.d_wvs.ap()[kc * 128:(kc + 1) * 128, :],
                            bdram["wvs"].w + ws_old)
                ws.w.append(t_wl)
                for co in range(2):
                    ci = pp * 2 + co
                    for n in range(2):
                        t_ = mm(PS0[:, co * 1024 + n * 512: co * 1024 + (n + 1) * 512],
                                E.b_ka[:, kc * 512 + ci * 128: kc * 512 + ci * 128 + 128],
                                ws_ap[:, n * 512:(n + 1) * 512],
                                kc == 0, kc == 7,
                                [t_wl] + bka.w + (pv_old if kc == 0 else []))
                        v_mms[co].append(t_)
                        ws.r.append(t_)
                        bka.r.append(t_)
            ps0a.w += v_mms[0] + v_mms[1]
            ps0b.w += v_mms[0] + v_mms[1]
            ps0c.w += v_mms[0] + v_mms[1]
            ps0d.w += v_mms[0] + v_mms[1]
            for co in range(2):
                ci = pp * 2 + co
                cc = g * GRP + ci
                big_old = bbig.reset()
                t_tv = vtt(E.t_big[:, 0:1024], E.b_rst[:, co * D:(co + 1) * D],
                           PS0[:, co * 1024:(co + 1) * 1024], ALU.mult,
                           v_mms[co] + r_toks + big_old)
                bbig.w.append(t_tv)
                brst[co].r.append(t_tv)
                for b_ in (ps0a, ps0b, ps0c, ps0d):
                    b_.r.append(t_tv)
                x2s = bx2[ci % 2]
                x2_old = x2s.reset()
                x4_ap = E.b_x2[:, (ci % 2) * D:(ci % 2 + 1) * D]
                t_x4 = vtt(x4_ap, E.b_w2[:, ci * D:(ci + 1) * D], E.t_big[:, 0:1024],
                           ALU.add, [t_tv] + bw2[ci].w + x2_old)
                x2s.w.append(t_x4)
                bw2[ci].r.append(t_x4)
                bbig.r.append(t_x4)
                t_out = dmaS(E.h_out.ap()[cc * 128:(cc + 1) * 128, :], x4_ap, [t_x4])
                x2s.r.append(t_out)
                out_toks.append(t_out)

    if stop_after == "p2":
        return finalize_early()

    join_all()

    t_mr = dmaS(E.p_murrow[:], E.h_murrow.ap(), [])
    t_mk = dmaS(E.p_mukrow[:], E.h_mukrow.ap(), [])
    t_o1 = vts(E.p_omr[:], E.p_murrow[:], -1.0, 1.0, ALU.mult, ALU.add, [t_mr])
    t_o2 = vts(E.p_omk[:], E.p_mukrow[:], -1.0, 1.0, ALU.mult, ALU.add, [t_mk])
    t_rc2 = dmaG(E.t_big[0:1, 0:D], E.d_cc2o.ap(), [cc2_tok["t"]] + bbig.reset())
    bbig.w.append(t_rc2)
    t_hal = vts(E.t_halo[:], E.t_big[0:1, 0:D], E.t_um[0:1, 0:1], None,
                ALU.mult, None, [t_rc2, ld[5]])
    bbig.r.append(t_hal)

    def patch_rows(x3n0, halo, murow, omrow, outC, extra):
        tA = vtt(E.p_rowA[:], x3n0, murow, ALU.mult,
                 [t_hal, t_mr, t_mk, t_o1, t_o2] + extra)
        tB = vtt(E.p_rowB[:], halo, omrow, ALU.mult, [tA])
        tC = vtt(outC, E.p_rowA[:], E.p_rowB[:], ALU.add, [tA, tB])
        return tC

    t_xr0 = patch_rows(E.t_x3n0[:], E.t_halo[:], E.p_murrow[:], E.p_omr[:],
                       E.p_rowC[:], [])
    t_xk0 = patch_rows(E.t_x3n0[:], E.t_halo[:], E.p_mukrow[:], E.p_omk[:],
                       E.p_rowD[:], [t_xr0])

    def row_to_cols_bf(row_ap, col_base, dep):
        toks = []
        for d in range(8):
            tm = mm(PS0[0:128, 1960 + d % 2:1961 + d % 2],
                    row_ap[:, d * 128:(d + 1) * 128],
                    E.t_eyebf[0:1, 0:1], True, True, [dep, tok_eyebf] + ps0d.reset())
            ps0d.w.append(tm)
            te_ = scopy(E.p_wcol[:, col_base + d:col_base + d + 1],
                        PS0[0:128, 1960 + d % 2:1961 + d % 2], [tm])
            ps0d.r.append(te_)
            toks.append(te_)
        return toks

    xr0c = row_to_cols_bf(E.p_rowC[:], 0, t_xr0)
    xk0c = row_to_cols_bf(E.p_rowD[:], 8, t_xk0)

    def patch_mm(col_base, wsrc, wbuf_d, psreg, psbuf, deps):
        mms = []
        for d in range(8):
            ws = bwstr[d % 4]
            ws_old = ws.reset()
            ws_ap = E.b_wstr[:, (d % 4) * D:(d % 4 + 1) * D]
            t_wl = dmaG(ws_ap, wsrc.ap()[d * 128:(d + 1) * 128, :],
                        wbuf_d.w + ws_old)
            ws.w.append(t_wl)
            for n in range(2):
                t_ = mm(psreg[:, n * 512:(n + 1) * 512],
                        E.p_wcol[:, col_base + d:col_base + d + 1],
                        ws_ap[:, n * 512:(n + 1) * 512],
                        d == 0, d == 7,
                        [t_wl] + deps + (psbuf.reset() if d == 0 and n == 0 else []))
                mms.append(t_)
                ws.r.append(t_)
        psbuf.w += mms
        return mms

    r0_mms = patch_mm(0, E.h_wr, bdram["wrs"], PS1[0:1, 0:1024], ps1, xr0c)
    t_r0 = act(E.p_rowC[:], PS1[0:1, 0:1024], AF.Sigmoid, r0_mms)
    ps1.r.append(t_r0)
    bwk2 = Buf()
    k0_mms = patch_mm(8, E.h_wkt, bwk2, PS1[0:1, 1024:2048], ps1, xk0c)
    t_rl0 = act(E.p_rowD[:], PS1[0:1, 1024:2048], AF.Relu, k0_mms)
    ps1.r.append(t_rl0)
    t_ka0 = vtt(E.p_rowE[:], E.p_rowD[:], E.p_rowD[:], ALU.mult, [t_rl0])
    ka0c = row_to_cols_bf(E.p_rowE[:], 16, t_ka0)
    v0_mms = patch_mm(16, E.h_wv, bdram["wvs"], PS0[0:1, 0:1024],
                      ps0a, ka0c)
    t_tv0 = vtt(E.p_rowA[:], E.p_rowC[:], PS0[0:1, 0:1024], ALU.mult,
                v0_mms + [t_r0])
    ps0a.r.append(t_tv0)
    t_x40 = vtt(E.p_rowB[:], E.t_xa0[:], E.p_rowA[:], ALU.add, [t_tv0])
    t_out0 = dmaS(E.h_out.ap()[0:1, :], E.p_rowB[:], [t_x40])
    out_toks.append(t_out0)

    n = s.dcnt[SY]
    fw = []
    for sidx in range(s.POOL):
        uses = (n - sidx + s.POOL - 1) // s.POOL
        if uses > 0:
            fw.append((s.dsem[SY][sidx], 16 * uses))
    s.ops[SY].append((fw, None, None, 0))

    with nc.Block() as block:
        s.replay(block)
    return nc, E.dbg_outs



_NC_CACHE = {}


def _get_nc(debug=False):
    key = bool(debug)
    if key not in _NC_CACHE:
        _NC_CACHE[key] = build_nc(debug)
    return _NC_CACHE[key]


def kernel(**inputs):
    in_maps = _host_layout(inputs)
    nc, _ = _get_nc(False)
    res = run_bass_kernel_spmd(nc, in_maps, core_ids=list(range(NCORE)))
    out = np.empty((4, 4096, 1024), np.float32)
    for c in range(NCORE):
        b, half = c // 2, c % 2
        out[b, half * T:(half + 1) * T] = res.results[c]["out"]
    return out
